# version 09
# baseline (speedup 1.0000x reference)
import numpy as np
import ml_dtypes
from contextlib import ExitStack

import concourse.bass as bass
import concourse.mybir as mybir
from concourse.bass_utils import run_bass_kernel_spmd

F32 = mybir.dt.float32
BF16 = mybir.dt.bfloat16
I32 = mybir.dt.int32
FP8 = mybir.dt.float8e4
AF = mybir.ActivationFunctionType
ALU = mybir.AluOpType
AX = mybir.AxisListType

MAGIC = 12582912.0
D = 2048
E = 2048
II = 8192
TOK = 2048
NCORE = 8
TTA = 256
NT = TOK // TTA
DC = D // 128
EC = E // 128
ICH = II // 128
ITILE = 512
HT = 512

DEBUG_OUTPUTS = False
PARANOID = False


class Ch:

    def __init__(self, sem):
        self.sem = sem
        self.n = 0

    def take(self, k=1):
        self.n += k
        return self.n


class Slot:

    def __init__(self):
        self.conds = []

    def freed_by(self, ch, val=None):
        self.conds = [(ch, ch.n if val is None else val)]

    def also_freed_by(self, ch, val=None):
        self.conds.append((ch, ch.n if val is None else val))

    def wait(self, eng):
        for ch, val in self.conds:
            eng.wait_ge(ch.sem, val)
        self.conds = []


def build_program():
    nc = bass.Bass("TRN2", target_bir_lowering=False, debug=False,
                   detect_race_conditions=False)

    x_own = nc.dram_tensor("x_own", [TOK, D], F32, kind="ExternalInput").ap()
    x_pre = nc.dram_tensor("x_pre", [TOK, D], F32, kind="ExternalInput").ap()
    w_i = nc.dram_tensor("w_i", [D, E], F32, kind="ExternalInput").ap()
    w_f = nc.dram_tensor("w_f", [D, E], F32, kind="ExternalInput").ap()
    w_g = nc.dram_tensor("w_g", [D, E], F32, kind="ExternalInput").ap()
    w_oT = nc.dram_tensor("w_oT", [E, E], F32, kind="ExternalInput").ap()
    s_i = nc.dram_tensor("s_i", [128, EC], F32, kind="ExternalInput").ap()
    s_f = nc.dram_tensor("s_f", [128, EC], F32, kind="ExternalInput").ap()
    s_g = nc.dram_tensor("s_g", [128, EC], F32, kind="ExternalInput").ap()
    s_o = nc.dram_tensor("s_o", [128, EC], F32, kind="ExternalInput").ap()
    gnw = nc.dram_tensor("gnw", [128, EC], F32, kind="ExternalInput").ap()
    maskv = nc.dram_tensor("maskv", [128, 1], F32, kind="ExternalInput").ap()
    ident = nc.dram_tensor("ident", [128, 128], BF16, kind="ExternalInput").ap()
    gate_w = nc.dram_tensor("gate_w", [II, D], F32, kind="ExternalInput").ap()
    up_w = nc.dram_tensor("up_w", [II, D], F32, kind="ExternalInput").ap()
    down_w = nc.dram_tensor("down_w", [D, II], F32, kind="ExternalInput").ap()

    out = nc.dram_tensor("out", [TOK, D], F32, kind="ExternalOutput").ap()

    skind = "ExternalOutput" if DEBUG_OUTPUTS else "Internal"
    sc_xfixT = nc.dram_tensor("sc_xfixT", [NT, 128, DC * TTA], BF16,
                              kind=skind).ap()
    sc_h = nc.dram_tensor("sc_h", [NT, 128, EC * TTA], F32, kind=skind).ap()
    sc_s = nc.dram_tensor("sc_s", [TOK, E], F32, kind=skind).ap()
    sc_h2qT = nc.dram_tensor("sc_h2qT", [TOK // 128, 128, ICH * 128], BF16,
                             kind=skind).ap()
    if DEBUG_OUTPUTS:
        dbg_xqT = nc.dram_tensor("dbg_xqT", [128, DC * TOK], BF16,
                                 kind="ExternalOutput").ap()
        dbg_gwT = nc.dram_tensor("dbg_gwT", [II // ITILE, 128, DC * ITILE], BF16,
                                 kind="ExternalOutput").ap()
        dbg_upT = nc.dram_tensor("dbg_upT", [II // ITILE, 128, DC * ITILE], BF16,
                                 kind="ExternalOutput").ap()
        dbg_h2b = nc.dram_tensor("dbg_h2b", [II // ITILE, TOK // 128, 128, ITILE],
                                 F32, kind="ExternalOutput").ap()

    TEN, VEC, SCA, GPS, SYN = nc.tensor, nc.vector, nc.scalar, nc.gpsimd, nc.sync

    with ExitStack() as top:
        def sb(stack, name, p, f, dt):
            return stack.enter_context(nc.sbuf_tensor(name, [p, f], dt))

        def sem(name):
            return Ch(top.enter_context(nc.semaphore(name)))

        vsi = sb(top, "vsi", 128, EC, F32)
        vsf = sb(top, "vsf", 128, EC, F32)
        vsg = sb(top, "vsg", 128, EC, F32)
        vso = sb(top, "vso", 128, EC, F32)
        vgn = sb(top, "vgn", 128, EC, F32)
        vgs = sb(top, "vgs", 128, EC, F32)
        vmask = sb(top, "vmask", 128, 1, F32)
        ones_f = sb(top, "ones_f", 128, 1, F32)
        ident_t = sb(top, "ident_t", 128, 128, BF16)
        state = sb(top, "state", 128, EC, F32)
        init_own = sb(top, "init_own", 128, EC, F32)
        rstd_b = sb(top, "rstd_b", 1, 256, F32)
        ones128 = sb(top, "ones128", 128, 128, F32)
        rbc_src = sb(top, "rbc_src", 128, TTA, F32)

        psA = top.enter_context(nc.psum_tensor("psA", [128, 512], F32))
        psB = top.enter_context(nc.psum_tensor("psB", [128, 512], F32))
        psC = top.enter_context(nc.psum_tensor("psC", [128, 512], F32))
        psD = top.enter_context(nc.psum_tensor("psD", [128, 512], F32))
        psT0 = top.enter_context(nc.psum_tensor("psT0", [128, 512], BF16))
        psT1 = top.enter_context(nc.psum_tensor("psT1", [128, 512], BF16))
        psU = top.enter_context(nc.psum_tensor("psU", [128, 512], F32))
        psV = top.enter_context(nc.psum_tensor("psV", [128, 256], F32))

        ld = sem("ld")
        ldx = sem("ldx")
        ldw = sem("ldw")
        lnd = sem("lnd")
        lnv = sem("lnv")
        lns = sem("lns")
        tfd = sem("tfd")
        trd = sem("trd")
        evd = sem("evd")
        mmd = sem("mmd")
        ivd = sem("ivd")
        fsd = sem("fsd")
        scd = sem("scd")
        std = sem("std")
        sto = sem("sto")
        qvd = sem("qvd")
        qsd = sem("qsd")
        gpd = sem("gpd")
        wcd = sem("wcd")
        stc = sem("stc")
        bar = [sem(f"bar{i}") for i in range(5)]

        ENGS = [SYN, TEN, VEC, SCA, GPS]

        def barrier():
            vals = []
            for i, e in enumerate(ENGS):
                vals.append(bar[i].take())
                e.sem_inc(bar[i].sem, 1)
            for i, e in enumerate(ENGS):
                for j in range(5):
                    if j != i:
                        e.wait_ge(bar[j].sem, vals[j])

        def maybe_barrier():
            if PARANOID:
                barrier()

        def load_cast_weight(wsrc, dst, nchunks, cols, wstage, wstage_slots):
            for k in range(nchunks):
                sl = wstage_slots[k % 2]
                sl.wait(SYN)
                SYN.dma_start(out=wstage[k % 2][:, :cols],
                              in_=wsrc[k * 128:(k + 1) * 128, :cols]
                              ).then_inc(ldw.sem, 16)
                v = ldw.take(16)
                SCA.wait_ge(ldw.sem, v)
                SCA.activation(dst[:, k * cols:(k + 1) * cols],
                               wstage[k % 2][:, :cols], AF.Copy
                               ).then_inc(wcd.sem, 1)
                wcd.take()
                sl.freed_by(wcd)

        def tofix(src_ap, kint_ap, out_ap):
            VEC.tensor_scalar(kint_ap, src_ap, 256.0, MAGIC, ALU.mult, ALU.add)
            return VEC.tensor_scalar(out_ap, kint_ap, 1.0 / 256.0, -MAGIC / 256.0,
                                     ALU.mult, ALU.add)

        C448 = float(np.float32(1.0 / 448.0))
        C448Q = C448 * 2.0 ** -9

        def quant(src_ap, W, bm, sc, rc, v, q8, k8, deq, sc2):
            nb = W // 32
            s3 = src_ap.rearrange("p (b c) -> p b c", c=32)
            VEC.tensor_reduce(bm[:, :nb], s3, AX.X, ALU.max,
                              apply_absolute_value=True).then_inc(qvd.sem, 1)
            v1 = qvd.take()
            SCA.wait_ge(qvd.sem, v1)
            SCA.activation(sc[:, :nb], bm[:, :nb], AF.Copy,
                           scale=C448, bias=1e-38)
            SCA.activation(sc2[:, :nb], bm[:, :nb], AF.Copy,
                           scale=C448Q, bias=1e-38 * 2.0 ** -9).then_inc(qsd.sem, 1)
            s1 = qsd.take()
            VEC.wait_ge(qsd.sem, s1)
            VEC.reciprocal(rc[:, :nb], sc[:, :nb]).then_inc(qvd.sem, 1)
            v2 = qvd.take()
            VEC.wait_ge(qvd.sem, v2)
            rc3 = rc[:, :nb].unsqueeze(2).broadcast_to([128, nb, 32])
            VEC.scalar_tensor_tensor(
                v[:, :W].rearrange("p (b c) -> p b c", c=32), s3,
                0.5, rc3, ALU.mult, ALU.mult).then_inc(qvd.sem, 1)
            vq = qvd.take()
            SCA.wait_ge(qvd.sem, vq)
            SCA.wait_ge(gpd.sem, gpd.n)
            SCA.activation(q8[:, :W], v[:, :W], AF.Copy)
            SCA.activation(k8[:, :W], q8[:, :W], AF.Copy,
                           scale=1024.0).then_inc(qsd.sem, 1)
            vs = qsd.take()
            sc3 = sc2[:, :nb].unsqueeze(2).broadcast_to([128, nb, 32])
            VEC.wait_ge(qsd.sem, vs)
            VEC.tensor_tensor(
                deq[:, :W].rearrange("p (b c) -> p b c", c=32),
                k8[:, :W].rearrange("p (b c) -> p b c", c=32),
                sc3, ALU.mult).then_inc(gpd.sem, 1)
            return gpd.take()

        lnst = sb(top, "lnst", 128, 16, F32)
        ln_free = [Slot(), Slot()]

        def emit_ln(x_ap, xn_ap, par, x_ready_val, lntrash):
            c0 = par * 8
            st = lnst
            SCA.wait_ge(ldx.sem, x_ready_val)
            ln_free[par].wait(SCA)
            SCA.activation(lntrash[:], x_ap, AF.Square,
                           accum_out=st[:, c0:c0 + 1])
            SCA.activation(lntrash[:], x_ap, AF.Copy,
                           accum_out=st[:, c0 + 1:c0 + 2]).then_inc(lnd.sem, 1)
            vl = lnd.take()
            VEC.wait_ge(lnd.sem, vl)
            VEC.tensor_scalar(st[:, c0 + 2:c0 + 3], st[:, c0 + 1:c0 + 2],
                              1.0 / 2048.0, None, ALU.mult)
            VEC.tensor_scalar(st[:, c0 + 3:c0 + 4], st[:, c0:c0 + 1],
                              1.0 / 2048.0, 1e-5, ALU.mult, ALU.add)
            VEC.scalar_tensor_tensor(st[:, c0 + 4:c0 + 5], st[:, c0 + 2:c0 + 3],
                                     1.0, st[:, c0 + 2:c0 + 3],
                                     ALU.mult, ALU.mult)
            VEC.tensor_tensor(st[:, c0 + 5:c0 + 6], st[:, c0 + 3:c0 + 4],
                              st[:, c0 + 4:c0 + 5],
                              ALU.subtract).then_inc(lnv.sem, 1)
            vv = lnv.take()
            ln_free[par].freed_by(lnv, vv)
            SCA.wait_ge(lnv.sem, vv)
            SCA.activation(st[:, c0 + 6:c0 + 7], st[:, c0 + 5:c0 + 6],
                           AF.Sqrt).then_inc(lns.sem, 1)
            vq = lns.take()
            VEC.wait_ge(lns.sem, vq)
            VEC.reciprocal(st[:, c0 + 7:c0 + 8], st[:, c0 + 6:c0 + 7])
            VEC.tensor_scalar(xn_ap, x_ap, st[:, c0 + 2:c0 + 3],
                              st[:, c0 + 7:c0 + 8], ALU.subtract, ALU.mult)

        tr_state = {"last_evac": [0, 0]}

        def transpose_blocks(src_fn, nblk, dst_fn, ready_ch, ready_val,
                             extra_wait=None):
            ngr = (nblk + 3) // 4
            for g in range(ngr):
                bank = (tr_state.setdefault("g", 0)) % 2
                tr_state["g"] = tr_state.get("g", 0) + 1
                ps = psT0 if bank == 0 else psT1
                TEN.wait_ge(ready_ch.sem, ready_val)
                if extra_wait is not None:
                    extra_wait.wait(TEN)
                    extra_wait = None
                if tr_state["last_evac"][bank]:
                    TEN.wait_ge(evd.sem, tr_state["last_evac"][bank])
                n_in_g = min(4 * g + 4, nblk) - 4 * g
                last = None
                for q in range(n_in_g):
                    j = 4 * g + q
                    last = TEN.transpose(ps[:, q * 128:(q + 1) * 128],
                                         src_fn(j), ident_t[:])
                last.then_inc(trd.sem, 1)
                tv = trd.take()
                SCA.wait_ge(trd.sem, tv)
                SCA.activation(dst_fn(g),
                               ps[:, :n_in_g * 128].rearrange(
                                   "p (a t) -> p a t", t=128),
                               AF.Copy).then_inc(evd.sem, 1)
                tr_state["last_evac"][bank] = evd.take()
            return evd.n

        for src, dst in ((s_i, vsi), (s_f, vsf), (s_g, vsg), (s_o, vso),
                         (gnw, vgn), (maskv, vmask), (ident, ident_t)):
            SYN.dma_start(out=dst[:], in_=src[:]).then_inc(ld.sem, 16)
            ld.take(16)
        VEC.wait_ge(ld.sem, ld.n)
        VEC.memset(ones_f[:], 1.0)
        VEC.memset(ones128[:], 1.0)
        VEC.memset(rbc_src[:], 0.0)
        VEC.scalar_tensor_tensor(vgs[:], vgn[:], -1.0, vso[:], ALU.mult, ALU.mult)
        VEC.memset(state[:], 0.0)

        with ExitStack() as ph:
            wib = sb(ph, "wib", 128, DC * E, BF16)
            wfb = sb(ph, "wfb", 128, DC * E, BF16)
            xstage = [sb(ph, f"xstage{j}", 128, D, F32) for j in range(2)]
            wstage = xstage
            xn_b = [sb(ph, f"xn{j}", 128, D, F32) for j in range(2)]
            kint = sb(ph, "kint", 128, D, I32)
            xfix_b = sb(ph, "xfix_b", 128, D, BF16)
            xfixT = [sb(ph, f"xfixT{j}", 128, DC * TTA, BF16) for j in range(2)]
            fb = [sb(ph, f"fb{j}", 128, TTA, F32) for j in range(2)]
            lnt = sb(ph, "lnt", 128, D, F32)
            i1b = sb(ph, "i1b", 128, TTA, F32)
            iqn = sb(ph, "iqn", 128, TTA, F32)
            hn = [sb(ph, f"hn{j}", 128, TTA, F32) for j in range(2)]

            wstage_slots = [Slot(), Slot()]
            xstage_slots = [Slot(), Slot()]
            xn_slots = [Slot(), Slot()]
            xfix_slot = Slot()
            xfixT_slots = [Slot(), Slot()]
            hn_slots = [Slot(), Slot()]
            fb_slots = [Slot(), Slot()]
            stc_vals = [0] * EC

            load_cast_weight(w_i, wib, DC, E, wstage, wstage_slots)
            load_cast_weight(w_f, wfb, DC, E, wstage, wstage_slots)
            w_ready = wcd.n
            for sl in xstage_slots:
                sl.freed_by(wcd, w_ready)

            def attn_if_scan(x_src, own):
                for t in range(NT):
                    tp = t % 2
                    xfixT_slots[tp].wait(TEN)
                    for c in range(2):
                        cp = c
                        row0 = t * TTA + c * 128
                        xstage_slots[cp].wait(SYN)
                        SYN.dma_start(out=xstage[cp][:],
                                      in_=x_src[row0:row0 + 128, :]
                                      ).then_inc(ldx.sem, 16)
                        vx = ldx.take(16)
                        xn_slots[cp].wait(VEC)
                        emit_ln(xstage[cp][:], xn_b[cp][:], cp, vx, lnt[:])
                        xfix_slot.wait(VEC)
                        tofix(xn_b[cp][:], kint[:], xfix_b[:]).then_inc(tfd.sem, 1)
                        vtf = tfd.take()
                        xstage_slots[cp].freed_by(tfd)
                        xn_slots[cp].freed_by(tfd)

                        def srcf(j, _xf=xfix_b):
                            return _xf[:, j * 128:(j + 1) * 128]

                        def dstf(g, _xT=xfixT[tp], _c=c):
                            return _xT[:].rearrange("p (dc tt) -> p dc tt",
                                                    tt=TTA)[:, 4 * g:4 * g + 4,
                                                            _c * 128:(_c + 1) * 128]

                        transpose_blocks(srcf, DC, dstf, tfd, vtf)
                        xfix_slot.freed_by(trd)
                    ev_tile = evd.n
                    last_mm_val = None
                    pi_free = [Slot(), Slot()]
                    pf_free = [Slot(), Slot()]
                    for ec in range(EC):
                        pi = psA if ec % 2 == 0 else psC
                        pf = psB if ec % 2 == 0 else psD
                        TEN.wait_ge(evd.sem, ev_tile)
                        if ec == 0:
                            TEN.wait_ge(wcd.sem, w_ready)
                        pi_free[ec % 2].wait(TEN)
                        pf_free[ec % 2].wait(TEN)
                        for dc in range(DC):
                            TEN.matmul(pi[:, :TTA],
                                       wib[:, dc * E + ec * 128:dc * E + ec * 128 + 128],
                                       xfixT[tp][:, dc * TTA:(dc + 1) * TTA],
                                       start=(dc == 0), stop=(dc == DC - 1))
                        mm = None
                        for dc in range(DC):
                            mm = TEN.matmul(pf[:, :TTA],
                                            wfb[:, dc * E + ec * 128:dc * E + ec * 128 + 128],
                                            xfixT[tp][:, dc * TTA:(dc + 1) * TTA],
                                            start=(dc == 0), stop=(dc == DC - 1))
                        mm.then_inc(mmd.sem, 1)
                        vm = mmd.take()
                        last_mm_val = vm
                        SCA.wait_ge(mmd.sem, vm)
                        fb_slots[ec % 2].wait(SCA)
                        SCA.activation(fb[ec % 2][:], pf[:, :TTA], AF.Sigmoid,
                                       scale=vsf[:, ec:ec + 1]).then_inc(fsd.sem, 1)
                        vf = fsd.take()
                        pf_free[ec % 2].freed_by(fsd)
                        VEC.wait_ge(mmd.sem, vm)
                        VEC.tensor_scalar(i1b[:], pi[:, :TTA], vsi[:, ec:ec + 1],
                                          None, ALU.mult).then_inc(ivd.sem, 1)
                        ivd.take()
                        pi_free[ec % 2].freed_by(ivd)
                        VEC.wait_ge(fsd.sem, vf)
                        VEC.scalar_tensor_tensor(iqn[:], fb[ec % 2][:], 1.0, i1b[:],
                                                 ALU.subtract, ALU.mult)
                        hb = hn[ec % 2]
                        hn_slots[ec % 2].wait(VEC)
                        if own and t == 0:
                            init_ap = init_own[:, ec:ec + 1]
                        else:
                            init_ap = state[:, ec:ec + 1]
                        VEC.wait_ge(stc.sem, stc_vals[ec])
                        VEC.tensor_tensor_scan(hb[:], fb[ec % 2][:], iqn[:],
                                               init_ap, ALU.mult,
                                               ALU.add).then_inc(scd.sem, 1)
                        vsc = scd.take()
                        SCA.wait_ge(scd.sem, vsc)
                        SCA.activation(state[:, ec:ec + 1], hb[:, TTA - 1:TTA],
                                       AF.Copy).then_inc(stc.sem, 1)
                        stc_vals[ec] = stc.take()
                        fb_slots[ec % 2].freed_by(scd, vsc)
                        if own:
                            SYN.wait_ge(scd.sem, vsc)
                            SYN.dma_start(
                                out=sc_h[t, :, ec * TTA:(ec + 1) * TTA],
                                in_=hb[:]).then_inc(std.sem, 16)
                            std.take(16)
                            hn_slots[ec % 2].freed_by(std)
                            hn_slots[ec % 2].also_freed_by(stc, stc_vals[ec])
                        else:
                            hn_slots[ec % 2].freed_by(stc, stc_vals[ec])
                    if own:
                        SYN.wait_ge(evd.sem, ev_tile)
                        SYN.dma_start(out=sc_xfixT[t],
                                      in_=xfixT[tp][:]).then_inc(std.sem, 16)
                        std.take(16)
                        xfixT_slots[tp].freed_by(std)
                        xfixT_slots[tp].also_freed_by(mmd, last_mm_val)
                    else:
                        xfixT_slots[tp].freed_by(mmd, last_mm_val)
                    maybe_barrier()

            attn_if_scan(x_pre, own=False)
            SCA.activation(init_own[:], state[:], AF.Copy,
                           scale=vmask[:, 0:1]).then_inc(stc.sem, 1)
            v0 = stc.take()
            for _e in range(EC):
                stc_vals[_e] = v0
            attn_if_scan(x_own, own=True)
            barrier()

        with ExitStack() as ph:
            wgb = sb(ph, "wgb", 128, DC * E, BF16)
            wob = sb(ph, "wob", 128, EC * E, BF16)
            xfT = [sb(ph, f"xfT{j}", 128, DC * TTA, BF16) for j in range(2)]
            hsb = sb(ph, "hsb", 128, EC * TTA, F32)
            silg = sb(ph, "silg", 128, EC * TTA, BF16)
            obuf = sb(ph, "obuf", 128, EC * TTA, BF16)
            hhb = [sb(ph, f"hhb{j}", 128, TTA, F32) for j in range(2)]
            t1b = sb(ph, "t1b", 128, TTA, F32)
            sstage = [sb(ph, f"sstage{j}", 128, E, F32) for j in range(2)]
            skint = sb(ph, "skint", 128, 512, I32)
            mstmp = sb(ph, "mstmp", 1, TTA, F32)
            wstage = sstage

            wstage_slots = [Slot(), Slot()]
            psV_slot = Slot()
            silg_slot = Slot()
            obuf_slot = Slot()
            rstd_slot = Slot()
            xfT_slots = [Slot(), Slot()]
            hsb_slot = Slot()
            sstage_slots = [Slot(), Slot()]
            hh_slots = [Slot(), Slot()]

            load_cast_weight(w_g, wgb, DC, E, wstage, wstage_slots)
            load_cast_weight(w_oT, wob, EC, E, wstage, wstage_slots)
            w2_ready = wcd.n
            for sl in sstage_slots:
                sl.freed_by(wcd, w2_ready)

            for t in range(NT):
                tp = t % 2
                t0 = t * TTA
                xfT_slots[tp].wait(SYN)
                SYN.dma_start(out=xfT[tp][:],
                              in_=sc_xfixT[t]).then_inc(ldx.sem, 16)
                vxf = ldx.take(16)
                hsb_slot.wait(SYN)
                SYN.dma_start(out=hsb[:],
                              in_=sc_h[t]).then_inc(ldx.sem, 16)
                vh = ldx.take(16)
                pg_free = [Slot(), Slot()]
                mm2 = None
                for ec in range(EC):
                    pg = psA if ec % 2 == 0 else psC
                    TEN.wait_ge(ldx.sem, vxf)
                    if ec == 0:
                        TEN.wait_ge(wcd.sem, w2_ready)
                    pg_free[ec % 2].wait(TEN)
                    mm = None
                    for dc in range(DC):
                        mm = TEN.matmul(pg[:, :TTA],
                                        wgb[:, dc * E + ec * 128:dc * E + ec * 128 + 128],
                                        xfT[tp][:, dc * TTA:(dc + 1) * TTA],
                                        start=(dc == 0), stop=(dc == DC - 1))
                    mm.then_inc(mmd.sem, 1)
                    vm = mmd.take()
                    if ec == EC - 1:
                        xfT_slots[tp].freed_by(mmd, vm)
                    SCA.wait_ge(mmd.sem, vm)
                    if ec == 0:
                        silg_slot.wait(SCA)
                    SCA.activation(silg[:, ec * TTA:(ec + 1) * TTA], pg[:, :TTA],
                                   AF.Silu,
                                   scale=vsg[:, ec:ec + 1]).then_inc(fsd.sem, 1)
                    fsd.take()
                    pg_free[ec % 2].freed_by(fsd)
                    VEC.wait_ge(ldx.sem, vh)
                    hq = hhb[ec % 2]
                    hh_slots[ec % 2].wait(VEC)
                    VEC.tensor_tensor(hq[:], hsb[:, ec * TTA:(ec + 1) * TTA],
                                      hsb[:, ec * TTA:(ec + 1) * TTA],
                                      ALU.mult).then_inc(qvd.sem, 1)
                    vq = qvd.take()
                    TEN.wait_ge(qvd.sem, vq)
                    TEN.matmul(psU[0:1, :TTA], ones_f[:], hq[:],
                               start=(ec == 0),
                               stop=(ec == EC - 1)).then_inc(mmd.sem, 1)
                    vm2 = mmd.take()
                    hh_slots[ec % 2].freed_by(mmd, vm2)
                vms = vm2
                VEC.wait_ge(mmd.sem, vms)
                VEC.tensor_scalar(mstmp[:], psU[0:1, :TTA], 1.0 / 2048.0, 1e-5,
                                  ALU.mult, ALU.add).then_inc(qvd.sem, 1)
                vq = qvd.take()
                SCA.wait_ge(qvd.sem, vq)
                rstd_slot.wait(SCA)
                SCA.activation(rstd_b[0:1, :TTA], mstmp[:],
                               AF.Sqrt).then_inc(qsd.sem, 1)
                vs2 = qsd.take()
                VEC.wait_ge(qsd.sem, vs2)
                psV_slot.wait(VEC)
                VEC.reciprocal(rbc_src[0:1, :TTA],
                               rstd_b[0:1, :TTA]).then_inc(qvd.sem, 1)
                vq = qvd.take()
                rstd_slot.freed_by(qvd, vq)
                TEN.wait_ge(qvd.sem, vq)
                TEN.matmul(psV[:, :TTA], ones128[:], rbc_src[:, :TTA],
                           start=True, stop=True).then_inc(mmd.sem, 1)
                vbc = mmd.take()
                VEC.wait_ge(mmd.sem, vbc)
                VEC.wait_ge(fsd.sem, fsd.n)
                obuf_slot.wait(VEC)
                lastob = None
                for ec in range(EC):
                    VEC.scalar_tensor_tensor(t1b[:],
                                             hsb[:, ec * TTA:(ec + 1) * TTA],
                                             vgs[:, ec:ec + 1], psV[:, :TTA],
                                             ALU.mult, ALU.mult)
                    lastob = VEC.tensor_tensor(obuf[:, ec * TTA:(ec + 1) * TTA],
                                               t1b[:],
                                               silg[:, ec * TTA:(ec + 1) * TTA],
                                               ALU.mult)
                lastob.then_inc(scd.sem, 1)
                vob_ready = scd.take()
                hsb_slot.freed_by(scd, vob_ready)
                psV_slot.freed_by(scd, vob_ready)
                po_free = [Slot(), Slot()]
                for tc in range(TTA // 128):
                    sp = sstage[tc % 2]
                    sstage_slots[tc % 2].wait(VEC)
                    for oc in range(4):
                        po = psB if oc % 2 == 0 else psD
                        TEN.wait_ge(scd.sem, vob_ready)
                        po_free[oc % 2].wait(TEN)
                        mm3 = None
                        for ec in range(EC):
                            mm3 = TEN.matmul(
                                po[:, :512],
                                obuf[:, ec * TTA + tc * 128:ec * TTA + tc * 128 + 128],
                                wob[:, ec * E + oc * 512:ec * E + (oc + 1) * 512],
                                start=(ec == 0), stop=(ec == EC - 1))
                        mm3.then_inc(mmd.sem, 1)
                        vm3 = mmd.take()
                        VEC.wait_ge(mmd.sem, vm3)
                        VEC.tensor_scalar(skint[:], po[:, :512], 256.0, MAGIC,
                                          ALU.mult, ALU.add)
                        VEC.tensor_scalar(sp[:, oc * 512:(oc + 1) * 512],
                                          skint[:], 1.0 / 256.0, -MAGIC / 256.0,
                                          ALU.mult, ALU.add).then_inc(ivd.sem, 1)
                        ivd.take()
                        po_free[oc % 2].freed_by(ivd)
                        obuf_slot.freed_by(mmd, vm3)
                        silg_slot.freed_by(mmd, vm3)
                    VEC.engine_nop().then_inc(scd.sem, 1)
                    vsr = scd.take()
                    SYN.wait_ge(scd.sem, vsr)
                    SYN.dma_start(out=sc_s[t0 + tc * 128:t0 + tc * 128 + 128, :],
                                  in_=sp[:]).then_inc(std.sem, 16)
                    std.take(16)
                    sstage_slots[tc % 2].freed_by(std)
                maybe_barrier()
            barrier()

        with ExitStack() as ph:
            xqT = sb(ph, "xqT", 128, DC * TOK, BF16)
            xstage = [sb(ph, f"x0stage{j}", 128, D, F32) for j in range(2)]
            _xnb = sb(ph, "xn0b", 128, D, F32)
            xn_b = [_xnb, _xnb]
            qv = sb(ph, "qv", 128, D, F32)
            q8b = sb(ph, "q8b", 128, D, FP8)
            qk = sb(ph, "qk", 128, D, I32)
            qdeq = sb(ph, "qdeq", 128, D, BF16)
            qbm = sb(ph, "qbm", 128, D // 32, F32)
            qsc = sb(ph, "qsc", 128, D // 32, F32)
            qrc = sb(ph, "qrc", 128, D // 32, F32)
            qs2 = sb(ph, "qs2", 128, D // 32, F32)
            gwT = [sb(ph, f"gwT{j}", 128, DC * ITILE, BF16) for j in range(2)]
            upT = [sb(ph, f"upT{j}", 128, DC * ITILE, BF16) for j in range(2)]
            silgate = [sb(ph, f"silgate{j}", 128, ITILE, F32) for j in range(2)]
            h2b = [sb(ph, f"h2b{j}", 128, ITILE, F32) for j in range(2)]
            h2v = [sb(ph, f"h2v{j}", 128, ITILE, F32) for j in range(2)]
            h2q8 = [sb(ph, f"h2q8{j}", 128, ITILE, FP8) for j in range(2)]
            h2k = [sb(ph, f"h2k{j}", 128, ITILE, I32) for j in range(2)]
            h2deq = [sb(ph, f"h2deq{j}", 128, ITILE, BF16) for j in range(2)]
            h2bm = [sb(ph, f"h2bm{j}", 128, ITILE // 32, F32) for j in range(2)]
            h2sc = [sb(ph, f"h2sc{j}", 128, ITILE // 32, F32) for j in range(2)]
            h2rc = [sb(ph, f"h2rc{j}", 128, ITILE // 32, F32) for j in range(2)]
            h2s2 = [sb(ph, f"h2s2{j}", 128, ITILE // 32, F32) for j in range(2)]
            h2qTs = [sb(ph, f"h2qTs{j}", 128, 512, BF16) for j in range(2)]

            xst_slots = [Slot(), Slot()]
            _xns = Slot()
            xn_slots = [_xns, _xns]
            qchain_slot = Slot()
            gwT_slots = [Slot(), Slot()]
            upT_slots = [Slot(), Slot()]
            h2qTs_slots = [Slot(), Slot()]
            psAB_slots = [Slot(), Slot()]
            silgate_slots = [Slot(), Slot()]
            h2chain_slots = [Slot(), Slot()]

            for g in range(TOK // 128):
                cp = g % 2
                xst_slots[cp].wait(SYN)
                SYN.dma_start(out=xstage[cp][:],
                              in_=x_own[g * 128:(g + 1) * 128, :]
                              ).then_inc(ldx.sem, 16)
                vx = ldx.take(16)
                xn_slots[cp].wait(VEC)
                emit_ln(xstage[cp][:], xn_b[cp][:], cp, vx, qv[:])
                xst_slots[cp].freed_by(qvd, qvd.n + 1)
                qchain_slot.wait(VEC)
                vdq = quant(xn_b[cp][:], D, qbm, qsc, qrc, qv, q8b, qk, qdeq, qs2)
                xn_slots[cp].freed_by(qvd)

                def srcq(j, _q=qdeq):
                    return _q[:, j * 128:(j + 1) * 128]

                def dstq(gg, _g=g):
                    return xqT[:].rearrange("p (dc tt) -> p dc tt",
                                            tt=TOK)[:, 4 * gg:4 * gg + 4,
                                                    _g * 128:(_g + 1) * 128]

                transpose_blocks(srcq, DC, dstq, gpd, vdq)
                qchain_slot.freed_by(evd)
            if DEBUG_OUTPUTS:
                SYN.wait_ge(evd.sem, evd.n)
                SYN.dma_start(out=dbg_xqT[:], in_=xqT[:]).then_inc(std.sem, 16)
                std.take(16)
            maybe_barrier()

            def prep_w(wsrc, dstT, dst_slots, it, par):
                dst_slots[par].wait(TEN)
                for sub in range(ITILE // 128):
                    cp = sub % 2
                    r0 = it * ITILE + sub * 128
                    xst_slots[cp].wait(SYN)
                    SYN.dma_start(out=xstage[cp][:, :D],
                                  in_=wsrc[r0:r0 + 128, :]).then_inc(ldw.sem, 16)
                    vw = ldw.take(16)
                    VEC.wait_ge(ldw.sem, vw)
                    qchain_slot.wait(VEC)
                    vdq = quant(xstage[cp][:, :D], D, qbm, qsc, qrc, qv, q8b,
                                qk, qdeq, qs2)
                    xst_slots[cp].freed_by(qvd)

                    def srcw(j, _q=qdeq):
                        return _q[:, j * 128:(j + 1) * 128]

                    def dstw(gg, _s=sub, _d=dstT, _p=par):
                        return _d[_p][:].rearrange("p (dc tt) -> p dc tt",
                                                   tt=ITILE)[
                            :, 4 * gg:4 * gg + 4, _s * 128:(_s + 1) * 128]

                    transpose_blocks(srcw, DC, dstw, gpd, vdq)
                    qchain_slot.freed_by(evd)

            c_pending = [None]
            for it in range(II // ITILE):
                par = it % 2
                prep_w(gate_w, gwT, gwT_slots, it, par)
                prep_w(up_w, upT, upT_slots, it, par)
                wready = evd.n
                if DEBUG_OUTPUTS:
                    SYN.wait_ge(evd.sem, wready)
                    SYN.dma_start(out=dbg_gwT[it],
                                  in_=gwT[par][:]).then_inc(std.sem, 16)
                    SYN.dma_start(out=dbg_upT[it],
                                  in_=upT[par][:]).then_inc(std.sem, 16)
                    std.take(32)
                lastmm = None
                for tc in range(TOK // 128):
                    pp = tc % 2
                    pg = psA if pp == 0 else psC
                    pu = psB if pp == 0 else psD
                    TEN.wait_ge(evd.sem, wready)
                    psAB_slots[pp].wait(TEN)
                    for hc in range(DC):
                        TEN.matmul(pg[:, :ITILE],
                                   xqT[:, hc * TOK + tc * 128:hc * TOK + tc * 128 + 128],
                                   gwT[par][:, hc * ITILE:(hc + 1) * ITILE],
                                   start=(hc == 0), stop=(hc == DC - 1))
                    mmu = None
                    for hc in range(DC):
                        mmu = TEN.matmul(pu[:, :ITILE],
                                         xqT[:, hc * TOK + tc * 128:hc * TOK + tc * 128 + 128],
                                         upT[par][:, hc * ITILE:(hc + 1) * ITILE],
                                         start=(hc == 0), stop=(hc == DC - 1))
                    mmu.then_inc(mmd.sem, 1)
                    vm = mmd.take()
                    lastmm = vm
                    if c_pending[0] is not None:
                        c_pending[0]()
                        c_pending[0] = None
                    SCA.wait_ge(mmd.sem, vm)
                    silgate_slots[pp].wait(SCA)
                    SCA.activation(silgate[pp][:], pg[:, :ITILE],
                                   AF.Silu).then_inc(fsd.sem, 1)
                    vf = fsd.take()
                    VEC.wait_ge(fsd.sem, vf)
                    h2chain_slots[pp].wait(VEC)
                    VEC.tensor_tensor(h2b[pp][:], silgate[pp][:], pu[:, :ITILE],
                                      ALU.mult).then_inc(qvd.sem, 1)
                    qvd.take()
                    psAB_slots[pp].freed_by(qvd)
                    silgate_slots[pp].freed_by(qvd)
                    if DEBUG_OUTPUTS:
                        VEC.engine_nop().then_inc(qvd.sem, 1)
                        _vq = qvd.take()
                        SYN.wait_ge(qvd.sem, _vq)
                        SYN.dma_start(out=dbg_h2b[it, tc],
                                      in_=h2b[pp][:]).then_inc(std.sem, 16)
                        std.take(16)
                    vdq = quant(h2b[pp][:], ITILE, h2bm[pp], h2sc[pp], h2rc[pp],
                                h2v[pp], h2q8[pp], h2k[pp], h2deq[pp], h2s2[pp])

                    def emit_tr(_it=it, _tc=tc, _pp=pp, _vdq=vdq):
                        def srch(j, _q=h2deq[_pp]):
                            return _q[:, j * 128:(j + 1) * 128]

                        sp = h2qTs[_tc % 2]
                        h2qTs_slots[_tc % 2].wait(TEN)

                        def dsth(gg, _sp=sp):
                            return _sp[:].rearrange("p (a t) -> p a t",
                                                    t=128)[:, 4 * gg:4 * gg + 4, :]

                        ev = transpose_blocks(srch, ITILE // 128, dsth, gpd, _vdq)
                        h2chain_slots[_pp].freed_by(evd, ev)
                        SYN.wait_ge(evd.sem, ev)
                        SYN.dma_start(
                            out=sc_h2qT[_tc, :, _it * (ITILE // 128) * 128:
                                        (_it + 1) * (ITILE // 128) * 128],
                            in_=sp[:],
                        ).then_inc(std.sem, 16)
                        std.take(16)
                        h2qTs_slots[_tc % 2].freed_by(std)

                    c_pending[0] = emit_tr
                gwT_slots[par].freed_by(mmd, lastmm)
                upT_slots[par].freed_by(mmd, lastmm)
                maybe_barrier()
            if c_pending[0] is not None:
                c_pending[0]()
                c_pending[0] = None
            barrier()

        with ExitStack() as ph:
            QW = 1024
            dwT = [sb(ph, f"dwT{j}", 128, ICH * HT, BF16) for j in range(2)]
            h2qsb = [sb(ph, f"h2qsb{j}", 128, ICH * 128, BF16) for j in range(2)]
            wstage = [sb(ph, f"w4stage{j}", 128, QW, F32) for j in range(2)]
            qv = sb(ph, "qv4", 128, QW, F32)
            q8b = sb(ph, "q8b4", 128, QW, FP8)
            qk = sb(ph, "qk4", 128, QW, I32)
            qdeq = sb(ph, "qdeq4", 128, QW, BF16)
            qbm = sb(ph, "qbm4", 128, QW // 32, F32)
            qsc = sb(ph, "qsc4", 128, QW // 32, F32)
            qrc = sb(ph, "qrc4", 128, QW // 32, F32)
            qs2 = sb(ph, "qs24", 128, QW // 32, F32)
            gl_b = [sb(ph, f"gl_b{j}", 128, HT, F32) for j in range(2)]
            t_sl = sb(ph, "t_sl", 128, HT, F32)
            s_sl = [sb(ph, f"s_sl{j}", 128, HT, F32) for j in range(2)]
            x_sl = [sb(ph, f"x_sl{j}", 128, HT, F32) for j in range(2)]
            o_sl = [sb(ph, f"o_sl{j}", 128, HT, F32) for j in range(2)]

            wst_slots = [Slot(), Slot()]
            qchain_slot = Slot()
            dwT_slots = [Slot(), Slot()]
            h2q_slots = [Slot(), Slot()]
            sx_slots = [Slot(), Slot()]
            o_slots = [Slot(), Slot()]
            gl_slots = [Slot(), Slot()]
            psA_slots = [Slot(), Slot()]

            for ht in range(D // HT):
                par = ht % 2
                dwT_slots[par].wait(TEN)
                for sub in range(HT // 128):
                    r0 = ht * HT + sub * 128
                    for qq in range(II // QW):
                        cp = (sub * (II // QW) + qq) % 2
                        wst_slots[cp].wait(SYN)
                        SYN.dma_start(out=wstage[cp][:],
                                      in_=down_w[r0:r0 + 128,
                                                 qq * QW:(qq + 1) * QW]
                                      ).then_inc(ldw.sem, 16)
                        vw = ldw.take(16)
                        VEC.wait_ge(ldw.sem, vw)
                        qchain_slot.wait(VEC)
                        vdq = quant(wstage[cp][:], QW, qbm, qsc, qrc, qv,
                                    q8b, qk, qdeq, qs2)
                        wst_slots[cp].freed_by(qvd)

                        def srcd(j, _q=qdeq):
                            return _q[:, j * 128:(j + 1) * 128]

                        def dstd(gg, _s=sub, _q=qq, _p=par):
                            return dwT[_p][:].rearrange("p (ic tt) -> p ic tt",
                                                        tt=HT)[
                                :, _q * (QW // 128) + 4 * gg:_q * (QW // 128) + 4 * gg + 4,
                                _s * 128:(_s + 1) * 128]

                        transpose_blocks(srcd, QW // 128, dstd, gpd, vdq)
                        qchain_slot.freed_by(evd)
                dw_ready = evd.n
                lastmm4 = None
                for tc in range(TOK // 128):
                    hp = tc % 2
                    h2q_slots[hp].wait(SYN)
                    SYN.dma_start(out=h2qsb[hp][:],
                                  in_=sc_h2qT[tc]).then_inc(ldx.sem, 16)
                    vh = ldx.take(16)
                    sx_slots[hp].wait(SYN)
                    SYN.dma_start(out=s_sl[hp][:],
                                  in_=sc_s[tc * 128:(tc + 1) * 128,
                                           ht * HT:(ht + 1) * HT]
                                  ).then_inc(ldx.sem, 16)
                    SYN.dma_start(out=x_sl[hp][:],
                                  in_=x_own[tc * 128:(tc + 1) * 128,
                                            ht * HT:(ht + 1) * HT]
                                  ).then_inc(ldx.sem, 16)
                    vx_ = ldx.take(32)
                    pd = psA if hp == 0 else psB
                    TEN.wait_ge(evd.sem, dw_ready)
                    TEN.wait_ge(ldx.sem, vh)
                    psA_slots[hp].wait(TEN)
                    mm4 = None
                    for ic in range(ICH):
                        mm4 = TEN.matmul(pd[:, :HT],
                                         h2qsb[hp][:, ic * 128:(ic + 1) * 128],
                                         dwT[par][:, ic * HT:(ic + 1) * HT],
                                         start=(ic == 0), stop=(ic == ICH - 1))
                    mm4.then_inc(mmd.sem, 1)
                    vm4 = mmd.take()
                    lastmm4 = vm4
                    h2q_slots[hp].freed_by(mmd, vm4)
                    SCA.wait_ge(mmd.sem, vm4)
                    gl_slots[hp].wait(SCA)
                    SCA.activation(gl_b[hp][:], pd[:, :HT],
                                   AF.Silu).then_inc(fsd.sem, 1)
                    vf4 = fsd.take()
                    psA_slots[hp].freed_by(fsd, vf4)
                    VEC.wait_ge(fsd.sem, vf4)
                    VEC.wait_ge(ldx.sem, vx_)
                    o_slots[hp].wait(VEC)
                    VEC.tensor_tensor(t_sl[:], s_sl[hp][:], gl_b[hp][:],
                                      ALU.mult).then_inc(qvd.sem, 1)
                    vtf_ = qvd.take()
                    VEC.wait_ge(qvd.sem, vtf_)
                    VEC.tensor_tensor(o_sl[hp][:], x_sl[hp][:], t_sl[:],
                                      ALU.add).then_inc(ivd.sem, 1)
                    vo = ivd.take()
                    sx_slots[hp].freed_by(ivd, vo)
                    gl_slots[hp].freed_by(ivd, vo)
                    SYN.wait_ge(ivd.sem, vo)
                    SYN.dma_start(out=out[tc * 128:(tc + 1) * 128,
                                          ht * HT:(ht + 1) * HT],
                                  in_=o_sl[hp][:]).then_inc(sto.sem, 16)
                    sto.take(16)
                    o_slots[hp].freed_by(sto)
                dwT_slots[par].freed_by(mmd, lastmm4)
                maybe_barrier()
            SYN.wait_ge(sto.sem, sto.n)

    return nc


def _prepare(inputs):
    x = np.ascontiguousarray(
        np.asarray(inputs["L0_attn_hidden_states"], dtype=np.float32)
    ).reshape(NCORE * TOK, D)
    w_i = np.ascontiguousarray(np.asarray(inputs["L0_attn_w_i"], np.float32))
    w_f = np.ascontiguousarray(np.asarray(inputs["L0_attn_w_f"], np.float32))
    w_g = np.ascontiguousarray(np.asarray(inputs["L0_attn_w_g"], np.float32))
    w_oT = np.ascontiguousarray(np.asarray(inputs["L0_attn_w_o"], np.float32).T)
    gate_w = np.ascontiguousarray(np.asarray(inputs["L0_ffn_gate_w"], np.float32))
    up_w = np.ascontiguousarray(np.asarray(inputs["L0_ffn_up_w"], np.float32))
    down_w = np.ascontiguousarray(np.asarray(inputs["L0_ffn_down_w"], np.float32))

    def vec128(v):
        return np.ascontiguousarray(np.asarray(v, np.float32).reshape(EC, 128).T)

    vs = {
        "s_i": vec128(inputs["L0_attn_s_i"]),
        "s_f": vec128(inputs["L0_attn_s_f"]),
        "s_g": vec128(inputs["L0_attn_s_g"]),
        "s_o": vec128(inputs["L0_attn_s_o"]),
        "gnw": vec128(inputs["L0_attn_g_norm_w"]),
    }
    ident = np.eye(128, dtype=ml_dtypes.bfloat16)

    in_maps = []
    for c in range(NCORE):
        b = c // 2
        m = {
            "x_own": x[c * TOK:(c + 1) * TOK],
            "x_pre": x[b * 2 * TOK:b * 2 * TOK + TOK],
            "w_i": w_i, "w_f": w_f, "w_g": w_g, "w_oT": w_oT,
            "gate_w": gate_w, "up_w": up_w, "down_w": down_w,
            "maskv": np.full((128, 1), float(c % 2), np.float32),
            "ident": ident,
        }
        m.update(vs)
        in_maps.append(m)
    return in_maps


def kernel(**inputs):
    in_maps = _prepare(inputs)
    nc = build_program()
    res = run_bass_kernel_spmd(nc, in_maps, core_ids=list(range(NCORE)))
    outv = np.concatenate([r["out"] for r in res.results], axis=0)
    return outv.reshape(4, 4096, 2048)


if __name__ == "__main__":
    import time
    t0 = time.time()
    nc = build_program()
    print(f"built in {time.time()-t0:.1f}s, instructions: {nc.n_instructions()}")


# version 13
# speedup vs baseline: 1.0687x; 1.0687x over previous
import numpy as np
import ml_dtypes
from contextlib import ExitStack

import concourse.bass as bass
import concourse.mybir as mybir
from concourse.bass_utils import run_bass_kernel_spmd

F32 = mybir.dt.float32
BF16 = mybir.dt.bfloat16
I32 = mybir.dt.int32
FP8 = mybir.dt.float8e4
AF = mybir.ActivationFunctionType
ALU = mybir.AluOpType
AX = mybir.AxisListType

MAGIC = 12582912.0
D = 2048
E = 2048
II = 8192
TOK = 2048
NCORE = 8
TTA = 256
NT = TOK // TTA
DC = D // 128
EC = E // 128
ICH = II // 128
ITILE = 512
HT = 512

DEBUG_OUTPUTS = False
PARANOID = False


class Ch:

    def __init__(self, sem):
        self.sem = sem
        self.n = 0

    def take(self, k=1):
        self.n += k
        return self.n


class Slot:

    def __init__(self):
        self.conds = []

    def freed_by(self, ch, val=None):
        self.conds = [(ch, ch.n if val is None else val)]

    def also_freed_by(self, ch, val=None):
        self.conds.append((ch, ch.n if val is None else val))

    def wait(self, eng):
        for ch, val in self.conds:
            eng.wait_ge(ch.sem, val)
        self.conds = []


def build_program():
    nc = bass.Bass("TRN2", target_bir_lowering=False, debug=False,
                   detect_race_conditions=False)

    x_own = nc.dram_tensor("x_own", [TOK, D], F32, kind="ExternalInput").ap()
    x_pre = nc.dram_tensor("x_pre", [TOK, D], F32, kind="ExternalInput").ap()
    w_i = nc.dram_tensor("w_i", [D, E], F32, kind="ExternalInput").ap()
    w_f = nc.dram_tensor("w_f", [D, E], F32, kind="ExternalInput").ap()
    w_g = nc.dram_tensor("w_g", [D, E], F32, kind="ExternalInput").ap()
    w_oT = nc.dram_tensor("w_oT", [E, E], F32, kind="ExternalInput").ap()
    s_i = nc.dram_tensor("s_i", [128, EC], F32, kind="ExternalInput").ap()
    s_f = nc.dram_tensor("s_f", [128, EC], F32, kind="ExternalInput").ap()
    s_g = nc.dram_tensor("s_g", [128, EC], F32, kind="ExternalInput").ap()
    s_o = nc.dram_tensor("s_o", [128, EC], F32, kind="ExternalInput").ap()
    gnw = nc.dram_tensor("gnw", [128, EC], F32, kind="ExternalInput").ap()
    maskv = nc.dram_tensor("maskv", [128, 1], F32, kind="ExternalInput").ap()
    ident = nc.dram_tensor("ident", [128, 128], BF16, kind="ExternalInput").ap()
    gate_w = nc.dram_tensor("gate_w", [II, D], F32, kind="ExternalInput").ap()
    up_w = nc.dram_tensor("up_w", [II, D], F32, kind="ExternalInput").ap()
    down_w = nc.dram_tensor("down_w", [D, II], F32, kind="ExternalInput").ap()

    out = nc.dram_tensor("out", [TOK, D], F32, kind="ExternalOutput").ap()

    skind = "ExternalOutput" if DEBUG_OUTPUTS else "Internal"
    sc_xfixT = nc.dram_tensor("sc_xfixT", [NT, 128, DC * TTA], BF16,
                              kind=skind).ap()
    sc_h = nc.dram_tensor("sc_h", [NT, 128, EC * TTA], F32, kind=skind).ap()
    sc_s = nc.dram_tensor("sc_s", [TOK, E], F32, kind=skind).ap()
    sc_h2qT = nc.dram_tensor("sc_h2qT", [TOK // 128, 128, ICH * 128], BF16,
                             kind=skind).ap()
    if DEBUG_OUTPUTS:
        dbg_xqT = nc.dram_tensor("dbg_xqT", [128, DC * TOK], BF16,
                                 kind="ExternalOutput").ap()
        dbg_gwT = nc.dram_tensor("dbg_gwT", [II // ITILE, 128, DC * ITILE], BF16,
                                 kind="ExternalOutput").ap()
        dbg_upT = nc.dram_tensor("dbg_upT", [II // ITILE, 128, DC * ITILE], BF16,
                                 kind="ExternalOutput").ap()
        dbg_h2b = nc.dram_tensor("dbg_h2b", [II // ITILE, TOK // 128, 128, ITILE],
                                 F32, kind="ExternalOutput").ap()

    TEN, VEC, SCA, GPS, SYN = nc.tensor, nc.vector, nc.scalar, nc.gpsimd, nc.sync

    with ExitStack() as top:
        def sb(stack, name, p, f, dt):
            return stack.enter_context(nc.sbuf_tensor(name, [p, f], dt))

        def sem(name):
            return Ch(top.enter_context(nc.semaphore(name)))

        vsi = sb(top, "vsi", 128, EC, F32)
        vsf = sb(top, "vsf", 128, EC, F32)
        vsg = sb(top, "vsg", 128, EC, F32)
        vso = sb(top, "vso", 128, EC, F32)
        vgn = sb(top, "vgn", 128, EC, F32)
        vgs = sb(top, "vgs", 128, EC, F32)
        vmask = sb(top, "vmask", 128, 1, F32)
        ones_f = sb(top, "ones_f", 128, 1, F32)
        ident_t = sb(top, "ident_t", 128, 128, BF16)
        state = sb(top, "state", 128, EC, F32)
        init_own = sb(top, "init_own", 128, EC, F32)
        rstd_b = sb(top, "rstd_b", 1, 256, F32)
        ones128 = sb(top, "ones128", 128, 128, F32)
        rbc_src = sb(top, "rbc_src", 128, TTA, F32)

        psA = top.enter_context(nc.psum_tensor("psA", [128, 512], F32))
        psB = top.enter_context(nc.psum_tensor("psB", [128, 512], F32))
        psC = top.enter_context(nc.psum_tensor("psC", [128, 512], F32))
        psD = top.enter_context(nc.psum_tensor("psD", [128, 512], F32))
        psT0 = top.enter_context(nc.psum_tensor("psT0", [128, 512], BF16))
        psT1 = top.enter_context(nc.psum_tensor("psT1", [128, 512], BF16))
        psU = top.enter_context(nc.psum_tensor("psU", [128, 512], F32))
        psV = top.enter_context(nc.psum_tensor("psV", [128, 256], F32))

        ld = sem("ld")
        ldx = sem("ldx")
        ldw = sem("ldw")
        lnd = sem("lnd")
        lnv = sem("lnv")
        lns = sem("lns")
        tfd = sem("tfd")
        trd = sem("trd")
        evd = sem("evd")
        mmd = sem("mmd")
        ivd = sem("ivd")
        fsd = sem("fsd")
        scd = sem("scd")
        std = sem("std")
        sto = sem("sto")
        qvd = sem("qvd")
        qsd = sem("qsd")
        gpd = sem("gpd")
        qvd2 = sem("qvd2")
        qsd2 = sem("qsd2")
        gpd2 = sem("gpd2")
        wcd = sem("wcd")
        stc = sem("stc")
        bar = [sem(f"bar{i}") for i in range(5)]

        ENGS = [SYN, TEN, VEC, SCA, GPS]

        def barrier():
            vals = []
            for i, e in enumerate(ENGS):
                vals.append(bar[i].take())
                e.sem_inc(bar[i].sem, 1)
            for i, e in enumerate(ENGS):
                for j in range(5):
                    if j != i:
                        e.wait_ge(bar[j].sem, vals[j])

        def maybe_barrier():
            if PARANOID:
                barrier()

        def load_cast_weight(wsrc, dst, nchunks, cols, wstage, wstage_slots):
            for k in range(nchunks):
                sl = wstage_slots[k % 2]
                sl.wait(SYN)
                SYN.dma_start(out=wstage[k % 2][:, :cols],
                              in_=wsrc[k * 128:(k + 1) * 128, :cols]
                              ).then_inc(ldw.sem, 16)
                v = ldw.take(16)
                SCA.wait_ge(ldw.sem, v)
                SCA.activation(dst[:, k * cols:(k + 1) * cols],
                               wstage[k % 2][:, :cols], AF.Copy
                               ).then_inc(wcd.sem, 1)
                wcd.take()
                sl.freed_by(wcd)

        def tofix(src_ap, kint_ap, out_ap):
            VEC.tensor_scalar(kint_ap, src_ap, 256.0, MAGIC, ALU.mult, ALU.add)
            return VEC.tensor_scalar(out_ap, kint_ap, 1.0 / 256.0, -MAGIC / 256.0,
                                     ALU.mult, ALU.add)

        C448 = float(np.float32(1.0 / 448.0))
        C448Q = C448 * 2.0 ** -9

        def quant(src_ap, W, bm, sc, rc, v, q8, k8, deq, sc2):
            nb = W // 32
            s3 = src_ap.rearrange("p (b c) -> p b c", c=32)
            VEC.tensor_reduce(bm[:, :nb], s3, AX.X, ALU.max,
                              apply_absolute_value=True).then_inc(qvd.sem, 1)
            v1 = qvd.take()
            SCA.wait_ge(qvd.sem, v1)
            SCA.activation(sc[:, :nb], bm[:, :nb], AF.Copy,
                           scale=C448, bias=1e-38)
            SCA.activation(sc2[:, :nb], bm[:, :nb], AF.Copy,
                           scale=C448Q, bias=1e-38 * 2.0 ** -9).then_inc(qsd.sem, 1)
            s1 = qsd.take()
            VEC.wait_ge(qsd.sem, s1)
            VEC.reciprocal(rc[:, :nb], sc[:, :nb]).then_inc(qvd.sem, 1)
            v2 = qvd.take()
            VEC.wait_ge(qvd.sem, v2)
            rc3 = rc[:, :nb].unsqueeze(2).broadcast_to([128, nb, 32])
            VEC.scalar_tensor_tensor(
                v[:, :W].rearrange("p (b c) -> p b c", c=32), s3,
                0.5, rc3, ALU.mult, ALU.mult).then_inc(qvd.sem, 1)
            vq = qvd.take()
            SCA.wait_ge(qvd.sem, vq)
            SCA.wait_ge(gpd.sem, gpd.n)
            SCA.activation(q8[:, :W], v[:, :W], AF.Copy)
            SCA.activation(k8[:, :W], q8[:, :W], AF.Copy,
                           scale=1024.0).then_inc(qsd.sem, 1)
            vs = qsd.take()
            sc3 = sc2[:, :nb].unsqueeze(2).broadcast_to([128, nb, 32])
            VEC.wait_ge(qsd.sem, vs)
            VEC.tensor_tensor(
                deq[:, :W].rearrange("p (b c) -> p b c", c=32),
                k8[:, :W].rearrange("p (b c) -> p b c", c=32),
                sc3, ALU.mult).then_inc(gpd.sem, 1)
            return gpd.take()

        def make_qpipe(ph, tag, W, dup=2):
            sets = []
            for s in range(dup):
                sets.append({
                    "bm": sb(ph, f"{tag}bm{s}", 128, W // 32, F32),
                    "sc": sb(ph, f"{tag}sc{s}", 128, W // 32, F32),
                    "s2": sb(ph, f"{tag}s2{s}", 128, W // 32, F32),
                    "rc": sb(ph, f"{tag}rc{s}", 128, W // 32, F32),
                    "qv": sb(ph, f"{tag}qv{s}", 128, W, F32),
                    "qk": sb(ph, f"{tag}qk{s}", 128, W, I32),
                    "qs_val": 0,
                    "vg_val": 0,
                })
            return {"sets": sets, "q8": sb(ph, f"{tag}q8", 128, W, FP8),
                    "dq": sb(ph, f"{tag}dq", 128, W, BF16),
                    "ev_val": 0, "k": 0, "pending": None}

        def q_feed(qp, src_ap, W, dma_ch=None, dma_val=None):
            st = qp["sets"][qp["k"] % len(qp["sets"])]
            qp["k"] += 1
            nb = W // 32
            s3 = src_ap.rearrange("p (b c) -> p b c", c=32)
            if dma_val is not None:
                VEC.wait_ge(dma_ch.sem, dma_val)
            VEC.tensor_reduce(st["bm"][:, :nb], s3, AX.X, ALU.max,
                              apply_absolute_value=True).then_inc(qvd2.sem, 1)
            VEC.wait_ge(qvd2.sem, qvd2.take())
            VEC.tensor_scalar(st["sc"][:, :nb], st["bm"][:, :nb], C448,
                              1e-38, ALU.mult, ALU.add)
            VEC.tensor_scalar(st["s2"][:, :nb], st["bm"][:, :nb], C448Q,
                              1e-38 * 2.0 ** -9, ALU.mult,
                              ALU.add).then_inc(qvd2.sem, 1)
            VEC.wait_ge(qvd2.sem, qvd2.take())
            VEC.reciprocal(st["rc"][:, :nb],
                           st["sc"][:, :nb]).then_inc(qvd2.sem, 1)
            VEC.wait_ge(qvd2.sem, qvd2.take())
            if st["qs_val"]:
                VEC.wait_ge(qsd2.sem, st["qs_val"])
            rc3 = st["rc"][:, :nb].unsqueeze(2).broadcast_to([128, nb, 32])
            VEC.scalar_tensor_tensor(
                st["qv"][:, :W].rearrange("p (b c) -> p b c", c=32), s3,
                0.5, rc3, ALU.mult, ALU.mult).then_inc(qvd2.sem, 1)
            vq = qvd2.take()
            SCA.wait_ge(qvd2.sem, vq)
            if st["vg_val"]:
                SCA.wait_ge(gpd2.sem, st["vg_val"])
            SCA.activation(qp["q8"][:, :W], st["qv"][:, :W], AF.Copy)
            SCA.activation(st["qk"][:, :W], qp["q8"][:, :W], AF.Copy,
                           scale=1024.0).then_inc(qsd2.sem, 1)
            vs = qsd2.take()
            st["qs_val"] = vs
            vg = gpd2.take()
            st["vg_val"] = vg
            if qp["pending"] is not None:
                qp["pending"]()

            def _deq(_st=st, _W=W, _nb=nb, _vs=vs):
                VEC.wait_ge(qsd2.sem, _vs)
                if qp["ev_val"]:
                    VEC.wait_ge(evd.sem, qp["ev_val"])
                sc3 = _st["s2"][:, :_nb].unsqueeze(2).broadcast_to(
                    [128, _nb, 32])
                VEC.tensor_tensor(
                    qp["dq"][:, :_W].rearrange("p (b c) -> p b c", c=32),
                    _st["qk"][:, :_W].rearrange("p (b c) -> p b c", c=32),
                    sc3, ALU.mult).then_inc(gpd2.sem, 1)

            qp["pending"] = _deq
            return vg, vq

        def q_flush(qp):
            if qp["pending"] is not None:
                qp["pending"]()
                qp["pending"] = None

        lnst = sb(top, "lnst", 128, 16, F32)
        ln_free = [Slot(), Slot()]

        def emit_ln(x_ap, xn_ap, par, x_ready_val, lntrash):
            c0 = par * 8
            st = lnst
            SCA.wait_ge(ldx.sem, x_ready_val)
            ln_free[par].wait(SCA)
            SCA.activation(lntrash[:], x_ap, AF.Square,
                           accum_out=st[:, c0:c0 + 1])
            SCA.activation(lntrash[:], x_ap, AF.Copy,
                           accum_out=st[:, c0 + 1:c0 + 2])
            SCA.activation(st[:, c0 + 2:c0 + 3], st[:, c0 + 1:c0 + 2],
                           AF.Copy, scale=1.0 / 2048.0)
            SCA.activation(st[:, c0 + 4:c0 + 5], st[:, c0 + 1:c0 + 2],
                           AF.Square, scale=1.0 / 2048.0)
            SCA.activation(st[:, c0 + 3:c0 + 4], st[:, c0:c0 + 1],
                           AF.Copy, scale=1.0 / 2048.0,
                           bias=1e-5).then_inc(lnd.sem, 1)
            vl = lnd.take()
            VEC.wait_ge(lnd.sem, vl)
            VEC.tensor_tensor(st[:, c0 + 5:c0 + 6], st[:, c0 + 3:c0 + 4],
                              st[:, c0 + 4:c0 + 5],
                              ALU.subtract).then_inc(lnv.sem, 1)
            vv = lnv.take()
            ln_free[par].freed_by(lnv, vv)
            SCA.wait_ge(lnv.sem, vv)
            SCA.activation(st[:, c0 + 6:c0 + 7], st[:, c0 + 5:c0 + 6],
                           AF.Sqrt).then_inc(lns.sem, 1)
            vq = lns.take()
            VEC.wait_ge(lns.sem, vq)
            VEC.reciprocal(st[:, c0 + 7:c0 + 8],
                           st[:, c0 + 6:c0 + 7]).then_inc(lnv.sem, 1)
            vv2 = lnv.take()
            VEC.wait_ge(lnv.sem, vv2)
            VEC.tensor_scalar(xn_ap, x_ap, st[:, c0 + 2:c0 + 3],
                              st[:, c0 + 7:c0 + 8], ALU.subtract, ALU.mult)

        tr_state = {"last_evac": [0, 0]}

        def transpose_blocks(src_fn, nblk, dst_fn, ready_ch, ready_val,
                             extra_wait=None):
            ngr = (nblk + 3) // 4
            for g in range(ngr):
                bank = (tr_state.setdefault("g", 0)) % 2
                tr_state["g"] = tr_state.get("g", 0) + 1
                ps = psT0 if bank == 0 else psT1
                TEN.wait_ge(ready_ch.sem, ready_val)
                if extra_wait is not None:
                    extra_wait.wait(TEN)
                    extra_wait = None
                if tr_state["last_evac"][bank]:
                    TEN.wait_ge(evd.sem, tr_state["last_evac"][bank])
                n_in_g = min(4 * g + 4, nblk) - 4 * g
                last = None
                for q in range(n_in_g):
                    j = 4 * g + q
                    last = TEN.transpose(ps[:, q * 128:(q + 1) * 128],
                                         src_fn(j), ident_t[:])
                last.then_inc(trd.sem, 1)
                tv = trd.take()
                SCA.wait_ge(trd.sem, tv)
                SCA.activation(dst_fn(g),
                               ps[:, :n_in_g * 128].rearrange(
                                   "p (a t) -> p a t", t=128),
                               AF.Copy).then_inc(evd.sem, 1)
                tr_state["last_evac"][bank] = evd.take()
            return evd.n

        for src, dst in ((s_i, vsi), (s_f, vsf), (s_g, vsg), (s_o, vso),
                         (gnw, vgn), (maskv, vmask), (ident, ident_t)):
            SYN.dma_start(out=dst[:], in_=src[:]).then_inc(ld.sem, 16)
            ld.take(16)
        VEC.wait_ge(ld.sem, ld.n)
        VEC.memset(ones_f[:], 1.0)
        VEC.memset(ones128[:], 1.0)
        VEC.memset(rbc_src[:], 0.0)
        VEC.scalar_tensor_tensor(vgs[:], vgn[:], -1.0, vso[:], ALU.mult, ALU.mult)
        VEC.memset(state[:], 0.0)

        with ExitStack() as ph:
            wib = sb(ph, "wib", 128, DC * E, BF16)
            wfb = sb(ph, "wfb", 128, DC * E, BF16)
            xstage = [sb(ph, f"xstage{j}", 128, D, F32) for j in range(2)]
            wstage = xstage
            xn_b = [sb(ph, f"xn{j}", 128, D, F32) for j in range(2)]
            kint = sb(ph, "kint", 128, D, I32)
            xfix_b = sb(ph, "xfix_b", 128, D, BF16)
            xfixT = [sb(ph, f"xfixT{j}", 128, DC * TTA, BF16) for j in range(2)]
            fb = [sb(ph, f"fb{j}", 128, TTA, F32) for j in range(2)]
            lnt = sb(ph, "lnt", 128, D, F32)
            i1b = sb(ph, "i1b", 128, TTA, F32)
            iqn = sb(ph, "iqn", 128, TTA, F32)
            hn = [sb(ph, f"hn{j}", 128, TTA, F32) for j in range(2)]

            wstage_slots = [Slot(), Slot()]
            xstage_slots = [Slot(), Slot()]
            xn_slots = [Slot(), Slot()]
            xfix_slot = Slot()
            xfixT_slots = [Slot(), Slot()]
            hn_slots = [Slot(), Slot()]
            fb_slots = [Slot(), Slot()]
            stc_vals = [0] * EC

            load_cast_weight(w_i, wib, DC, E, wstage, wstage_slots)
            load_cast_weight(w_f, wfb, DC, E, wstage, wstage_slots)
            w_ready = wcd.n
            for sl in xstage_slots:
                sl.freed_by(wcd, w_ready)

            def attn_if_scan(x_src, own):
                for t in range(NT):
                    tp = t % 2
                    xfixT_slots[tp].wait(TEN)
                    for c in range(2):
                        cp = c
                        row0 = t * TTA + c * 128
                        xstage_slots[cp].wait(SYN)
                        SYN.dma_start(out=xstage[cp][:],
                                      in_=x_src[row0:row0 + 128, :]
                                      ).then_inc(ldx.sem, 16)
                        vx = ldx.take(16)
                        xn_slots[cp].wait(VEC)
                        emit_ln(xstage[cp][:], xn_b[cp][:], cp, vx, lnt[:])
                        xfix_slot.wait(VEC)
                        tofix(xn_b[cp][:], kint[:], xfix_b[:]).then_inc(tfd.sem, 1)
                        vtf = tfd.take()
                        xstage_slots[cp].freed_by(tfd)
                        xn_slots[cp].freed_by(tfd)

                        def srcf(j, _xf=xfix_b):
                            return _xf[:, j * 128:(j + 1) * 128]

                        def dstf(g, _xT=xfixT[tp], _c=c):
                            return _xT[:].rearrange("p (dc tt) -> p dc tt",
                                                    tt=TTA)[:, 4 * g:4 * g + 4,
                                                            _c * 128:(_c + 1) * 128]

                        transpose_blocks(srcf, DC, dstf, tfd, vtf)
                        xfix_slot.freed_by(trd)
                    ev_tile = evd.n
                    last_mm_val = None
                    pi_free = [Slot(), Slot()]
                    pf_free = [Slot(), Slot()]
                    for ec in range(EC):
                        pi = psA if ec % 2 == 0 else psC
                        pf = psB if ec % 2 == 0 else psD
                        TEN.wait_ge(evd.sem, ev_tile)
                        if ec == 0:
                            TEN.wait_ge(wcd.sem, w_ready)
                        pi_free[ec % 2].wait(TEN)
                        pf_free[ec % 2].wait(TEN)
                        for dc in range(DC):
                            TEN.matmul(pi[:, :TTA],
                                       wib[:, dc * E + ec * 128:dc * E + ec * 128 + 128],
                                       xfixT[tp][:, dc * TTA:(dc + 1) * TTA],
                                       start=(dc == 0), stop=(dc == DC - 1))
                        mm = None
                        for dc in range(DC):
                            mm = TEN.matmul(pf[:, :TTA],
                                            wfb[:, dc * E + ec * 128:dc * E + ec * 128 + 128],
                                            xfixT[tp][:, dc * TTA:(dc + 1) * TTA],
                                            start=(dc == 0), stop=(dc == DC - 1))
                        mm.then_inc(mmd.sem, 1)
                        vm = mmd.take()
                        last_mm_val = vm
                        SCA.wait_ge(mmd.sem, vm)
                        fb_slots[ec % 2].wait(SCA)
                        SCA.activation(fb[ec % 2][:], pf[:, :TTA], AF.Sigmoid,
                                       scale=vsf[:, ec:ec + 1]).then_inc(fsd.sem, 1)
                        vf = fsd.take()
                        pf_free[ec % 2].freed_by(fsd)
                        VEC.wait_ge(mmd.sem, vm)
                        VEC.tensor_scalar(i1b[:], pi[:, :TTA], vsi[:, ec:ec + 1],
                                          None, ALU.mult).then_inc(ivd.sem, 1)
                        ivd.take()
                        pi_free[ec % 2].freed_by(ivd)
                        VEC.wait_ge(fsd.sem, vf)
                        VEC.scalar_tensor_tensor(iqn[:], fb[ec % 2][:], 1.0, i1b[:],
                                                 ALU.subtract, ALU.mult)
                        hb = hn[ec % 2]
                        hn_slots[ec % 2].wait(VEC)
                        if own and t == 0:
                            init_ap = init_own[:, ec:ec + 1]
                        else:
                            init_ap = state[:, ec:ec + 1]
                        VEC.wait_ge(stc.sem, stc_vals[ec])
                        VEC.tensor_tensor_scan(hb[:], fb[ec % 2][:], iqn[:],
                                               init_ap, ALU.mult,
                                               ALU.add).then_inc(scd.sem, 1)
                        vsc = scd.take()
                        SCA.wait_ge(scd.sem, vsc)
                        SCA.activation(state[:, ec:ec + 1], hb[:, TTA - 1:TTA],
                                       AF.Copy).then_inc(stc.sem, 1)
                        stc_vals[ec] = stc.take()
                        fb_slots[ec % 2].freed_by(scd, vsc)
                        if own:
                            SYN.wait_ge(scd.sem, vsc)
                            SYN.dma_start(
                                out=sc_h[t, :, ec * TTA:(ec + 1) * TTA],
                                in_=hb[:]).then_inc(std.sem, 16)
                            std.take(16)
                            hn_slots[ec % 2].freed_by(std)
                            hn_slots[ec % 2].also_freed_by(stc, stc_vals[ec])
                        else:
                            hn_slots[ec % 2].freed_by(stc, stc_vals[ec])
                    if own:
                        SYN.wait_ge(evd.sem, ev_tile)
                        SYN.dma_start(out=sc_xfixT[t],
                                      in_=xfixT[tp][:]).then_inc(std.sem, 16)
                        std.take(16)
                        xfixT_slots[tp].freed_by(std)
                        xfixT_slots[tp].also_freed_by(mmd, last_mm_val)
                    else:
                        xfixT_slots[tp].freed_by(mmd, last_mm_val)
                    maybe_barrier()

            attn_if_scan(x_pre, own=False)
            SCA.activation(init_own[:], state[:], AF.Copy,
                           scale=vmask[:, 0:1]).then_inc(stc.sem, 1)
            v0 = stc.take()
            for _e in range(EC):
                stc_vals[_e] = v0
            attn_if_scan(x_own, own=True)
            barrier()

        with ExitStack() as ph:
            wgb = sb(ph, "wgb", 128, DC * E, BF16)
            wob = sb(ph, "wob", 128, EC * E, BF16)
            xfT = [sb(ph, f"xfT{j}", 128, DC * TTA, BF16) for j in range(2)]
            hsb = sb(ph, "hsb", 128, EC * TTA, F32)
            silg = sb(ph, "silg", 128, EC * TTA, BF16)
            obuf = sb(ph, "obuf", 128, EC * TTA, BF16)
            hhb = [sb(ph, f"hhb{j}", 128, TTA, F32) for j in range(2)]
            t1b = sb(ph, "t1b", 128, TTA, F32)
            sstage = [sb(ph, f"sstage{j}", 128, E, F32) for j in range(2)]
            skint = sb(ph, "skint", 128, 512, I32)
            mstmp = sb(ph, "mstmp", 1, TTA, F32)
            wstage = sstage

            wstage_slots = [Slot(), Slot()]
            psV_slot = Slot()
            silg_slot = Slot()
            obuf_slot = Slot()
            rstd_slot = Slot()
            xfT_slots = [Slot(), Slot()]
            hsb_slot = Slot()
            sstage_slots = [Slot(), Slot()]
            hh_slots = [Slot(), Slot()]

            load_cast_weight(w_g, wgb, DC, E, wstage, wstage_slots)
            load_cast_weight(w_oT, wob, EC, E, wstage, wstage_slots)
            w2_ready = wcd.n
            for sl in sstage_slots:
                sl.freed_by(wcd, w2_ready)

            for t in range(NT):
                tp = t % 2
                t0 = t * TTA
                xfT_slots[tp].wait(SYN)
                SYN.dma_start(out=xfT[tp][:],
                              in_=sc_xfixT[t]).then_inc(ldx.sem, 16)
                vxf = ldx.take(16)
                hsb_slot.wait(SYN)
                SYN.dma_start(out=hsb[:],
                              in_=sc_h[t]).then_inc(ldx.sem, 16)
                vh = ldx.take(16)
                pg_free = [Slot(), Slot()]
                mm2 = None
                for ec in range(EC):
                    pg = psA if ec % 2 == 0 else psC
                    TEN.wait_ge(ldx.sem, vxf)
                    if ec == 0:
                        TEN.wait_ge(wcd.sem, w2_ready)
                    pg_free[ec % 2].wait(TEN)
                    mm = None
                    for dc in range(DC):
                        mm = TEN.matmul(pg[:, :TTA],
                                        wgb[:, dc * E + ec * 128:dc * E + ec * 128 + 128],
                                        xfT[tp][:, dc * TTA:(dc + 1) * TTA],
                                        start=(dc == 0), stop=(dc == DC - 1))
                    mm.then_inc(mmd.sem, 1)
                    vm = mmd.take()
                    if ec == EC - 1:
                        xfT_slots[tp].freed_by(mmd, vm)
                    SCA.wait_ge(mmd.sem, vm)
                    if ec == 0:
                        silg_slot.wait(SCA)
                    SCA.activation(silg[:, ec * TTA:(ec + 1) * TTA], pg[:, :TTA],
                                   AF.Silu,
                                   scale=vsg[:, ec:ec + 1]).then_inc(fsd.sem, 1)
                    fsd.take()
                    pg_free[ec % 2].freed_by(fsd)
                    VEC.wait_ge(ldx.sem, vh)
                    hq = hhb[ec % 2]
                    hh_slots[ec % 2].wait(VEC)
                    VEC.tensor_tensor(hq[:], hsb[:, ec * TTA:(ec + 1) * TTA],
                                      hsb[:, ec * TTA:(ec + 1) * TTA],
                                      ALU.mult).then_inc(qvd.sem, 1)
                    vq = qvd.take()
                    TEN.wait_ge(qvd.sem, vq)
                    TEN.matmul(psU[0:1, :TTA], ones_f[:], hq[:],
                               start=(ec == 0),
                               stop=(ec == EC - 1)).then_inc(mmd.sem, 1)
                    vm2 = mmd.take()
                    hh_slots[ec % 2].freed_by(mmd, vm2)
                vms = vm2
                VEC.wait_ge(mmd.sem, vms)
                VEC.tensor_scalar(mstmp[:], psU[0:1, :TTA], 1.0 / 2048.0, 1e-5,
                                  ALU.mult, ALU.add).then_inc(qvd.sem, 1)
                vq = qvd.take()
                SCA.wait_ge(qvd.sem, vq)
                rstd_slot.wait(SCA)
                SCA.activation(rstd_b[0:1, :TTA], mstmp[:],
                               AF.Sqrt).then_inc(qsd.sem, 1)
                vs2 = qsd.take()
                VEC.wait_ge(qsd.sem, vs2)
                psV_slot.wait(VEC)
                VEC.reciprocal(rbc_src[0:1, :TTA],
                               rstd_b[0:1, :TTA]).then_inc(qvd.sem, 1)
                vq = qvd.take()
                rstd_slot.freed_by(qvd, vq)
                TEN.wait_ge(qvd.sem, vq)
                TEN.matmul(psV[:, :TTA], ones128[:], rbc_src[:, :TTA],
                           start=True, stop=True).then_inc(mmd.sem, 1)
                vbc = mmd.take()
                VEC.wait_ge(mmd.sem, vbc)
                VEC.wait_ge(fsd.sem, fsd.n)
                obuf_slot.wait(VEC)
                lastob = None
                for ec in range(EC):
                    VEC.scalar_tensor_tensor(t1b[:],
                                             hsb[:, ec * TTA:(ec + 1) * TTA],
                                             vgs[:, ec:ec + 1], psV[:, :TTA],
                                             ALU.mult, ALU.mult)
                    lastob = VEC.tensor_tensor(obuf[:, ec * TTA:(ec + 1) * TTA],
                                               t1b[:],
                                               silg[:, ec * TTA:(ec + 1) * TTA],
                                               ALU.mult)
                lastob.then_inc(scd.sem, 1)
                vob_ready = scd.take()
                hsb_slot.freed_by(scd, vob_ready)
                psV_slot.freed_by(scd, vob_ready)
                po_free = [Slot(), Slot()]
                for tc in range(TTA // 128):
                    sp = sstage[tc % 2]
                    sstage_slots[tc % 2].wait(VEC)
                    for oc in range(4):
                        po = psB if oc % 2 == 0 else psD
                        TEN.wait_ge(scd.sem, vob_ready)
                        po_free[oc % 2].wait(TEN)
                        mm3 = None
                        for ec in range(EC):
                            mm3 = TEN.matmul(
                                po[:, :512],
                                obuf[:, ec * TTA + tc * 128:ec * TTA + tc * 128 + 128],
                                wob[:, ec * E + oc * 512:ec * E + (oc + 1) * 512],
                                start=(ec == 0), stop=(ec == EC - 1))
                        mm3.then_inc(mmd.sem, 1)
                        vm3 = mmd.take()
                        VEC.wait_ge(mmd.sem, vm3)
                        VEC.tensor_scalar(skint[:], po[:, :512], 256.0, MAGIC,
                                          ALU.mult, ALU.add)
                        VEC.tensor_scalar(sp[:, oc * 512:(oc + 1) * 512],
                                          skint[:], 1.0 / 256.0, -MAGIC / 256.0,
                                          ALU.mult, ALU.add).then_inc(ivd.sem, 1)
                        ivd.take()
                        po_free[oc % 2].freed_by(ivd)
                        obuf_slot.freed_by(mmd, vm3)
                        silg_slot.freed_by(mmd, vm3)
                    VEC.engine_nop().then_inc(scd.sem, 1)
                    vsr = scd.take()
                    SYN.wait_ge(scd.sem, vsr)
                    SYN.dma_start(out=sc_s[t0 + tc * 128:t0 + tc * 128 + 128, :],
                                  in_=sp[:]).then_inc(std.sem, 16)
                    std.take(16)
                    sstage_slots[tc % 2].freed_by(std)
                maybe_barrier()
            barrier()

        with ExitStack() as ph:
            xqT = sb(ph, "xqT", 128, DC * TOK, BF16)
            xstage = [sb(ph, f"x0stage{j}", 128, D, F32) for j in range(2)]
            qp = make_qpipe(ph, "qp", D, dup=2)
            gwT = [sb(ph, f"gwT{j}", 128, DC * ITILE, BF16) for j in range(2)]
            upT = [sb(ph, f"upT{j}", 128, DC * ITILE, BF16) for j in range(2)]
            silgate = [sb(ph, f"silgate{j}", 128, ITILE, F32) for j in range(2)]
            h2b = [sb(ph, f"h2b{j}", 128, ITILE, F32) for j in range(2)]
            h2v = [sb(ph, f"h2v{j}", 128, ITILE, F32) for j in range(2)]
            h2q8s = sb(ph, "h2q8s", 128, ITILE, FP8)
            h2q8 = [h2q8s, h2q8s]
            h2k = [sb(ph, f"h2k{j}", 128, ITILE, I32) for j in range(2)]
            h2deq = [sb(ph, f"h2deq{j}", 128, ITILE, BF16) for j in range(2)]
            h2bm = [sb(ph, f"h2bm{j}", 128, ITILE // 32, F32) for j in range(2)]
            h2sc = [sb(ph, f"h2sc{j}", 128, ITILE // 32, F32) for j in range(2)]
            h2rc = [sb(ph, f"h2rc{j}", 128, ITILE // 32, F32) for j in range(2)]
            h2s2 = [sb(ph, f"h2s2{j}", 128, ITILE // 32, F32) for j in range(2)]
            _h2qTs = sb(ph, "h2qTs", 128, 512, BF16)
            h2qTs = [_h2qTs, _h2qTs]

            xst_slots = [Slot(), Slot()]
            _xns = Slot()
            xn_slots = [_xns, _xns]
            gwT_slots = [Slot(), Slot()]
            upT_slots = [Slot(), Slot()]
            _h2qTs_slot = Slot()
            h2qTs_slots = [_h2qTs_slot, _h2qTs_slot]
            psAB_slots = [Slot(), Slot()]
            silgate_slots = [Slot(), Slot()]
            h2chain_slots = [Slot(), Slot()]

            def b0_transp(g, vg):
                def srcq(j, _q=qp["dq"]):
                    return _q[:, j * 128:(j + 1) * 128]

                def dstq(gg, _g=g):
                    return xqT[:].rearrange("p (dc tt) -> p dc tt",
                                            tt=TOK)[:, 4 * gg:4 * gg + 4,
                                                    _g * 128:(_g + 1) * 128]

                qp["ev_val"] = transpose_blocks(srcq, DC, dstq, gpd2, vg)

            b0_vgs = {}
            for g in range(TOK // 128):
                cp = g % 2
                xst_slots[cp].wait(SYN)
                SYN.dma_start(out=xstage[cp][:],
                              in_=x_own[g * 128:(g + 1) * 128, :]
                              ).then_inc(ldx.sem, 16)
                vx = ldx.take(16)
                xnbuf = qp["sets"][1 - cp]["qv"]
                _qsv = qp["sets"][1 - cp]["qs_val"]
                if _qsv:
                    VEC.wait_ge(qsd2.sem, _qsv)
                emit_ln(xstage[cp][:], xnbuf[:], cp, vx,
                        qp["sets"][cp]["qv"][:])
                vg, vq = q_feed(qp, xnbuf[:], D)
                b0_vgs[g] = vg
                xst_slots[cp].freed_by(qvd2, vq)
                if g > 0:
                    b0_transp(g - 1, b0_vgs[g - 1])
            q_flush(qp)
            b0_transp(TOK // 128 - 1, b0_vgs[TOK // 128 - 1])
            if DEBUG_OUTPUTS:
                SYN.wait_ge(evd.sem, evd.n)
                SYN.dma_start(out=dbg_xqT[:], in_=xqT[:]).then_inc(std.sem, 16)
                std.take(16)
            maybe_barrier()

            NIT = II // ITILE

            def issue_wload(p, j):
                c = p * 8 + j
                xst_slots[c % 2].wait(SYN)
                w = gate_w if j < 4 else up_w
                r0 = p * ITILE + (j % 4) * 128
                SYN.dma_start(out=xstage[c % 2][:, :D],
                              in_=w[r0:r0 + 128, :]).then_inc(ldw.sem, 16)
                return ldw.take(16)

            def do_feed(p, j, vw):
                c = p * 8 + j
                vg, vq = q_feed(qp, xstage[c % 2][:, :D], D, ldw, vw)
                xst_slots[c % 2].freed_by(qvd2, vq)
                return vg

            def do_transp(p, j, vg):
                par2 = p % 2
                dstT = gwT if j < 4 else upT
                slots = gwT_slots if j < 4 else upT_slots
                if j == 0 or j == 4:
                    slots[par2].wait(TEN)
                sub = j % 4

                def srcw(jj, _q=qp["dq"]):
                    return _q[:, jj * 128:(jj + 1) * 128]

                def dstw(gg, _s=sub, _d=dstT, _p=par2):
                    return _d[_p][:].rearrange("p (dc tt) -> p dc tt",
                                               tt=ITILE)[
                        :, 4 * gg:4 * gg + 4, _s * 128:(_s + 1) * 128]

                ev = transpose_blocks(srcw, DC, dstw, gpd2, vg)
                qp["ev_val"] = ev
                return ev

            vws = {0: issue_wload(0, 0), 1: issue_wload(0, 1)}
            vgs_ = {}
            for j in range(8):
                vgs_[j] = do_feed(0, j, vws[j])
                if j + 2 < 8:
                    vws[j + 2] = issue_wload(0, j + 2)
                if j > 0:
                    do_transp(0, j - 1, vgs_[j - 1])
            q_flush(qp)
            batch_ready = do_transp(0, 7, vgs_[7])

            c_pending = [None]
            for it in range(NIT):
                par = it % 2
                prep = it + 1 < NIT
                if prep:
                    vws = {0: issue_wload(it + 1, 0),
                           1: issue_wload(it + 1, 1)}
                    vgs_ = {}
                wready = batch_ready
                if DEBUG_OUTPUTS:
                    SYN.wait_ge(evd.sem, wready)
                    SYN.dma_start(out=dbg_gwT[it],
                                  in_=gwT[par][:]).then_inc(std.sem, 16)
                    SYN.dma_start(out=dbg_upT[it],
                                  in_=upT[par][:]).then_inc(std.sem, 16)
                    std.take(32)
                lastmm = None
                for tc in range(TOK // 128):
                    pp = tc % 2
                    pg = psA if pp == 0 else psC
                    pu = psB if pp == 0 else psD
                    TEN.wait_ge(evd.sem, wready)
                    psAB_slots[pp].wait(TEN)
                    for hc in range(DC):
                        TEN.matmul(pg[:, :ITILE],
                                   xqT[:, hc * TOK + tc * 128:hc * TOK + tc * 128 + 128],
                                   gwT[par][:, hc * ITILE:(hc + 1) * ITILE],
                                   start=(hc == 0), stop=(hc == DC - 1))
                    mmu = None
                    for hc in range(DC):
                        mmu = TEN.matmul(pu[:, :ITILE],
                                         xqT[:, hc * TOK + tc * 128:hc * TOK + tc * 128 + 128],
                                         upT[par][:, hc * ITILE:(hc + 1) * ITILE],
                                         start=(hc == 0), stop=(hc == DC - 1))
                    mmu.then_inc(mmd.sem, 1)
                    vm = mmd.take()
                    lastmm = vm
                    if c_pending[0] is not None:
                        c_pending[0]()
                        c_pending[0] = None
                    SCA.wait_ge(mmd.sem, vm)
                    silgate_slots[pp].wait(SCA)
                    SCA.activation(silgate[pp][:], pg[:, :ITILE],
                                   AF.Silu).then_inc(fsd.sem, 1)
                    vf = fsd.take()
                    VEC.wait_ge(fsd.sem, vf)
                    h2chain_slots[pp].wait(VEC)
                    VEC.tensor_tensor(h2b[pp][:], silgate[pp][:], pu[:, :ITILE],
                                      ALU.mult).then_inc(qvd.sem, 1)
                    qvd.take()
                    psAB_slots[pp].freed_by(qvd)
                    silgate_slots[pp].freed_by(qvd)
                    if DEBUG_OUTPUTS:
                        VEC.engine_nop().then_inc(qvd.sem, 1)
                        _vq = qvd.take()
                        SYN.wait_ge(qvd.sem, _vq)
                        SYN.dma_start(out=dbg_h2b[it, tc],
                                      in_=h2b[pp][:]).then_inc(std.sem, 16)
                        std.take(16)
                    vdq = quant(h2b[pp][:], ITILE, h2bm[pp], h2sc[pp], h2rc[pp],
                                h2v[pp], h2q8[pp], h2k[pp], h2deq[pp], h2s2[pp])

                    def emit_tr(_it=it, _tc=tc, _pp=pp, _vdq=vdq):
                        def srch(j, _q=h2deq[_pp]):
                            return _q[:, j * 128:(j + 1) * 128]

                        sp = h2qTs[_tc % 2]
                        h2qTs_slots[_tc % 2].wait(TEN)

                        def dsth(gg, _sp=sp):
                            return _sp[:].rearrange("p (a t) -> p a t",
                                                    t=128)[:, 4 * gg:4 * gg + 4, :]

                        ev = transpose_blocks(srch, ITILE // 128, dsth, gpd, _vdq)
                        h2chain_slots[_pp].freed_by(evd, ev)
                        SYN.wait_ge(evd.sem, ev)
                        SYN.dma_start(
                            out=sc_h2qT[_tc, :, _it * (ITILE // 128) * 128:
                                        (_it + 1) * (ITILE // 128) * 128],
                            in_=sp[:],
                        ).then_inc(std.sem, 16)
                        std.take(16)
                        h2qTs_slots[_tc % 2].freed_by(std)

                    c_pending[0] = emit_tr
                    if prep:
                        if tc % 2 == 0:
                            j = tc // 2
                            vgs_[j] = do_feed(it + 1, j, vws[j])
                            if j + 2 < 8:
                                vws[j + 2] = issue_wload(it + 1, j + 2)
                        elif tc >= 3:
                            j = (tc - 3) // 2
                            do_transp(it + 1, j, vgs_[j])
                gwT_slots[par].freed_by(mmd, lastmm)
                upT_slots[par].freed_by(mmd, lastmm)
                if prep:
                    q_flush(qp)
                    batch_ready = do_transp(it + 1, 7, vgs_[7])
                maybe_barrier()
            if c_pending[0] is not None:
                c_pending[0]()
                c_pending[0] = None
            barrier()

        with ExitStack() as ph:
            QW = 1024
            NQ = II // QW
            NCH = (HT // 128) * NQ
            dwT = [sb(ph, f"dwT{j}", 128, ICH * HT, BF16) for j in range(2)]
            h2qsb = [sb(ph, f"h2qsb{j}", 128, ICH * 128, BF16) for j in range(2)]
            wstage = [sb(ph, f"w4stage{j}", 128, QW, F32) for j in range(3)]
            qpd = make_qpipe(ph, "qd", QW, dup=2)
            _glb = sb(ph, "gl_b", 128, HT, F32)
            gl_b = [_glb, _glb]
            t_sl = sb(ph, "t_sl", 128, HT, F32)
            _ssl = sb(ph, "s_sl", 128, HT, F32)
            s_sl = [_ssl, _ssl]
            _xsl = sb(ph, "x_sl", 128, HT, F32)
            x_sl = [_xsl, _xsl]
            o_sl = [sb(ph, f"o_sl{j}", 128, HT, F32) for j in range(2)]

            wst_slots = [Slot() for _ in range(3)]
            dwT_slots = [Slot(), Slot()]
            h2q_slots = [Slot(), Slot()]
            _sx = Slot()
            sx_slots = [_sx, _sx]
            o_slots = [Slot(), Slot()]
            _gl = Slot()
            gl_slots = [_gl, _gl]
            psA_slots = [Slot(), Slot()]

            def d_load(p, c):
                wst_slots[c % 3].wait(SYN)
                sub, qq = divmod(c, NQ)
                r0 = p * HT + sub * 128
                SYN.dma_start(out=wstage[c % 3][:],
                              in_=down_w[r0:r0 + 128, qq * QW:(qq + 1) * QW]
                              ).then_inc(ldw.sem, 16)
                return ldw.take(16)

            def d_feed(c, vw):
                vg, vq = q_feed(qpd, wstage[c % 3][:], QW, ldw, vw)
                wst_slots[c % 3].freed_by(qvd2, vq)
                return vg

            def d_transp(p, c, vg):
                par2 = p % 2
                if c == 0:
                    dwT_slots[par2].wait(TEN)
                sub, qq = divmod(c, NQ)

                def srcd(j, _q=qpd["dq"]):
                    return _q[:, j * 128:(j + 1) * 128]

                def dstd(gg, _s=sub, _q=qq, _p=par2):
                    return dwT[_p][:].rearrange("p (ic tt) -> p ic tt",
                                                tt=HT)[
                        :, _q * (QW // 128) + 4 * gg:
                        _q * (QW // 128) + 4 * gg + 4,
                        _s * 128:(_s + 1) * 128]

                ev = transpose_blocks(srcd, QW // 128, dstd, gpd2, vg)
                qpd["ev_val"] = ev
                return ev

            vws = {c: d_load(0, c) for c in range(3)}
            vgs_ = {}
            for c in range(NCH):
                vgs_[c] = d_feed(c, vws[c])
                if c + 3 < NCH:
                    vws[c + 3] = d_load(0, c + 3)
                if c > 0:
                    d_transp(0, c - 1, vgs_[c - 1])
            q_flush(qpd)
            dw_ready = d_transp(0, NCH - 1, vgs_[NCH - 1])

            for ht in range(D // HT):
                par = ht % 2
                prep = ht + 1 < D // HT
                if prep:
                    vws = {c: d_load(ht + 1, c) for c in range(3)}
                    vgs_ = {}
                wready4 = dw_ready
                lastmm4 = None
                for tc in range(TOK // 128):
                    hp = tc % 2
                    h2q_slots[hp].wait(SYN)
                    SYN.dma_start(out=h2qsb[hp][:],
                                  in_=sc_h2qT[tc]).then_inc(ldx.sem, 16)
                    vh = ldx.take(16)
                    sx_slots[hp].wait(SYN)
                    SYN.dma_start(out=s_sl[hp][:],
                                  in_=sc_s[tc * 128:(tc + 1) * 128,
                                           ht * HT:(ht + 1) * HT]
                                  ).then_inc(ldx.sem, 16)
                    SYN.dma_start(out=x_sl[hp][:],
                                  in_=x_own[tc * 128:(tc + 1) * 128,
                                            ht * HT:(ht + 1) * HT]
                                  ).then_inc(ldx.sem, 16)
                    vx_ = ldx.take(32)
                    pd = psA if hp == 0 else psB
                    TEN.wait_ge(evd.sem, wready4)
                    TEN.wait_ge(ldx.sem, vh)
                    psA_slots[hp].wait(TEN)
                    mm4 = None
                    for ic in range(ICH):
                        mm4 = TEN.matmul(pd[:, :HT],
                                         h2qsb[hp][:, ic * 128:(ic + 1) * 128],
                                         dwT[par][:, ic * HT:(ic + 1) * HT],
                                         start=(ic == 0), stop=(ic == ICH - 1))
                    mm4.then_inc(mmd.sem, 1)
                    vm4 = mmd.take()
                    lastmm4 = vm4
                    h2q_slots[hp].freed_by(mmd, vm4)
                    SCA.wait_ge(mmd.sem, vm4)
                    gl_slots[hp].wait(SCA)
                    SCA.activation(gl_b[hp][:], pd[:, :HT],
                                   AF.Silu).then_inc(fsd.sem, 1)
                    vf4 = fsd.take()
                    psA_slots[hp].freed_by(fsd, vf4)
                    VEC.wait_ge(fsd.sem, vf4)
                    VEC.wait_ge(ldx.sem, vx_)
                    o_slots[hp].wait(VEC)
                    VEC.tensor_tensor(t_sl[:], s_sl[hp][:], gl_b[hp][:],
                                      ALU.mult).then_inc(qvd.sem, 1)
                    vtf_ = qvd.take()
                    VEC.wait_ge(qvd.sem, vtf_)
                    VEC.tensor_tensor(o_sl[hp][:], x_sl[hp][:], t_sl[:],
                                      ALU.add).then_inc(ivd.sem, 1)
                    vo = ivd.take()
                    sx_slots[hp].freed_by(ivd, vo)
                    gl_slots[hp].freed_by(ivd, vo)
                    SYN.wait_ge(ivd.sem, vo)
                    SYN.dma_start(out=out[tc * 128:(tc + 1) * 128,
                                          ht * HT:(ht + 1) * HT],
                                  in_=o_sl[hp][:]).then_inc(sto.sem, 16)
                    sto.take(16)
                    o_slots[hp].freed_by(sto)
                    if prep:
                        c0, c1 = 2 * tc, 2 * tc + 1
                        if c0 >= 2:
                            d_transp(ht + 1, c0 - 2, vgs_[c0 - 2])
                        vgs_[c0] = d_feed(c0, vws[c0])
                        if c0 + 3 < NCH:
                            vws[c0 + 3] = d_load(ht + 1, c0 + 3)
                        if c1 >= 2:
                            d_transp(ht + 1, c1 - 2, vgs_[c1 - 2])
                        vgs_[c1] = d_feed(c1, vws[c1])
                        if c1 + 3 < NCH:
                            vws[c1 + 3] = d_load(ht + 1, c1 + 3)
                dwT_slots[par].freed_by(mmd, lastmm4)
                if prep:
                    d_transp(ht + 1, NCH - 2, vgs_[NCH - 2])
                    q_flush(qpd)
                    dw_ready = d_transp(ht + 1, NCH - 1, vgs_[NCH - 1])
                maybe_barrier()
            SYN.wait_ge(sto.sem, sto.n)

    return nc


def _prepare(inputs):
    x = np.ascontiguousarray(
        np.asarray(inputs["L0_attn_hidden_states"], dtype=np.float32)
    ).reshape(NCORE * TOK, D)
    w_i = np.ascontiguousarray(np.asarray(inputs["L0_attn_w_i"], np.float32))
    w_f = np.ascontiguousarray(np.asarray(inputs["L0_attn_w_f"], np.float32))
    w_g = np.ascontiguousarray(np.asarray(inputs["L0_attn_w_g"], np.float32))
    w_oT = np.ascontiguousarray(np.asarray(inputs["L0_attn_w_o"], np.float32).T)
    gate_w = np.ascontiguousarray(np.asarray(inputs["L0_ffn_gate_w"], np.float32))
    up_w = np.ascontiguousarray(np.asarray(inputs["L0_ffn_up_w"], np.float32))
    down_w = np.ascontiguousarray(np.asarray(inputs["L0_ffn_down_w"], np.float32))

    def vec128(v):
        return np.ascontiguousarray(np.asarray(v, np.float32).reshape(EC, 128).T)

    vs = {
        "s_i": vec128(inputs["L0_attn_s_i"]),
        "s_f": vec128(inputs["L0_attn_s_f"]),
        "s_g": vec128(inputs["L0_attn_s_g"]),
        "s_o": vec128(inputs["L0_attn_s_o"]),
        "gnw": vec128(inputs["L0_attn_g_norm_w"]),
    }
    ident = np.eye(128, dtype=ml_dtypes.bfloat16)

    in_maps = []
    for c in range(NCORE):
        b = c // 2
        m = {
            "x_own": x[c * TOK:(c + 1) * TOK],
            "x_pre": x[b * 2 * TOK:b * 2 * TOK + TOK],
            "w_i": w_i, "w_f": w_f, "w_g": w_g, "w_oT": w_oT,
            "gate_w": gate_w, "up_w": up_w, "down_w": down_w,
            "maskv": np.full((128, 1), float(c % 2), np.float32),
            "ident": ident,
        }
        m.update(vs)
        in_maps.append(m)
    return in_maps


def kernel(**inputs):
    in_maps = _prepare(inputs)
    nc = build_program()
    res = run_bass_kernel_spmd(nc, in_maps, core_ids=list(range(NCORE)))
    outv = np.concatenate([r["out"] for r in res.results], axis=0)
    return outv.reshape(4, 4096, 2048)


if __name__ == "__main__":
    import time
    t0 = time.time()
    nc = build_program()
    print(f"built in {time.time()-t0:.1f}s, instructions: {nc.n_instructions()}")


# version 16
# speedup vs baseline: 1.0931x; 1.0228x over previous
import numpy as np
import ml_dtypes
from contextlib import ExitStack

import concourse.bass as bass
import concourse.mybir as mybir
from concourse.bass_utils import run_bass_kernel_spmd

F32 = mybir.dt.float32
BF16 = mybir.dt.bfloat16
I32 = mybir.dt.int32
FP8 = mybir.dt.float8e4
AF = mybir.ActivationFunctionType
ALU = mybir.AluOpType
AX = mybir.AxisListType

MAGIC = 12582912.0
D = 2048
E = 2048
II = 8192
TOK = 2048
NCORE = 8
TTA = 256
NT = TOK // TTA
DC = D // 128
EC = E // 128
ICH = II // 128
ITILE = 512
HT = 512

DEBUG_OUTPUTS = False
PARANOID = False


class Ch:

    def __init__(self, sem):
        self.sem = sem
        self.n = 0

    def take(self, k=1):
        self.n += k
        return self.n


class Slot:

    def __init__(self):
        self.conds = []

    def freed_by(self, ch, val=None):
        self.conds = [(ch, ch.n if val is None else val)]

    def also_freed_by(self, ch, val=None):
        self.conds.append((ch, ch.n if val is None else val))

    def wait(self, eng):
        for ch, val in self.conds:
            eng.wait_ge(ch.sem, val)
        self.conds = []


def build_program():
    nc = bass.Bass("TRN2", target_bir_lowering=False, debug=False,
                   detect_race_conditions=False)

    x_own = nc.dram_tensor("x_own", [TOK, D], F32, kind="ExternalInput").ap()
    x_pre = nc.dram_tensor("x_pre", [TOK, D], F32, kind="ExternalInput").ap()
    w_i = nc.dram_tensor("w_i", [D, E], F32, kind="ExternalInput").ap()
    w_f = nc.dram_tensor("w_f", [D, E], F32, kind="ExternalInput").ap()
    w_g = nc.dram_tensor("w_g", [D, E], F32, kind="ExternalInput").ap()
    w_oT = nc.dram_tensor("w_oT", [E, E], F32, kind="ExternalInput").ap()
    s_i = nc.dram_tensor("s_i", [128, EC], F32, kind="ExternalInput").ap()
    s_f = nc.dram_tensor("s_f", [128, EC], F32, kind="ExternalInput").ap()
    s_g = nc.dram_tensor("s_g", [128, EC], F32, kind="ExternalInput").ap()
    s_o = nc.dram_tensor("s_o", [128, EC], F32, kind="ExternalInput").ap()
    gnw = nc.dram_tensor("gnw", [128, EC], F32, kind="ExternalInput").ap()
    maskv = nc.dram_tensor("maskv", [128, 1], F32, kind="ExternalInput").ap()
    ident = nc.dram_tensor("ident", [128, 128], BF16, kind="ExternalInput").ap()
    gate_w = nc.dram_tensor("gate_w", [II, D], F32, kind="ExternalInput").ap()
    up_w = nc.dram_tensor("up_w", [II, D], F32, kind="ExternalInput").ap()
    down_w = nc.dram_tensor("down_w", [D, II], F32, kind="ExternalInput").ap()

    out = nc.dram_tensor("out", [TOK, D], F32, kind="ExternalOutput").ap()

    skind = "ExternalOutput" if DEBUG_OUTPUTS else "Internal"
    sc_xfixT = nc.dram_tensor("sc_xfixT", [NT, 128, DC * TTA], BF16,
                              kind=skind).ap()
    sc_h = nc.dram_tensor("sc_h", [NT, 128, EC * TTA], F32, kind=skind).ap()
    sc_s = nc.dram_tensor("sc_s", [TOK, E], F32, kind=skind).ap()
    sc_h2qT = nc.dram_tensor("sc_h2qT", [TOK // 128, 128, ICH * 128], BF16,
                             kind=skind).ap()
    if DEBUG_OUTPUTS:
        dbg_xqT = nc.dram_tensor("dbg_xqT", [128, DC * TOK], BF16,
                                 kind="ExternalOutput").ap()
        dbg_gwT = nc.dram_tensor("dbg_gwT", [II // ITILE, 128, DC * ITILE], BF16,
                                 kind="ExternalOutput").ap()
        dbg_upT = nc.dram_tensor("dbg_upT", [II // ITILE, 128, DC * ITILE], BF16,
                                 kind="ExternalOutput").ap()
        dbg_h2b = nc.dram_tensor("dbg_h2b", [II // ITILE, TOK // 128, 128, ITILE],
                                 F32, kind="ExternalOutput").ap()

    TEN, VEC, SCA, GPS, SYN = nc.tensor, nc.vector, nc.scalar, nc.gpsimd, nc.sync

    with ExitStack() as top:
        def sb(stack, name, p, f, dt):
            return stack.enter_context(nc.sbuf_tensor(name, [p, f], dt))

        def sem(name):
            return Ch(top.enter_context(nc.semaphore(name)))

        vsi = sb(top, "vsi", 128, EC, F32)
        vsf = sb(top, "vsf", 128, EC, F32)
        vsg = sb(top, "vsg", 128, EC, F32)
        vso = sb(top, "vso", 128, EC, F32)
        vgn = sb(top, "vgn", 128, EC, F32)
        vgs = sb(top, "vgs", 128, EC, F32)
        vmask = sb(top, "vmask", 128, 1, F32)
        ones_f = sb(top, "ones_f", 128, 1, F32)
        ident_t = sb(top, "ident_t", 128, 128, BF16)
        state = sb(top, "state", 128, EC, F32)
        init_own = sb(top, "init_own", 128, EC, F32)
        rstd_b = sb(top, "rstd_b", 1, 256, F32)
        ones128 = sb(top, "ones128", 128, 128, F32)
        rbc_src = sb(top, "rbc_src", 128, TTA, F32)

        psA = top.enter_context(nc.psum_tensor("psA", [128, 512], F32))
        psB = top.enter_context(nc.psum_tensor("psB", [128, 512], F32))
        psC = top.enter_context(nc.psum_tensor("psC", [128, 512], F32))
        psD = top.enter_context(nc.psum_tensor("psD", [128, 512], F32))
        psT0 = top.enter_context(nc.psum_tensor("psT0", [128, 512], BF16))
        psT1 = top.enter_context(nc.psum_tensor("psT1", [128, 512], BF16))
        psU = top.enter_context(nc.psum_tensor("psU", [128, 512], F32))
        psV = top.enter_context(nc.psum_tensor("psV", [128, 256], F32))

        ld = sem("ld")
        ldx = sem("ldx")
        ldw = sem("ldw")
        lnd = sem("lnd")
        lnv = sem("lnv")
        lns = sem("lns")
        tfd = sem("tfd")
        trd = sem("trd")
        evd = sem("evd")
        mmd = sem("mmd")
        ivd = sem("ivd")
        fsd = sem("fsd")
        scd = sem("scd")
        std = sem("std")
        sto = sem("sto")
        qvd = sem("qvd")
        qsd = sem("qsd")
        gpd = sem("gpd")
        qvd2 = sem("qvd2")
        qsd2 = sem("qsd2")
        gpd2 = sem("gpd2")
        hch = [sem(f"hch{i}") for i in range(4)]
        wcd = sem("wcd")
        stc = sem("stc")
        bar = [sem(f"bar{i}") for i in range(5)]

        ENGS = [SYN, TEN, VEC, SCA, GPS]

        def barrier():
            vals = []
            for i, e in enumerate(ENGS):
                vals.append(bar[i].take())
                e.sem_inc(bar[i].sem, 1)
            for i, e in enumerate(ENGS):
                for j in range(5):
                    if j != i:
                        e.wait_ge(bar[j].sem, vals[j])

        def maybe_barrier():
            if PARANOID:
                barrier()

        def load_cast_weight(wsrc, dst, nchunks, cols, wstage, wstage_slots):
            for k in range(nchunks):
                sl = wstage_slots[k % 2]
                sl.wait(SYN)
                SYN.dma_start(out=wstage[k % 2][:, :cols],
                              in_=wsrc[k * 128:(k + 1) * 128, :cols]
                              ).then_inc(ldw.sem, 16)
                v = ldw.take(16)
                SCA.wait_ge(ldw.sem, v)
                SCA.activation(dst[:, k * cols:(k + 1) * cols],
                               wstage[k % 2][:, :cols], AF.Copy
                               ).then_inc(wcd.sem, 1)
                wcd.take()
                sl.freed_by(wcd)

        def tofix(src_ap, kint_ap, out_ap):
            VEC.tensor_scalar(kint_ap, src_ap, 256.0, MAGIC, ALU.mult, ALU.add)
            return VEC.tensor_scalar(out_ap, kint_ap, 1.0 / 256.0, -MAGIC / 256.0,
                                     ALU.mult, ALU.add)

        C448 = float(np.float32(1.0 / 448.0))
        C448Q = C448 * 2.0 ** -9

        def quant(src_ap, W, bm, sc, rc, v, q8, k8, deq, sc2):
            nb = W // 32
            s3 = src_ap.rearrange("p (b c) -> p b c", c=32)
            VEC.tensor_reduce(bm[:, :nb], s3, AX.X, ALU.max,
                              apply_absolute_value=True).then_inc(qvd.sem, 1)
            v1 = qvd.take()
            SCA.wait_ge(qvd.sem, v1)
            SCA.activation(sc[:, :nb], bm[:, :nb], AF.Copy,
                           scale=C448, bias=1e-38)
            SCA.activation(sc2[:, :nb], bm[:, :nb], AF.Copy,
                           scale=C448Q, bias=1e-38 * 2.0 ** -9).then_inc(qsd.sem, 1)
            s1 = qsd.take()
            VEC.wait_ge(qsd.sem, s1)
            VEC.reciprocal(rc[:, :nb], sc[:, :nb]).then_inc(qvd.sem, 1)
            v2 = qvd.take()
            VEC.wait_ge(qvd.sem, v2)
            rc3 = rc[:, :nb].unsqueeze(2).broadcast_to([128, nb, 32])
            VEC.scalar_tensor_tensor(
                v[:, :W].rearrange("p (b c) -> p b c", c=32), s3,
                0.5, rc3, ALU.mult, ALU.mult).then_inc(qvd.sem, 1)
            vq = qvd.take()
            SCA.wait_ge(qvd.sem, vq)
            SCA.wait_ge(gpd.sem, gpd.n)
            SCA.activation(q8[:, :W], v[:, :W], AF.Copy)
            SCA.activation(k8[:, :W], q8[:, :W], AF.Copy,
                           scale=1024.0).then_inc(qsd.sem, 1)
            vs = qsd.take()
            sc3 = sc2[:, :nb].unsqueeze(2).broadcast_to([128, nb, 32])
            VEC.wait_ge(qsd.sem, vs)
            VEC.tensor_tensor(
                deq[:, :W].rearrange("p (b c) -> p b c", c=32),
                k8[:, :W].rearrange("p (b c) -> p b c", c=32),
                sc3, ALU.mult).then_inc(gpd.sem, 1)
            return gpd.take()

        def quant2(src_ap, W, bm, sc, rc, v, q8, k8, deq, sc2):
            nb = W // 32
            s3 = src_ap.rearrange("p (b c) -> p b c", c=32)
            VEC.tensor_reduce(bm[:, :nb], s3, AX.X, ALU.max,
                              apply_absolute_value=True).then_inc(qvd.sem, 1)
            VEC.wait_ge(qvd.sem, qvd.take())
            VEC.tensor_scalar(sc[:, :nb], bm[:, :nb], C448, 1e-38,
                              ALU.mult, ALU.add)
            VEC.tensor_scalar(sc2[:, :nb], bm[:, :nb], C448Q,
                              1e-38 * 2.0 ** -9, ALU.mult,
                              ALU.add).then_inc(qvd.sem, 1)
            VEC.wait_ge(qvd.sem, qvd.take())
            VEC.reciprocal(rc[:, :nb], sc[:, :nb]).then_inc(qvd.sem, 1)
            VEC.wait_ge(qvd.sem, qvd.take())
            rc3 = rc[:, :nb].unsqueeze(2).broadcast_to([128, nb, 32])
            VEC.scalar_tensor_tensor(
                v[:, :W].rearrange("p (b c) -> p b c", c=32), s3,
                0.5, rc3, ALU.mult, ALU.mult).then_inc(qvd.sem, 1)
            vq = qvd.take()
            SCA.wait_ge(qvd.sem, vq)
            SCA.wait_ge(gpd.sem, gpd.n)
            SCA.activation(q8[:, :W], v[:, :W], AF.Copy)
            SCA.activation(k8[:, :W], q8[:, :W], AF.Copy,
                           scale=1024.0).then_inc(qsd.sem, 1)
            vs = qsd.take()
            sc3 = sc2[:, :nb].unsqueeze(2).broadcast_to([128, nb, 32])
            VEC.wait_ge(qsd.sem, vs)
            VEC.tensor_tensor(
                deq[:, :W].rearrange("p (b c) -> p b c", c=32),
                k8[:, :W].rearrange("p (b c) -> p b c", c=32),
                sc3, ALU.mult).then_inc(gpd.sem, 1)
            return gpd.take()

        def make_qpipe(ph, tag, W, dup=2):
            sets = []
            for s in range(dup):
                sets.append({
                    "bm": sb(ph, f"{tag}bm{s}", 128, W // 32, F32),
                    "sc": sb(ph, f"{tag}sc{s}", 128, W // 32, F32),
                    "s2": sb(ph, f"{tag}s2{s}", 128, W // 32, F32),
                    "rc": sb(ph, f"{tag}rc{s}", 128, W // 32, F32),
                    "qv": sb(ph, f"{tag}qv{s}", 128, W, F32),
                    "qk": sb(ph, f"{tag}qk{s}", 128, W, I32),
                    "qs_val": 0,
                    "vg_val": 0,
                })
            return {"sets": sets, "q8": sb(ph, f"{tag}q8", 128, W, FP8),
                    "dq": sb(ph, f"{tag}dq", 128, W, BF16),
                    "ev_val": 0, "k": 0, "pending": None}

        def q_feed(qp, src_ap, W, dma_ch=None, dma_val=None):
            st = qp["sets"][qp["k"] % len(qp["sets"])]
            qp["k"] += 1
            nb = W // 32
            s3 = src_ap.rearrange("p (b c) -> p b c", c=32)
            if dma_val is not None:
                VEC.wait_ge(dma_ch.sem, dma_val)
            VEC.tensor_reduce(st["bm"][:, :nb], s3, AX.X, ALU.max,
                              apply_absolute_value=True).then_inc(qvd2.sem, 1)
            VEC.wait_ge(qvd2.sem, qvd2.take())
            VEC.tensor_scalar(st["sc"][:, :nb], st["bm"][:, :nb], C448,
                              1e-38, ALU.mult, ALU.add)
            VEC.tensor_scalar(st["s2"][:, :nb], st["bm"][:, :nb], C448Q,
                              1e-38 * 2.0 ** -9, ALU.mult,
                              ALU.add).then_inc(qvd2.sem, 1)
            VEC.wait_ge(qvd2.sem, qvd2.take())
            VEC.reciprocal(st["rc"][:, :nb],
                           st["sc"][:, :nb]).then_inc(qvd2.sem, 1)
            VEC.wait_ge(qvd2.sem, qvd2.take())
            if st["qs_val"]:
                VEC.wait_ge(qsd2.sem, st["qs_val"])
            rc3 = st["rc"][:, :nb].unsqueeze(2).broadcast_to([128, nb, 32])
            VEC.scalar_tensor_tensor(
                st["qv"][:, :W].rearrange("p (b c) -> p b c", c=32), s3,
                0.5, rc3, ALU.mult, ALU.mult).then_inc(qvd2.sem, 1)
            vq = qvd2.take()
            SCA.wait_ge(qvd2.sem, vq)
            if st["vg_val"]:
                SCA.wait_ge(gpd2.sem, st["vg_val"])
            SCA.activation(qp["q8"][:, :W], st["qv"][:, :W], AF.Copy)
            SCA.activation(st["qk"][:, :W], qp["q8"][:, :W], AF.Copy,
                           scale=1024.0).then_inc(qsd2.sem, 1)
            vs = qsd2.take()
            st["qs_val"] = vs
            vg = gpd2.take()
            st["vg_val"] = vg
            if qp["pending"] is not None:
                qp["pending"]()

            def _deq(_st=st, _W=W, _nb=nb, _vs=vs):
                VEC.wait_ge(qsd2.sem, _vs)
                if qp["ev_val"]:
                    VEC.wait_ge(evd.sem, qp["ev_val"])
                sc3 = _st["s2"][:, :_nb].unsqueeze(2).broadcast_to(
                    [128, _nb, 32])
                VEC.tensor_tensor(
                    qp["dq"][:, :_W].rearrange("p (b c) -> p b c", c=32),
                    _st["qk"][:, :_W].rearrange("p (b c) -> p b c", c=32),
                    sc3, ALU.mult).then_inc(gpd2.sem, 1)

            qp["pending"] = _deq
            return vg, vq

        def q_flush(qp):
            if qp["pending"] is not None:
                qp["pending"]()
                qp["pending"] = None

        lnst = sb(top, "lnst", 128, 16, F32)
        ln_free = [Slot(), Slot()]

        def emit_ln(x_ap, xn_ap, par, x_ready_val, lntrash):
            c0 = par * 8
            st = lnst
            SCA.wait_ge(ldx.sem, x_ready_val)
            ln_free[par].wait(SCA)
            SCA.activation(lntrash[:], x_ap, AF.Square,
                           accum_out=st[:, c0:c0 + 1])
            SCA.activation(lntrash[:], x_ap, AF.Copy,
                           accum_out=st[:, c0 + 1:c0 + 2])
            SCA.activation(st[:, c0 + 2:c0 + 3], st[:, c0 + 1:c0 + 2],
                           AF.Copy, scale=1.0 / 2048.0)
            SCA.activation(st[:, c0 + 4:c0 + 5], st[:, c0 + 1:c0 + 2],
                           AF.Square, scale=1.0 / 2048.0)
            SCA.activation(st[:, c0 + 3:c0 + 4], st[:, c0:c0 + 1],
                           AF.Copy, scale=1.0 / 2048.0,
                           bias=1e-5).then_inc(lnd.sem, 1)
            vl = lnd.take()
            VEC.wait_ge(lnd.sem, vl)
            VEC.tensor_tensor(st[:, c0 + 5:c0 + 6], st[:, c0 + 3:c0 + 4],
                              st[:, c0 + 4:c0 + 5],
                              ALU.subtract).then_inc(lnv.sem, 1)
            vv = lnv.take()
            ln_free[par].freed_by(lnv, vv)
            SCA.wait_ge(lnv.sem, vv)
            SCA.activation(st[:, c0 + 6:c0 + 7], st[:, c0 + 5:c0 + 6],
                           AF.Sqrt).then_inc(lns.sem, 1)
            vq = lns.take()
            VEC.wait_ge(lns.sem, vq)
            VEC.reciprocal(st[:, c0 + 7:c0 + 8],
                           st[:, c0 + 6:c0 + 7]).then_inc(lnv.sem, 1)
            vv2 = lnv.take()
            VEC.wait_ge(lnv.sem, vv2)
            VEC.tensor_scalar(xn_ap, x_ap, st[:, c0 + 2:c0 + 3],
                              st[:, c0 + 7:c0 + 8], ALU.subtract, ALU.mult)

        tr_state = {"last_evac": [0, 0]}

        def transpose_blocks(src_fn, nblk, dst_fn, ready_ch, ready_val,
                             extra_wait=None):
            ngr = (nblk + 3) // 4
            for g in range(ngr):
                bank = (tr_state.setdefault("g", 0)) % 2
                tr_state["g"] = tr_state.get("g", 0) + 1
                ps = psT0 if bank == 0 else psT1
                TEN.wait_ge(ready_ch.sem, ready_val)
                if extra_wait is not None:
                    extra_wait.wait(TEN)
                    extra_wait = None
                if tr_state["last_evac"][bank]:
                    TEN.wait_ge(evd.sem, tr_state["last_evac"][bank])
                n_in_g = min(4 * g + 4, nblk) - 4 * g
                last = None
                for q in range(n_in_g):
                    j = 4 * g + q
                    last = TEN.transpose(ps[:, q * 128:(q + 1) * 128],
                                         src_fn(j), ident_t[:])
                last.then_inc(trd.sem, 1)
                tv = trd.take()
                SCA.wait_ge(trd.sem, tv)
                SCA.activation(dst_fn(g),
                               ps[:, :n_in_g * 128].rearrange(
                                   "p (a t) -> p a t", t=128),
                               AF.Copy).then_inc(evd.sem, 1)
                tr_state["last_evac"][bank] = evd.take()
            return evd.n

        for src, dst in ((s_i, vsi), (s_f, vsf), (s_g, vsg), (s_o, vso),
                         (gnw, vgn), (maskv, vmask), (ident, ident_t)):
            SYN.dma_start(out=dst[:], in_=src[:]).then_inc(ld.sem, 16)
            ld.take(16)
        VEC.wait_ge(ld.sem, ld.n)
        VEC.memset(ones_f[:], 1.0)
        VEC.memset(ones128[:], 1.0)
        VEC.memset(rbc_src[:], 0.0)
        VEC.scalar_tensor_tensor(vgs[:], vgn[:], -1.0, vso[:], ALU.mult, ALU.mult)
        VEC.memset(state[:], 0.0)

        with ExitStack() as ph:
            wib = sb(ph, "wib", 128, DC * E, BF16)
            wfb = sb(ph, "wfb", 128, DC * E, BF16)
            xstage = [sb(ph, f"xstage{j}", 128, D, F32) for j in range(2)]
            wstage = xstage
            xn_b = [sb(ph, f"xn{j}", 128, D, F32) for j in range(2)]
            kint = sb(ph, "kint", 128, D, I32)
            xfix_b = sb(ph, "xfix_b", 128, D, BF16)
            xfixT = [sb(ph, f"xfixT{j}", 128, DC * TTA, BF16) for j in range(2)]
            fb = [sb(ph, f"fb{j}", 128, TTA, F32) for j in range(2)]
            lnt = sb(ph, "lnt", 128, D, F32)
            i1b = sb(ph, "i1b", 128, TTA, F32)
            iqn = sb(ph, "iqn", 128, TTA, F32)
            hn = [sb(ph, f"hn{j}", 128, TTA, F32) for j in range(2)]

            wstage_slots = [Slot(), Slot()]
            xstage_slots = [Slot(), Slot()]
            xn_slots = [Slot(), Slot()]
            xfix_slot = Slot()
            xfixT_slots = [Slot(), Slot()]
            hn_slots = [Slot(), Slot()]
            fb_slots = [Slot(), Slot()]
            stc_vals = [0] * EC

            load_cast_weight(w_i, wib, DC, E, wstage, wstage_slots)
            load_cast_weight(w_f, wfb, DC, E, wstage, wstage_slots)
            w_ready = wcd.n
            for sl in xstage_slots:
                sl.freed_by(wcd, w_ready)

            def attn_if_scan(x_src, own):
                for t in range(NT):
                    tp = t % 2
                    xfixT_slots[tp].wait(TEN)
                    for c in range(2):
                        cp = c
                        row0 = t * TTA + c * 128
                        xstage_slots[cp].wait(SYN)
                        SYN.dma_start(out=xstage[cp][:],
                                      in_=x_src[row0:row0 + 128, :]
                                      ).then_inc(ldx.sem, 16)
                        vx = ldx.take(16)
                        xn_slots[cp].wait(VEC)
                        emit_ln(xstage[cp][:], xn_b[cp][:], cp, vx, lnt[:])
                        xfix_slot.wait(VEC)
                        tofix(xn_b[cp][:], kint[:], xfix_b[:]).then_inc(tfd.sem, 1)
                        vtf = tfd.take()
                        xstage_slots[cp].freed_by(tfd)
                        xn_slots[cp].freed_by(tfd)

                        def srcf(j, _xf=xfix_b):
                            return _xf[:, j * 128:(j + 1) * 128]

                        def dstf(g, _xT=xfixT[tp], _c=c):
                            return _xT[:].rearrange("p (dc tt) -> p dc tt",
                                                    tt=TTA)[:, 4 * g:4 * g + 4,
                                                            _c * 128:(_c + 1) * 128]

                        transpose_blocks(srcf, DC, dstf, tfd, vtf)
                        xfix_slot.freed_by(trd)
                    ev_tile = evd.n
                    last_mm_val = None
                    pi_free = [Slot(), Slot()]
                    pf_free = [Slot(), Slot()]
                    for ec in range(EC):
                        pi = psA if ec % 2 == 0 else psC
                        pf = psB if ec % 2 == 0 else psD
                        TEN.wait_ge(evd.sem, ev_tile)
                        if ec == 0:
                            TEN.wait_ge(wcd.sem, w_ready)
                        pi_free[ec % 2].wait(TEN)
                        pf_free[ec % 2].wait(TEN)
                        for dc in range(DC):
                            TEN.matmul(pi[:, :TTA],
                                       wib[:, dc * E + ec * 128:dc * E + ec * 128 + 128],
                                       xfixT[tp][:, dc * TTA:(dc + 1) * TTA],
                                       start=(dc == 0), stop=(dc == DC - 1))
                        mm = None
                        for dc in range(DC):
                            mm = TEN.matmul(pf[:, :TTA],
                                            wfb[:, dc * E + ec * 128:dc * E + ec * 128 + 128],
                                            xfixT[tp][:, dc * TTA:(dc + 1) * TTA],
                                            start=(dc == 0), stop=(dc == DC - 1))
                        mm.then_inc(mmd.sem, 1)
                        vm = mmd.take()
                        last_mm_val = vm
                        SCA.wait_ge(mmd.sem, vm)
                        fb_slots[ec % 2].wait(SCA)
                        SCA.activation(fb[ec % 2][:], pf[:, :TTA], AF.Sigmoid,
                                       scale=vsf[:, ec:ec + 1]).then_inc(fsd.sem, 1)
                        vf = fsd.take()
                        pf_free[ec % 2].freed_by(fsd)
                        VEC.wait_ge(mmd.sem, vm)
                        VEC.tensor_scalar(i1b[:], pi[:, :TTA], vsi[:, ec:ec + 1],
                                          None, ALU.mult).then_inc(ivd.sem, 1)
                        ivd.take()
                        pi_free[ec % 2].freed_by(ivd)
                        VEC.wait_ge(fsd.sem, vf)
                        VEC.scalar_tensor_tensor(iqn[:], fb[ec % 2][:], 1.0, i1b[:],
                                                 ALU.subtract, ALU.mult)
                        hb = hn[ec % 2]
                        hn_slots[ec % 2].wait(VEC)
                        if own and t == 0:
                            init_ap = init_own[:, ec:ec + 1]
                        else:
                            init_ap = state[:, ec:ec + 1]
                        VEC.wait_ge(stc.sem, stc_vals[ec])
                        VEC.tensor_tensor_scan(hb[:], fb[ec % 2][:], iqn[:],
                                               init_ap, ALU.mult,
                                               ALU.add).then_inc(scd.sem, 1)
                        vsc = scd.take()
                        SCA.wait_ge(scd.sem, vsc)
                        SCA.activation(state[:, ec:ec + 1], hb[:, TTA - 1:TTA],
                                       AF.Copy).then_inc(stc.sem, 1)
                        stc_vals[ec] = stc.take()
                        fb_slots[ec % 2].freed_by(scd, vsc)
                        if own:
                            SYN.wait_ge(scd.sem, vsc)
                            SYN.dma_start(
                                out=sc_h[t, :, ec * TTA:(ec + 1) * TTA],
                                in_=hb[:]).then_inc(std.sem, 16)
                            std.take(16)
                            hn_slots[ec % 2].freed_by(std)
                            hn_slots[ec % 2].also_freed_by(stc, stc_vals[ec])
                        else:
                            hn_slots[ec % 2].freed_by(stc, stc_vals[ec])
                    if own:
                        SYN.wait_ge(evd.sem, ev_tile)
                        SYN.dma_start(out=sc_xfixT[t],
                                      in_=xfixT[tp][:]).then_inc(std.sem, 16)
                        std.take(16)
                        xfixT_slots[tp].freed_by(std)
                        xfixT_slots[tp].also_freed_by(mmd, last_mm_val)
                    else:
                        xfixT_slots[tp].freed_by(mmd, last_mm_val)
                    maybe_barrier()

            attn_if_scan(x_pre, own=False)
            SCA.activation(init_own[:], state[:], AF.Copy,
                           scale=vmask[:, 0:1]).then_inc(stc.sem, 1)
            v0 = stc.take()
            for _e in range(EC):
                stc_vals[_e] = v0
            attn_if_scan(x_own, own=True)
            barrier()

        with ExitStack() as ph:
            wgb = sb(ph, "wgb", 128, DC * E, BF16)
            wob = sb(ph, "wob", 128, EC * E, BF16)
            xfT = [sb(ph, f"xfT{j}", 128, DC * TTA, BF16) for j in range(2)]
            hsb = sb(ph, "hsb", 128, EC * TTA, F32)
            silg = sb(ph, "silg", 128, EC * TTA, BF16)
            obuf = sb(ph, "obuf", 128, EC * TTA, BF16)
            hhb = [sb(ph, f"hhb{j}", 128, TTA, F32) for j in range(2)]
            t1b = sb(ph, "t1b", 128, TTA, F32)
            sstage = [sb(ph, f"sstage{j}", 128, E, F32) for j in range(2)]
            skint = sb(ph, "skint", 128, 512, I32)
            mstmp = sb(ph, "mstmp", 1, TTA, F32)
            wstage = sstage

            wstage_slots = [Slot(), Slot()]
            psV_slot = Slot()
            silg_slot = Slot()
            obuf_slot = Slot()
            rstd_slot = Slot()
            xfT_slots = [Slot(), Slot()]
            hsb_slot = Slot()
            sstage_slots = [Slot(), Slot()]
            hh_slots = [Slot(), Slot()]

            load_cast_weight(w_g, wgb, DC, E, wstage, wstage_slots)
            load_cast_weight(w_oT, wob, EC, E, wstage, wstage_slots)
            w2_ready = wcd.n
            for sl in sstage_slots:
                sl.freed_by(wcd, w2_ready)

            for t in range(NT):
                tp = t % 2
                t0 = t * TTA
                xfT_slots[tp].wait(SYN)
                SYN.dma_start(out=xfT[tp][:],
                              in_=sc_xfixT[t]).then_inc(ldx.sem, 16)
                vxf = ldx.take(16)
                hsb_slot.wait(SYN)
                SYN.dma_start(out=hsb[:],
                              in_=sc_h[t]).then_inc(ldx.sem, 16)
                vh = ldx.take(16)
                pg_free = [Slot(), Slot()]
                mm2 = None
                for ec in range(EC):
                    pg = psA if ec % 2 == 0 else psC
                    TEN.wait_ge(ldx.sem, vxf)
                    if ec == 0:
                        TEN.wait_ge(wcd.sem, w2_ready)
                    pg_free[ec % 2].wait(TEN)
                    mm = None
                    for dc in range(DC):
                        mm = TEN.matmul(pg[:, :TTA],
                                        wgb[:, dc * E + ec * 128:dc * E + ec * 128 + 128],
                                        xfT[tp][:, dc * TTA:(dc + 1) * TTA],
                                        start=(dc == 0), stop=(dc == DC - 1))
                    mm.then_inc(mmd.sem, 1)
                    vm = mmd.take()
                    if ec == EC - 1:
                        xfT_slots[tp].freed_by(mmd, vm)
                    SCA.wait_ge(mmd.sem, vm)
                    if ec == 0:
                        silg_slot.wait(SCA)
                    SCA.activation(silg[:, ec * TTA:(ec + 1) * TTA], pg[:, :TTA],
                                   AF.Silu,
                                   scale=vsg[:, ec:ec + 1]).then_inc(fsd.sem, 1)
                    fsd.take()
                    pg_free[ec % 2].freed_by(fsd)
                    VEC.wait_ge(ldx.sem, vh)
                    hq = hhb[ec % 2]
                    hh_slots[ec % 2].wait(VEC)
                    VEC.tensor_tensor(hq[:], hsb[:, ec * TTA:(ec + 1) * TTA],
                                      hsb[:, ec * TTA:(ec + 1) * TTA],
                                      ALU.mult).then_inc(qvd.sem, 1)
                    vq = qvd.take()
                    TEN.wait_ge(qvd.sem, vq)
                    TEN.matmul(psU[0:1, :TTA], ones_f[:], hq[:],
                               start=(ec == 0),
                               stop=(ec == EC - 1)).then_inc(mmd.sem, 1)
                    vm2 = mmd.take()
                    hh_slots[ec % 2].freed_by(mmd, vm2)
                vms = vm2
                VEC.wait_ge(mmd.sem, vms)
                VEC.tensor_scalar(mstmp[:], psU[0:1, :TTA], 1.0 / 2048.0, 1e-5,
                                  ALU.mult, ALU.add).then_inc(qvd.sem, 1)
                vq = qvd.take()
                SCA.wait_ge(qvd.sem, vq)
                rstd_slot.wait(SCA)
                SCA.activation(rstd_b[0:1, :TTA], mstmp[:],
                               AF.Sqrt).then_inc(qsd.sem, 1)
                vs2 = qsd.take()
                VEC.wait_ge(qsd.sem, vs2)
                psV_slot.wait(VEC)
                VEC.reciprocal(rbc_src[0:1, :TTA],
                               rstd_b[0:1, :TTA]).then_inc(qvd.sem, 1)
                vq = qvd.take()
                rstd_slot.freed_by(qvd, vq)
                TEN.wait_ge(qvd.sem, vq)
                TEN.matmul(psV[:, :TTA], ones128[:], rbc_src[:, :TTA],
                           start=True, stop=True).then_inc(mmd.sem, 1)
                vbc = mmd.take()
                VEC.wait_ge(mmd.sem, vbc)
                VEC.wait_ge(fsd.sem, fsd.n)
                obuf_slot.wait(VEC)
                lastob = None
                for ec in range(EC):
                    VEC.scalar_tensor_tensor(t1b[:],
                                             hsb[:, ec * TTA:(ec + 1) * TTA],
                                             vgs[:, ec:ec + 1], psV[:, :TTA],
                                             ALU.mult, ALU.mult)
                    lastob = VEC.tensor_tensor(obuf[:, ec * TTA:(ec + 1) * TTA],
                                               t1b[:],
                                               silg[:, ec * TTA:(ec + 1) * TTA],
                                               ALU.mult)
                lastob.then_inc(scd.sem, 1)
                vob_ready = scd.take()
                hsb_slot.freed_by(scd, vob_ready)
                psV_slot.freed_by(scd, vob_ready)
                po_free = [Slot(), Slot()]
                for tc in range(TTA // 128):
                    sp = sstage[tc % 2]
                    sstage_slots[tc % 2].wait(VEC)
                    for oc in range(4):
                        po = psB if oc % 2 == 0 else psD
                        TEN.wait_ge(scd.sem, vob_ready)
                        po_free[oc % 2].wait(TEN)
                        mm3 = None
                        for ec in range(EC):
                            mm3 = TEN.matmul(
                                po[:, :512],
                                obuf[:, ec * TTA + tc * 128:ec * TTA + tc * 128 + 128],
                                wob[:, ec * E + oc * 512:ec * E + (oc + 1) * 512],
                                start=(ec == 0), stop=(ec == EC - 1))
                        mm3.then_inc(mmd.sem, 1)
                        vm3 = mmd.take()
                        VEC.wait_ge(mmd.sem, vm3)
                        VEC.tensor_scalar(skint[:], po[:, :512], 256.0, MAGIC,
                                          ALU.mult, ALU.add)
                        VEC.tensor_scalar(sp[:, oc * 512:(oc + 1) * 512],
                                          skint[:], 1.0 / 256.0, -MAGIC / 256.0,
                                          ALU.mult, ALU.add).then_inc(ivd.sem, 1)
                        ivd.take()
                        po_free[oc % 2].freed_by(ivd)
                        obuf_slot.freed_by(mmd, vm3)
                        silg_slot.freed_by(mmd, vm3)
                    VEC.engine_nop().then_inc(scd.sem, 1)
                    vsr = scd.take()
                    SYN.wait_ge(scd.sem, vsr)
                    SYN.dma_start(out=sc_s[t0 + tc * 128:t0 + tc * 128 + 128, :],
                                  in_=sp[:]).then_inc(std.sem, 16)
                    std.take(16)
                    sstage_slots[tc % 2].freed_by(std)
                maybe_barrier()
            barrier()

        with ExitStack() as ph:
            xqT = sb(ph, "xqT", 128, DC * TOK, BF16)
            xstage = [sb(ph, f"x0stage{j}", 128, D, F32) for j in range(2)]
            qp = make_qpipe(ph, "qp", D, dup=2)
            gwT = [sb(ph, f"gwT{j}", 128, DC * ITILE, BF16) for j in range(2)]
            upT = [sb(ph, f"upT{j}", 128, DC * ITILE, BF16) for j in range(2)]
            silgate = [sb(ph, f"silgate{j}", 128, ITILE, F32) for j in range(2)]
            h2b = [sb(ph, f"h2b{j}", 128, ITILE, F32) for j in range(2)]
            h2v = [sb(ph, f"h2v{j}", 128, ITILE, F32) for j in range(2)]
            h2q8s = sb(ph, "h2q8s", 128, ITILE, FP8)
            h2q8 = [h2q8s, h2q8s]
            h2k = [sb(ph, f"h2k{j}", 128, ITILE, I32) for j in range(2)]
            h2deq = [sb(ph, f"h2deq{j}", 128, ITILE, BF16) for j in range(2)]
            h2bm = [sb(ph, f"h2bm{j}", 128, ITILE // 32, F32) for j in range(2)]
            h2sc = [sb(ph, f"h2sc{j}", 128, ITILE // 32, F32) for j in range(2)]
            h2rc = [sb(ph, f"h2rc{j}", 128, ITILE // 32, F32) for j in range(2)]
            h2s2 = [sb(ph, f"h2s2{j}", 128, ITILE // 32, F32) for j in range(2)]
            _h2qTs = sb(ph, "h2qTs", 128, 512, BF16)
            h2qTs = [_h2qTs, _h2qTs]

            xst_slots = [Slot(), Slot()]
            _xns = Slot()
            xn_slots = [_xns, _xns]
            gwT_slots = [Slot(), Slot()]
            upT_slots = [Slot(), Slot()]
            _h2qTs_slot = Slot()
            h2qTs_slots = [_h2qTs_slot, _h2qTs_slot]
            psAB_slots = [Slot(), Slot()]
            silgate_slots = [Slot(), Slot()]
            h2chain_slots = [Slot(), Slot()]

            def b0_transp(g, vg):
                def srcq(j, _q=qp["dq"]):
                    return _q[:, j * 128:(j + 1) * 128]

                def dstq(gg, _g=g):
                    return xqT[:].rearrange("p (dc tt) -> p dc tt",
                                            tt=TOK)[:, 4 * gg:4 * gg + 4,
                                                    _g * 128:(_g + 1) * 128]

                qp["ev_val"] = transpose_blocks(srcq, DC, dstq, gpd2, vg)

            b0_vgs = {}
            for g in range(TOK // 128):
                cp = g % 2
                xst_slots[cp].wait(SYN)
                SYN.dma_start(out=xstage[cp][:],
                              in_=x_own[g * 128:(g + 1) * 128, :]
                              ).then_inc(ldx.sem, 16)
                vx = ldx.take(16)
                xnbuf = qp["sets"][1 - cp]["qv"]
                _qsv = qp["sets"][1 - cp]["qs_val"]
                if _qsv:
                    VEC.wait_ge(qsd2.sem, _qsv)
                emit_ln(xstage[cp][:], xnbuf[:], cp, vx,
                        qp["sets"][cp]["qv"][:])
                vg, vq = q_feed(qp, xnbuf[:], D)
                b0_vgs[g] = vg
                xst_slots[cp].freed_by(qvd2, vq)
                if g > 0:
                    b0_transp(g - 1, b0_vgs[g - 1])
            q_flush(qp)
            b0_transp(TOK // 128 - 1, b0_vgs[TOK // 128 - 1])
            if DEBUG_OUTPUTS:
                SYN.wait_ge(evd.sem, evd.n)
                SYN.dma_start(out=dbg_xqT[:], in_=xqT[:]).then_inc(std.sem, 16)
                std.take(16)
            maybe_barrier()

            NIT = II // ITILE

            def issue_wload(p, j):
                c = p * 8 + j
                xst_slots[c % 2].wait(SYN)
                w = gate_w if j < 4 else up_w
                r0 = p * ITILE + (j % 4) * 128
                SYN.dma_start(out=xstage[c % 2][:, :D],
                              in_=w[r0:r0 + 128, :]).then_inc(ldw.sem, 16)
                return ldw.take(16)

            def do_feed(p, j, vw):
                c = p * 8 + j
                vg, vq = q_feed(qp, xstage[c % 2][:, :D], D, ldw, vw)
                xst_slots[c % 2].freed_by(qvd2, vq)
                return vg

            def do_transp(p, j, vg):
                par2 = p % 2
                dstT = gwT if j < 4 else upT
                slots = gwT_slots if j < 4 else upT_slots
                if j == 0 or j == 4:
                    slots[par2].wait(TEN)
                sub = j % 4

                def srcw(jj, _q=qp["dq"]):
                    return _q[:, jj * 128:(jj + 1) * 128]

                def dstw(gg, _s=sub, _d=dstT, _p=par2):
                    return _d[_p][:].rearrange("p (dc tt) -> p dc tt",
                                               tt=ITILE)[
                        :, 4 * gg:4 * gg + 4, _s * 128:(_s + 1) * 128]

                ev = transpose_blocks(srcw, DC, dstw, gpd2, vg)
                qp["ev_val"] = ev
                return ev

            vws = {0: issue_wload(0, 0), 1: issue_wload(0, 1)}
            vgs_ = {}
            for j in range(8):
                vgs_[j] = do_feed(0, j, vws[j])
                if j + 2 < 8:
                    vws[j + 2] = issue_wload(0, j + 2)
                if j > 0:
                    do_transp(0, j - 1, vgs_[j - 1])
            q_flush(qp)
            batch_ready = do_transp(0, 7, vgs_[7])

            c_pending = [None]
            for it in range(NIT):
                par = it % 2
                prep = it + 1 < NIT
                if prep:
                    vws = {0: issue_wload(it + 1, 0),
                           1: issue_wload(it + 1, 1)}
                    vgs_ = {}
                wready = batch_ready
                if DEBUG_OUTPUTS:
                    SYN.wait_ge(evd.sem, wready)
                    SYN.dma_start(out=dbg_gwT[it],
                                  in_=gwT[par][:]).then_inc(std.sem, 16)
                    SYN.dma_start(out=dbg_upT[it],
                                  in_=upT[par][:]).then_inc(std.sem, 16)
                    std.take(32)
                lastmm = None
                for tc in range(TOK // 128):
                    pp = tc % 2
                    pg = psA if pp == 0 else psC
                    pu = psB if pp == 0 else psD
                    TEN.wait_ge(evd.sem, wready)
                    psAB_slots[pp].wait(TEN)
                    for hc in range(DC):
                        TEN.matmul(pg[:, :ITILE],
                                   xqT[:, hc * TOK + tc * 128:hc * TOK + tc * 128 + 128],
                                   gwT[par][:, hc * ITILE:(hc + 1) * ITILE],
                                   start=(hc == 0), stop=(hc == DC - 1))
                    mmu = None
                    for hc in range(DC):
                        mmu = TEN.matmul(pu[:, :ITILE],
                                         xqT[:, hc * TOK + tc * 128:hc * TOK + tc * 128 + 128],
                                         upT[par][:, hc * ITILE:(hc + 1) * ITILE],
                                         start=(hc == 0), stop=(hc == DC - 1))
                    mmu.then_inc(mmd.sem, 1)
                    vm = mmd.take()
                    lastmm = vm
                    if c_pending[0] is not None:
                        c_pending[0]()
                        c_pending[0] = None
                    SCA.wait_ge(mmd.sem, vm)
                    silgate_slots[pp].wait(SCA)
                    SCA.activation(silgate[pp][:], pg[:, :ITILE],
                                   AF.Silu).then_inc(fsd.sem, 1)
                    vf = fsd.take()
                    VEC.wait_ge(fsd.sem, vf)
                    h2chain_slots[pp].wait(VEC)
                    VEC.tensor_tensor(h2b[pp][:], silgate[pp][:], pu[:, :ITILE],
                                      ALU.mult).then_inc(qvd.sem, 1)
                    qvd.take()
                    psAB_slots[pp].freed_by(qvd)
                    silgate_slots[pp].freed_by(qvd)
                    if DEBUG_OUTPUTS:
                        VEC.engine_nop().then_inc(qvd.sem, 1)
                        _vq = qvd.take()
                        SYN.wait_ge(qvd.sem, _vq)
                        SYN.dma_start(out=dbg_h2b[it, tc],
                                      in_=h2b[pp][:]).then_inc(std.sem, 16)
                        std.take(16)
                    vdq = quant(h2b[pp][:], ITILE, h2bm[pp], h2sc[pp],
                                h2rc[pp], h2v[pp], h2q8[pp], h2k[pp],
                                h2deq[pp], h2s2[pp])

                    def emit_tr(_it=it, _tc=tc, _pp=pp, _vdq=vdq):
                        def srch(j, _q=h2deq[_pp]):
                            return _q[:, j * 128:(j + 1) * 128]

                        sp = h2qTs[_tc % 2]
                        h2qTs_slots[_tc % 2].wait(TEN)

                        def dsth(gg, _sp=sp):
                            return _sp[:].rearrange("p (a t) -> p a t",
                                                    t=128)[:, 4 * gg:4 * gg + 4, :]

                        ev = transpose_blocks(srch, ITILE // 128, dsth, gpd, _vdq)
                        h2chain_slots[_pp].freed_by(evd, ev)
                        SYN.wait_ge(evd.sem, ev)
                        SYN.dma_start(
                            out=sc_h2qT[_tc, :, _it * (ITILE // 128) * 128:
                                        (_it + 1) * (ITILE // 128) * 128],
                            in_=sp[:],
                        ).then_inc(std.sem, 16)
                        std.take(16)
                        h2qTs_slots[_tc % 2].freed_by(std)

                    c_pending[0] = emit_tr
                    if prep:
                        if tc % 2 == 0:
                            j = tc // 2
                            vgs_[j] = do_feed(it + 1, j, vws[j])
                            if j + 2 < 8:
                                vws[j + 2] = issue_wload(it + 1, j + 2)
                        elif tc >= 3:
                            j = (tc - 3) // 2
                            do_transp(it + 1, j, vgs_[j])
                gwT_slots[par].freed_by(mmd, lastmm)
                upT_slots[par].freed_by(mmd, lastmm)
                if prep:
                    q_flush(qp)
                    batch_ready = do_transp(it + 1, 7, vgs_[7])
                maybe_barrier()
            if c_pending[0] is not None:
                c_pending[0]()
                c_pending[0] = None
            barrier()

        with ExitStack() as ph:
            QW = 1024
            NQ = II // QW
            NCH = (HT // 128) * NQ
            dwT = [sb(ph, f"dwT{j}", 128, ICH * HT, BF16) for j in range(2)]
            h2qsb = [sb(ph, f"h2qsb{j}", 128, ICH * 128, BF16) for j in range(2)]
            wstage = [sb(ph, f"w4stage{j}", 128, QW, F32) for j in range(3)]
            qpd = make_qpipe(ph, "qd", QW, dup=2)
            _glb = sb(ph, "gl_b", 128, HT, F32)
            gl_b = [_glb, _glb]
            t_sl = sb(ph, "t_sl", 128, HT, F32)
            _ssl = sb(ph, "s_sl", 128, HT, F32)
            s_sl = [_ssl, _ssl]
            _xsl = sb(ph, "x_sl", 128, HT, F32)
            x_sl = [_xsl, _xsl]
            o_sl = [sb(ph, f"o_sl{j}", 128, HT, F32) for j in range(2)]

            wst_slots = [Slot() for _ in range(3)]
            dwT_slots = [Slot(), Slot()]
            h2q_slots = [Slot(), Slot()]
            _sx = Slot()
            sx_slots = [_sx, _sx]
            o_slots = [Slot(), Slot()]
            _gl = Slot()
            gl_slots = [_gl, _gl]
            psA_slots = [Slot(), Slot()]

            def d_load(p, c):
                wst_slots[c % 3].wait(SYN)
                sub, qq = divmod(c, NQ)
                r0 = p * HT + sub * 128
                SYN.dma_start(out=wstage[c % 3][:],
                              in_=down_w[r0:r0 + 128, qq * QW:(qq + 1) * QW]
                              ).then_inc(ldw.sem, 16)
                return ldw.take(16)

            def d_feed(c, vw):
                vg, vq = q_feed(qpd, wstage[c % 3][:], QW, ldw, vw)
                wst_slots[c % 3].freed_by(qvd2, vq)
                return vg

            def d_transp(p, c, vg):
                par2 = p % 2
                if c == 0:
                    dwT_slots[par2].wait(TEN)
                sub, qq = divmod(c, NQ)

                def srcd(j, _q=qpd["dq"]):
                    return _q[:, j * 128:(j + 1) * 128]

                def dstd(gg, _s=sub, _q=qq, _p=par2):
                    return dwT[_p][:].rearrange("p (ic tt) -> p ic tt",
                                                tt=HT)[
                        :, _q * (QW // 128) + 4 * gg:
                        _q * (QW // 128) + 4 * gg + 4,
                        _s * 128:(_s + 1) * 128]

                ev = transpose_blocks(srcd, QW // 128, dstd, gpd2, vg)
                qpd["ev_val"] = ev
                return ev

            vws = {c: d_load(0, c) for c in range(3)}
            vgs_ = {}
            for c in range(NCH):
                vgs_[c] = d_feed(c, vws[c])
                if c + 3 < NCH:
                    vws[c + 3] = d_load(0, c + 3)
                if c > 0:
                    d_transp(0, c - 1, vgs_[c - 1])
            q_flush(qpd)
            dw_ready = d_transp(0, NCH - 1, vgs_[NCH - 1])

            for ht in range(D // HT):
                par = ht % 2
                prep = ht + 1 < D // HT
                if prep:
                    vws = {c: d_load(ht + 1, c) for c in range(3)}
                    vgs_ = {}
                wready4 = dw_ready
                lastmm4 = None
                for tc in range(TOK // 128):
                    hp = tc % 2
                    h2q_slots[hp].wait(SYN)
                    vhs = []
                    for ck in range(4):
                        _ch = hch[ck]
                        SYN.dma_start(
                            out=h2qsb[hp][:, ck * 2048:(ck + 1) * 2048],
                            in_=sc_h2qT[tc, :, ck * 2048:(ck + 1) * 2048]
                        ).then_inc(_ch.sem, 16)
                        vhs.append((_ch, _ch.take(16)))
                    sx_slots[hp].wait(SYN)
                    SYN.dma_start(out=s_sl[hp][:],
                                  in_=sc_s[tc * 128:(tc + 1) * 128,
                                           ht * HT:(ht + 1) * HT]
                                  ).then_inc(ldx.sem, 16)
                    SYN.dma_start(out=x_sl[hp][:],
                                  in_=x_own[tc * 128:(tc + 1) * 128,
                                            ht * HT:(ht + 1) * HT]
                                  ).then_inc(ldx.sem, 16)
                    vx_ = ldx.take(32)
                    pd = psA if hp == 0 else psB
                    TEN.wait_ge(evd.sem, wready4)
                    psA_slots[hp].wait(TEN)
                    mm4 = None
                    for ic in range(ICH):
                        if ic % (ICH // 4) == 0:
                            _ch, _v = vhs[ic // (ICH // 4)]
                            TEN.wait_ge(_ch.sem, _v)
                        mm4 = TEN.matmul(pd[:, :HT],
                                         h2qsb[hp][:, ic * 128:(ic + 1) * 128],
                                         dwT[par][:, ic * HT:(ic + 1) * HT],
                                         start=(ic == 0), stop=(ic == ICH - 1))
                    mm4.then_inc(mmd.sem, 1)
                    vm4 = mmd.take()
                    lastmm4 = vm4
                    h2q_slots[hp].freed_by(mmd, vm4)
                    SCA.wait_ge(mmd.sem, vm4)
                    gl_slots[hp].wait(SCA)
                    SCA.activation(gl_b[hp][:], pd[:, :HT],
                                   AF.Silu).then_inc(fsd.sem, 1)
                    vf4 = fsd.take()
                    psA_slots[hp].freed_by(fsd, vf4)
                    VEC.wait_ge(fsd.sem, vf4)
                    VEC.wait_ge(ldx.sem, vx_)
                    o_slots[hp].wait(VEC)
                    VEC.tensor_tensor(t_sl[:], s_sl[hp][:], gl_b[hp][:],
                                      ALU.mult).then_inc(qvd.sem, 1)
                    vtf_ = qvd.take()
                    VEC.wait_ge(qvd.sem, vtf_)
                    VEC.tensor_tensor(o_sl[hp][:], x_sl[hp][:], t_sl[:],
                                      ALU.add).then_inc(ivd.sem, 1)
                    vo = ivd.take()
                    sx_slots[hp].freed_by(ivd, vo)
                    gl_slots[hp].freed_by(ivd, vo)
                    SYN.wait_ge(ivd.sem, vo)
                    SYN.dma_start(out=out[tc * 128:(tc + 1) * 128,
                                          ht * HT:(ht + 1) * HT],
                                  in_=o_sl[hp][:]).then_inc(sto.sem, 16)
                    sto.take(16)
                    o_slots[hp].freed_by(sto)
                    if prep:
                        c0, c1 = 2 * tc, 2 * tc + 1
                        if c0 >= 2:
                            d_transp(ht + 1, c0 - 2, vgs_[c0 - 2])
                        vgs_[c0] = d_feed(c0, vws[c0])
                        if c0 + 3 < NCH:
                            vws[c0 + 3] = d_load(ht + 1, c0 + 3)
                        if c1 >= 2:
                            d_transp(ht + 1, c1 - 2, vgs_[c1 - 2])
                        vgs_[c1] = d_feed(c1, vws[c1])
                        if c1 + 3 < NCH:
                            vws[c1 + 3] = d_load(ht + 1, c1 + 3)
                dwT_slots[par].freed_by(mmd, lastmm4)
                if prep:
                    d_transp(ht + 1, NCH - 2, vgs_[NCH - 2])
                    q_flush(qpd)
                    dw_ready = d_transp(ht + 1, NCH - 1, vgs_[NCH - 1])
                maybe_barrier()
            SYN.wait_ge(sto.sem, sto.n)

    return nc


def _prepare(inputs):
    x = np.ascontiguousarray(
        np.asarray(inputs["L0_attn_hidden_states"], dtype=np.float32)
    ).reshape(NCORE * TOK, D)
    w_i = np.ascontiguousarray(np.asarray(inputs["L0_attn_w_i"], np.float32))
    w_f = np.ascontiguousarray(np.asarray(inputs["L0_attn_w_f"], np.float32))
    w_g = np.ascontiguousarray(np.asarray(inputs["L0_attn_w_g"], np.float32))
    w_oT = np.ascontiguousarray(np.asarray(inputs["L0_attn_w_o"], np.float32).T)
    gate_w = np.ascontiguousarray(np.asarray(inputs["L0_ffn_gate_w"], np.float32))
    up_w = np.ascontiguousarray(np.asarray(inputs["L0_ffn_up_w"], np.float32))
    down_w = np.ascontiguousarray(np.asarray(inputs["L0_ffn_down_w"], np.float32))

    def vec128(v):
        return np.ascontiguousarray(np.asarray(v, np.float32).reshape(EC, 128).T)

    vs = {
        "s_i": vec128(inputs["L0_attn_s_i"]),
        "s_f": vec128(inputs["L0_attn_s_f"]),
        "s_g": vec128(inputs["L0_attn_s_g"]),
        "s_o": vec128(inputs["L0_attn_s_o"]),
        "gnw": vec128(inputs["L0_attn_g_norm_w"]),
    }
    ident = np.eye(128, dtype=ml_dtypes.bfloat16)

    in_maps = []
    for c in range(NCORE):
        b = c // 2
        m = {
            "x_own": x[c * TOK:(c + 1) * TOK],
            "x_pre": x[b * 2 * TOK:b * 2 * TOK + TOK],
            "w_i": w_i, "w_f": w_f, "w_g": w_g, "w_oT": w_oT,
            "gate_w": gate_w, "up_w": up_w, "down_w": down_w,
            "maskv": np.full((128, 1), float(c % 2), np.float32),
            "ident": ident,
        }
        m.update(vs)
        in_maps.append(m)
    return in_maps


def kernel(**inputs):
    in_maps = _prepare(inputs)
    nc = build_program()
    res = run_bass_kernel_spmd(nc, in_maps, core_ids=list(range(NCORE)))
    outv = np.concatenate([r["out"] for r in res.results], axis=0)
    return outv.reshape(4, 4096, 2048)


if __name__ == "__main__":
    import time
    t0 = time.time()
    nc = build_program()
    print(f"built in {time.time()-t0:.1f}s, instructions: {nc.n_instructions()}")


# version 17 (final)
# speedup vs baseline: 1.1347x; 1.0381x over previous
import numpy as np
import ml_dtypes
from contextlib import ExitStack

import concourse.bass as bass
import concourse.mybir as mybir
from concourse.bass_utils import run_bass_kernel_spmd

F32 = mybir.dt.float32
BF16 = mybir.dt.bfloat16
I32 = mybir.dt.int32
FP8 = mybir.dt.float8e4
AF = mybir.ActivationFunctionType
ALU = mybir.AluOpType
AX = mybir.AxisListType

MAGIC = 12582912.0
D = 2048
E = 2048
II = 8192
TOK = 2048
NCORE = 8
TTA = 256
NT = TOK // TTA
DC = D // 128
EC = E // 128
ICH = II // 128
ITILE = 512
HT = 512

DEBUG_OUTPUTS = False
PARANOID = False


class Ch:

    def __init__(self, sem):
        self.sem = sem
        self.n = 0

    def take(self, k=1):
        self.n += k
        return self.n


class Slot:

    def __init__(self):
        self.conds = []

    def freed_by(self, ch, val=None):
        self.conds = [(ch, ch.n if val is None else val)]

    def also_freed_by(self, ch, val=None):
        self.conds.append((ch, ch.n if val is None else val))

    def wait(self, eng):
        for ch, val in self.conds:
            eng.wait_ge(ch.sem, val)
        self.conds = []


def build_program():
    nc = bass.Bass("TRN2", target_bir_lowering=False, debug=False,
                   detect_race_conditions=False)

    x_own = nc.dram_tensor("x_own", [TOK, D], F32, kind="ExternalInput").ap()
    x_pre = nc.dram_tensor("x_pre", [TOK, D], F32, kind="ExternalInput").ap()
    w_i = nc.dram_tensor("w_i", [D, E], F32, kind="ExternalInput").ap()
    w_f = nc.dram_tensor("w_f", [D, E], F32, kind="ExternalInput").ap()
    w_g = nc.dram_tensor("w_g", [D, E], F32, kind="ExternalInput").ap()
    w_oT = nc.dram_tensor("w_oT", [E, E], F32, kind="ExternalInput").ap()
    s_i = nc.dram_tensor("s_i", [128, EC], F32, kind="ExternalInput").ap()
    s_f = nc.dram_tensor("s_f", [128, EC], F32, kind="ExternalInput").ap()
    s_g = nc.dram_tensor("s_g", [128, EC], F32, kind="ExternalInput").ap()
    s_o = nc.dram_tensor("s_o", [128, EC], F32, kind="ExternalInput").ap()
    gnw = nc.dram_tensor("gnw", [128, EC], F32, kind="ExternalInput").ap()
    maskv = nc.dram_tensor("maskv", [128, 1], F32, kind="ExternalInput").ap()
    ident = nc.dram_tensor("ident", [128, 128], BF16, kind="ExternalInput").ap()
    gate_w = nc.dram_tensor("gate_w", [II, D], F32, kind="ExternalInput").ap()
    up_w = nc.dram_tensor("up_w", [II, D], F32, kind="ExternalInput").ap()
    down_w = nc.dram_tensor("down_w", [D, II], F32, kind="ExternalInput").ap()

    out = nc.dram_tensor("out", [TOK, D], F32, kind="ExternalOutput").ap()

    skind = "ExternalOutput" if DEBUG_OUTPUTS else "Internal"
    sc_xfixT = nc.dram_tensor("sc_xfixT", [NT, 128, DC * TTA], BF16,
                              kind=skind).ap()
    sc_h = nc.dram_tensor("sc_h", [NT, 128, EC * TTA], F32, kind=skind).ap()
    sc_s = nc.dram_tensor("sc_s", [TOK, E], F32, kind=skind).ap()
    sc_h2qT = nc.dram_tensor("sc_h2qT", [TOK // 128, 128, ICH * 128], BF16,
                             kind=skind).ap()
    if DEBUG_OUTPUTS:
        dbg_xqT = nc.dram_tensor("dbg_xqT", [128, DC * TOK], BF16,
                                 kind="ExternalOutput").ap()
        dbg_gwT = nc.dram_tensor("dbg_gwT", [II // ITILE, 128, DC * ITILE], BF16,
                                 kind="ExternalOutput").ap()
        dbg_upT = nc.dram_tensor("dbg_upT", [II // ITILE, 128, DC * ITILE], BF16,
                                 kind="ExternalOutput").ap()
        dbg_h2b = nc.dram_tensor("dbg_h2b", [II // ITILE, TOK // 128, 128, ITILE],
                                 F32, kind="ExternalOutput").ap()

    TEN, VEC, SCA, GPS, SYN = nc.tensor, nc.vector, nc.scalar, nc.gpsimd, nc.sync

    with ExitStack() as top:
        def sb(stack, name, p, f, dt):
            return stack.enter_context(nc.sbuf_tensor(name, [p, f], dt))

        def sem(name):
            return Ch(top.enter_context(nc.semaphore(name)))

        vsi = sb(top, "vsi", 128, EC, F32)
        vsf = sb(top, "vsf", 128, EC, F32)
        vsg = sb(top, "vsg", 128, EC, F32)
        vso = sb(top, "vso", 128, EC, F32)
        vgn = sb(top, "vgn", 128, EC, F32)
        vgs = sb(top, "vgs", 128, EC, F32)
        vmask = sb(top, "vmask", 128, 1, F32)
        ones_f = sb(top, "ones_f", 128, 1, F32)
        ident_t = sb(top, "ident_t", 128, 128, BF16)
        state = sb(top, "state", 128, EC, F32)
        init_own = sb(top, "init_own", 128, EC, F32)
        rstd_b = sb(top, "rstd_b", 1, 256, F32)
        ones128 = sb(top, "ones128", 128, 128, F32)
        rbc_src = sb(top, "rbc_src", 128, TTA, F32)

        psA = top.enter_context(nc.psum_tensor("psA", [128, 512], F32))
        psB = top.enter_context(nc.psum_tensor("psB", [128, 512], F32))
        psC = top.enter_context(nc.psum_tensor("psC", [128, 512], F32))
        psD = top.enter_context(nc.psum_tensor("psD", [128, 512], F32))
        psT0 = top.enter_context(nc.psum_tensor("psT0", [128, 512], BF16))
        psT1 = top.enter_context(nc.psum_tensor("psT1", [128, 512], BF16))
        psU = top.enter_context(nc.psum_tensor("psU", [128, 512], F32))
        psV = top.enter_context(nc.psum_tensor("psV", [128, 256], F32))

        ld = sem("ld")
        ldx = sem("ldx")
        ldw = sem("ldw")
        lnd = sem("lnd")
        lnv = sem("lnv")
        lns = sem("lns")
        tfd = sem("tfd")
        trd = sem("trd")
        evd = sem("evd")
        mmd = sem("mmd")
        ivd = sem("ivd")
        fsd = sem("fsd")
        scd = sem("scd")
        std = sem("std")
        sto = sem("sto")
        qvd = sem("qvd")
        qsd = sem("qsd")
        gpd = sem("gpd")
        qvd2 = sem("qvd2")
        qsd2 = sem("qsd2")
        gpd2 = sem("gpd2")
        hch = [sem(f"hch{i}") for i in range(4)]
        wcd = sem("wcd")
        stc = sem("stc")
        bar = [sem(f"bar{i}") for i in range(5)]

        ENGS = [SYN, TEN, VEC, SCA, GPS]

        def barrier():
            vals = []
            for i, e in enumerate(ENGS):
                vals.append(bar[i].take())
                e.sem_inc(bar[i].sem, 1)
            for i, e in enumerate(ENGS):
                for j in range(5):
                    if j != i:
                        e.wait_ge(bar[j].sem, vals[j])

        def maybe_barrier():
            if PARANOID:
                barrier()

        def load_cast_weight(wsrc, dst, nchunks, cols, wstage, wstage_slots):
            for k in range(nchunks):
                sl = wstage_slots[k % 2]
                sl.wait(SYN)
                SYN.dma_start(out=wstage[k % 2][:, :cols],
                              in_=wsrc[k * 128:(k + 1) * 128, :cols]
                              ).then_inc(ldw.sem, 16)
                v = ldw.take(16)
                SCA.wait_ge(ldw.sem, v)
                SCA.activation(dst[:, k * cols:(k + 1) * cols],
                               wstage[k % 2][:, :cols], AF.Copy
                               ).then_inc(wcd.sem, 1)
                wcd.take()
                sl.freed_by(wcd)

        def tofix(src_ap, kint_ap, out_ap):
            VEC.tensor_scalar(kint_ap, src_ap, 256.0, MAGIC, ALU.mult, ALU.add)
            return VEC.tensor_scalar(out_ap, kint_ap, 1.0 / 256.0, -MAGIC / 256.0,
                                     ALU.mult, ALU.add)

        C448 = float(np.float32(1.0 / 448.0))
        C448Q = C448 * 2.0 ** -9

        def quant(src_ap, W, bm, sc, rc, v, q8, k8, deq, sc2):
            nb = W // 32
            s3 = src_ap.rearrange("p (b c) -> p b c", c=32)
            VEC.tensor_reduce(bm[:, :nb], s3, AX.X, ALU.max,
                              apply_absolute_value=True).then_inc(qvd.sem, 1)
            v1 = qvd.take()
            SCA.wait_ge(qvd.sem, v1)
            SCA.activation(sc[:, :nb], bm[:, :nb], AF.Copy,
                           scale=C448, bias=1e-38)
            SCA.activation(sc2[:, :nb], bm[:, :nb], AF.Copy,
                           scale=C448Q, bias=1e-38 * 2.0 ** -9).then_inc(qsd.sem, 1)
            s1 = qsd.take()
            VEC.wait_ge(qsd.sem, s1)
            VEC.reciprocal(rc[:, :nb], sc[:, :nb]).then_inc(qvd.sem, 1)
            v2 = qvd.take()
            VEC.wait_ge(qvd.sem, v2)
            rc3 = rc[:, :nb].unsqueeze(2).broadcast_to([128, nb, 32])
            VEC.scalar_tensor_tensor(
                v[:, :W].rearrange("p (b c) -> p b c", c=32), s3,
                0.5, rc3, ALU.mult, ALU.mult).then_inc(qvd.sem, 1)
            vq = qvd.take()
            SCA.wait_ge(qvd.sem, vq)
            SCA.wait_ge(gpd.sem, gpd.n)
            SCA.activation(q8[:, :W], v[:, :W], AF.Copy)
            SCA.activation(k8[:, :W], q8[:, :W], AF.Copy,
                           scale=1024.0).then_inc(qsd.sem, 1)
            vs = qsd.take()
            sc3 = sc2[:, :nb].unsqueeze(2).broadcast_to([128, nb, 32])
            VEC.wait_ge(qsd.sem, vs)
            VEC.tensor_tensor(
                deq[:, :W].rearrange("p (b c) -> p b c", c=32),
                k8[:, :W].rearrange("p (b c) -> p b c", c=32),
                sc3, ALU.mult).then_inc(gpd.sem, 1)
            return gpd.take()

        def quant2(src_ap, W, bm, sc, rc, v, q8, k8, deq, sc2):
            nb = W // 32
            s3 = src_ap.rearrange("p (b c) -> p b c", c=32)
            VEC.tensor_reduce(bm[:, :nb], s3, AX.X, ALU.max,
                              apply_absolute_value=True).then_inc(qvd.sem, 1)
            VEC.wait_ge(qvd.sem, qvd.take())
            VEC.tensor_scalar(sc[:, :nb], bm[:, :nb], C448, 1e-38,
                              ALU.mult, ALU.add)
            VEC.tensor_scalar(sc2[:, :nb], bm[:, :nb], C448Q,
                              1e-38 * 2.0 ** -9, ALU.mult,
                              ALU.add).then_inc(qvd.sem, 1)
            VEC.wait_ge(qvd.sem, qvd.take())
            VEC.reciprocal(rc[:, :nb], sc[:, :nb]).then_inc(qvd.sem, 1)
            VEC.wait_ge(qvd.sem, qvd.take())
            rc3 = rc[:, :nb].unsqueeze(2).broadcast_to([128, nb, 32])
            VEC.scalar_tensor_tensor(
                v[:, :W].rearrange("p (b c) -> p b c", c=32), s3,
                0.5, rc3, ALU.mult, ALU.mult).then_inc(qvd.sem, 1)
            vq = qvd.take()
            SCA.wait_ge(qvd.sem, vq)
            SCA.wait_ge(gpd.sem, gpd.n)
            SCA.activation(q8[:, :W], v[:, :W], AF.Copy)
            SCA.activation(k8[:, :W], q8[:, :W], AF.Copy,
                           scale=1024.0).then_inc(qsd.sem, 1)
            vs = qsd.take()
            sc3 = sc2[:, :nb].unsqueeze(2).broadcast_to([128, nb, 32])
            VEC.wait_ge(qsd.sem, vs)
            VEC.tensor_tensor(
                deq[:, :W].rearrange("p (b c) -> p b c", c=32),
                k8[:, :W].rearrange("p (b c) -> p b c", c=32),
                sc3, ALU.mult).then_inc(gpd.sem, 1)
            return gpd.take()

        def make_qpipe(ph, tag, W, dup=2):
            sets = []
            for s in range(dup):
                sets.append({
                    "bm": sb(ph, f"{tag}bm{s}", 128, W // 32, F32),
                    "sc": sb(ph, f"{tag}sc{s}", 128, W // 32, F32),
                    "s2": sb(ph, f"{tag}s2{s}", 128, W // 32, F32),
                    "rc": sb(ph, f"{tag}rc{s}", 128, W // 32, F32),
                    "qv": sb(ph, f"{tag}qv{s}", 128, W, F32),
                    "qk": sb(ph, f"{tag}qk{s}", 128, W, I32),
                    "qs_val": 0,
                    "vg_val": 0,
                })
            return {"sets": sets, "q8": sb(ph, f"{tag}q8", 128, W, FP8),
                    "dq": sb(ph, f"{tag}dq", 128, W, BF16),
                    "ev_val": 0, "k": 0, "pending": None}

        def q_feed(qp, src_ap, W, dma_ch=None, dma_val=None):
            st = qp["sets"][qp["k"] % len(qp["sets"])]
            qp["k"] += 1
            nb = W // 32
            s3 = src_ap.rearrange("p (b c) -> p b c", c=32)
            if dma_val is not None:
                VEC.wait_ge(dma_ch.sem, dma_val)
            VEC.tensor_reduce(st["bm"][:, :nb], s3, AX.X, ALU.max,
                              apply_absolute_value=True).then_inc(qvd2.sem, 1)
            VEC.wait_ge(qvd2.sem, qvd2.take())
            VEC.tensor_scalar(st["sc"][:, :nb], st["bm"][:, :nb], C448,
                              1e-38, ALU.mult, ALU.add)
            VEC.tensor_scalar(st["s2"][:, :nb], st["bm"][:, :nb], C448Q,
                              1e-38 * 2.0 ** -9, ALU.mult,
                              ALU.add).then_inc(qvd2.sem, 1)
            VEC.wait_ge(qvd2.sem, qvd2.take())
            VEC.reciprocal(st["rc"][:, :nb],
                           st["sc"][:, :nb]).then_inc(qvd2.sem, 1)
            VEC.wait_ge(qvd2.sem, qvd2.take())
            if st["qs_val"]:
                VEC.wait_ge(qsd2.sem, st["qs_val"])
            rc3 = st["rc"][:, :nb].unsqueeze(2).broadcast_to([128, nb, 32])
            VEC.scalar_tensor_tensor(
                st["qv"][:, :W].rearrange("p (b c) -> p b c", c=32), s3,
                0.5, rc3, ALU.mult, ALU.mult).then_inc(qvd2.sem, 1)
            vq = qvd2.take()
            SCA.wait_ge(qvd2.sem, vq)
            if st["vg_val"]:
                SCA.wait_ge(gpd2.sem, st["vg_val"])
            SCA.activation(qp["q8"][:, :W], st["qv"][:, :W], AF.Copy)
            SCA.activation(st["qk"][:, :W], qp["q8"][:, :W], AF.Copy,
                           scale=1024.0).then_inc(qsd2.sem, 1)
            vs = qsd2.take()
            st["qs_val"] = vs
            vg = gpd2.take()
            st["vg_val"] = vg
            if qp["pending"] is not None:
                qp["pending"]()

            def _deq(_st=st, _W=W, _nb=nb, _vs=vs):
                VEC.wait_ge(qsd2.sem, _vs)
                if qp["ev_val"]:
                    VEC.wait_ge(evd.sem, qp["ev_val"])
                sc3 = _st["s2"][:, :_nb].unsqueeze(2).broadcast_to(
                    [128, _nb, 32])
                VEC.tensor_tensor(
                    qp["dq"][:, :_W].rearrange("p (b c) -> p b c", c=32),
                    _st["qk"][:, :_W].rearrange("p (b c) -> p b c", c=32),
                    sc3, ALU.mult).then_inc(gpd2.sem, 1)

            qp["pending"] = _deq
            return vg, vq

        def q_flush(qp):
            if qp["pending"] is not None:
                qp["pending"]()
                qp["pending"] = None

        lnst = sb(top, "lnst", 128, 16, F32)
        ln_free = [Slot(), Slot()]

        def emit_ln(x_ap, xn_ap, par, x_ready_val, lntrash):
            c0 = par * 8
            st = lnst
            SCA.wait_ge(ldx.sem, x_ready_val)
            ln_free[par].wait(SCA)
            SCA.activation(lntrash[:], x_ap, AF.Square,
                           accum_out=st[:, c0:c0 + 1])
            SCA.activation(lntrash[:], x_ap, AF.Copy,
                           accum_out=st[:, c0 + 1:c0 + 2])
            SCA.activation(st[:, c0 + 2:c0 + 3], st[:, c0 + 1:c0 + 2],
                           AF.Copy, scale=1.0 / 2048.0)
            SCA.activation(st[:, c0 + 4:c0 + 5], st[:, c0 + 1:c0 + 2],
                           AF.Square, scale=1.0 / 2048.0)
            SCA.activation(st[:, c0 + 3:c0 + 4], st[:, c0:c0 + 1],
                           AF.Copy, scale=1.0 / 2048.0,
                           bias=1e-5).then_inc(lnd.sem, 1)
            vl = lnd.take()
            VEC.wait_ge(lnd.sem, vl)
            VEC.tensor_tensor(st[:, c0 + 5:c0 + 6], st[:, c0 + 3:c0 + 4],
                              st[:, c0 + 4:c0 + 5],
                              ALU.subtract).then_inc(lnv.sem, 1)
            vv = lnv.take()
            ln_free[par].freed_by(lnv, vv)
            SCA.wait_ge(lnv.sem, vv)
            SCA.activation(st[:, c0 + 6:c0 + 7], st[:, c0 + 5:c0 + 6],
                           AF.Sqrt).then_inc(lns.sem, 1)
            vq = lns.take()
            VEC.wait_ge(lns.sem, vq)
            VEC.reciprocal(st[:, c0 + 7:c0 + 8],
                           st[:, c0 + 6:c0 + 7]).then_inc(lnv.sem, 1)
            vv2 = lnv.take()
            VEC.wait_ge(lnv.sem, vv2)
            VEC.tensor_scalar(xn_ap, x_ap, st[:, c0 + 2:c0 + 3],
                              st[:, c0 + 7:c0 + 8], ALU.subtract, ALU.mult)

        tr_state = {"last_evac": [0, 0]}

        def transpose_blocks(src_fn, nblk, dst_fn, ready_ch, ready_val,
                             extra_wait=None):
            ngr = (nblk + 3) // 4
            for g in range(ngr):
                bank = (tr_state.setdefault("g", 0)) % 2
                tr_state["g"] = tr_state.get("g", 0) + 1
                ps = psT0 if bank == 0 else psT1
                TEN.wait_ge(ready_ch.sem, ready_val)
                if extra_wait is not None:
                    extra_wait.wait(TEN)
                    extra_wait = None
                if tr_state["last_evac"][bank]:
                    TEN.wait_ge(evd.sem, tr_state["last_evac"][bank])
                n_in_g = min(4 * g + 4, nblk) - 4 * g
                last = None
                for q in range(n_in_g):
                    j = 4 * g + q
                    last = TEN.transpose(ps[:, q * 128:(q + 1) * 128],
                                         src_fn(j), ident_t[:])
                last.then_inc(trd.sem, 1)
                tv = trd.take()
                SCA.wait_ge(trd.sem, tv)
                SCA.activation(dst_fn(g),
                               ps[:, :n_in_g * 128].rearrange(
                                   "p (a t) -> p a t", t=128),
                               AF.Copy).then_inc(evd.sem, 1)
                tr_state["last_evac"][bank] = evd.take()
            return evd.n

        for src, dst in ((s_i, vsi), (s_f, vsf), (s_g, vsg), (s_o, vso),
                         (gnw, vgn), (maskv, vmask), (ident, ident_t)):
            SYN.dma_start(out=dst[:], in_=src[:]).then_inc(ld.sem, 16)
            ld.take(16)
        VEC.wait_ge(ld.sem, ld.n)
        VEC.memset(ones_f[:], 1.0)
        VEC.memset(ones128[:], 1.0)
        VEC.memset(rbc_src[:], 0.0)
        VEC.scalar_tensor_tensor(vgs[:], vgn[:], -1.0, vso[:], ALU.mult, ALU.mult)
        VEC.memset(state[:], 0.0)

        with ExitStack() as ph:
            wib = sb(ph, "wib", 128, DC * E, BF16)
            wfb = sb(ph, "wfb", 128, DC * E, BF16)
            xstage = [sb(ph, f"xstage{j}", 128, D, F32) for j in range(2)]
            wstage = xstage
            xn_b = [sb(ph, f"xn{j}", 128, D, F32) for j in range(2)]
            kint = sb(ph, "kint", 128, D, I32)
            xfix_b = sb(ph, "xfix_b", 128, D, BF16)
            xfixT = [sb(ph, f"xfixT{j}", 128, DC * TTA, BF16) for j in range(2)]
            fb = [sb(ph, f"fb{j}", 128, TTA, F32) for j in range(2)]
            lnt = sb(ph, "lnt", 128, D, F32)
            i1b = sb(ph, "i1b", 128, TTA, F32)
            iqn = sb(ph, "iqn", 128, TTA, F32)
            hn = [sb(ph, f"hn{j}", 128, TTA, F32) for j in range(2)]

            wstage_slots = [Slot(), Slot()]
            xstage_slots = [Slot(), Slot()]
            xn_slots = [Slot(), Slot()]
            xfix_slot = Slot()
            xfixT_slots = [Slot(), Slot()]
            hn_slots = [Slot(), Slot()]
            fb_slots = [Slot(), Slot()]
            stc_vals = [0] * EC

            load_cast_weight(w_i, wib, DC, E, wstage, wstage_slots)
            load_cast_weight(w_f, wfb, DC, E, wstage, wstage_slots)
            w_ready = wcd.n
            for sl in xstage_slots:
                sl.freed_by(wcd, w_ready)

            def attn_if_scan(x_src, own):
                for t in range(NT):
                    tp = t % 2
                    xfixT_slots[tp].wait(TEN)
                    for c in range(2):
                        cp = c
                        row0 = t * TTA + c * 128
                        xstage_slots[cp].wait(SYN)
                        SYN.dma_start(out=xstage[cp][:],
                                      in_=x_src[row0:row0 + 128, :]
                                      ).then_inc(ldx.sem, 16)
                        vx = ldx.take(16)
                        xn_slots[cp].wait(VEC)
                        emit_ln(xstage[cp][:], xn_b[cp][:], cp, vx, lnt[:])
                        xfix_slot.wait(VEC)
                        tofix(xn_b[cp][:], kint[:], xfix_b[:]).then_inc(tfd.sem, 1)
                        vtf = tfd.take()
                        xstage_slots[cp].freed_by(tfd)
                        xn_slots[cp].freed_by(tfd)

                        def srcf(j, _xf=xfix_b):
                            return _xf[:, j * 128:(j + 1) * 128]

                        def dstf(g, _xT=xfixT[tp], _c=c):
                            return _xT[:].rearrange("p (dc tt) -> p dc tt",
                                                    tt=TTA)[:, 4 * g:4 * g + 4,
                                                            _c * 128:(_c + 1) * 128]

                        transpose_blocks(srcf, DC, dstf, tfd, vtf)
                        xfix_slot.freed_by(trd)
                    ev_tile = evd.n
                    last_mm_val = None
                    pi_free = [Slot(), Slot()]
                    pf_free = [Slot(), Slot()]
                    for ec in range(EC):
                        pi = psA if ec % 2 == 0 else psC
                        pf = psB if ec % 2 == 0 else psD
                        TEN.wait_ge(evd.sem, ev_tile)
                        if ec == 0:
                            TEN.wait_ge(wcd.sem, w_ready)
                        pi_free[ec % 2].wait(TEN)
                        pf_free[ec % 2].wait(TEN)
                        for dc in range(DC):
                            TEN.matmul(pi[:, :TTA],
                                       wib[:, dc * E + ec * 128:dc * E + ec * 128 + 128],
                                       xfixT[tp][:, dc * TTA:(dc + 1) * TTA],
                                       start=(dc == 0), stop=(dc == DC - 1))
                        mm = None
                        for dc in range(DC):
                            mm = TEN.matmul(pf[:, :TTA],
                                            wfb[:, dc * E + ec * 128:dc * E + ec * 128 + 128],
                                            xfixT[tp][:, dc * TTA:(dc + 1) * TTA],
                                            start=(dc == 0), stop=(dc == DC - 1))
                        mm.then_inc(mmd.sem, 1)
                        vm = mmd.take()
                        last_mm_val = vm
                        SCA.wait_ge(mmd.sem, vm)
                        fb_slots[ec % 2].wait(SCA)
                        SCA.activation(fb[ec % 2][:], pf[:, :TTA], AF.Sigmoid,
                                       scale=vsf[:, ec:ec + 1]).then_inc(fsd.sem, 1)
                        vf = fsd.take()
                        pf_free[ec % 2].freed_by(fsd)
                        VEC.wait_ge(mmd.sem, vm)
                        VEC.tensor_scalar(i1b[:], pi[:, :TTA], vsi[:, ec:ec + 1],
                                          None, ALU.mult).then_inc(ivd.sem, 1)
                        ivd.take()
                        pi_free[ec % 2].freed_by(ivd)
                        VEC.wait_ge(fsd.sem, vf)
                        VEC.scalar_tensor_tensor(iqn[:], fb[ec % 2][:], 1.0, i1b[:],
                                                 ALU.subtract, ALU.mult)
                        hb = hn[ec % 2]
                        hn_slots[ec % 2].wait(VEC)
                        if own and t == 0:
                            init_ap = init_own[:, ec:ec + 1]
                        else:
                            init_ap = state[:, ec:ec + 1]
                        VEC.wait_ge(stc.sem, stc_vals[ec])
                        VEC.tensor_tensor_scan(hb[:], fb[ec % 2][:], iqn[:],
                                               init_ap, ALU.mult,
                                               ALU.add).then_inc(scd.sem, 1)
                        vsc = scd.take()
                        SCA.wait_ge(scd.sem, vsc)
                        SCA.activation(state[:, ec:ec + 1], hb[:, TTA - 1:TTA],
                                       AF.Copy).then_inc(stc.sem, 1)
                        stc_vals[ec] = stc.take()
                        fb_slots[ec % 2].freed_by(scd, vsc)
                        if own:
                            SYN.wait_ge(scd.sem, vsc)
                            SYN.dma_start(
                                out=sc_h[t, :, ec * TTA:(ec + 1) * TTA],
                                in_=hb[:]).then_inc(std.sem, 16)
                            std.take(16)
                            hn_slots[ec % 2].freed_by(std)
                            hn_slots[ec % 2].also_freed_by(stc, stc_vals[ec])
                        else:
                            hn_slots[ec % 2].freed_by(stc, stc_vals[ec])
                    if own:
                        SYN.wait_ge(evd.sem, ev_tile)
                        SYN.dma_start(out=sc_xfixT[t],
                                      in_=xfixT[tp][:]).then_inc(std.sem, 16)
                        std.take(16)
                        xfixT_slots[tp].freed_by(std)
                        xfixT_slots[tp].also_freed_by(mmd, last_mm_val)
                    else:
                        xfixT_slots[tp].freed_by(mmd, last_mm_val)
                    maybe_barrier()

            attn_if_scan(x_pre, own=False)
            SCA.activation(init_own[:], state[:], AF.Copy,
                           scale=vmask[:, 0:1]).then_inc(stc.sem, 1)
            v0 = stc.take()
            for _e in range(EC):
                stc_vals[_e] = v0
            attn_if_scan(x_own, own=True)
            barrier()

        with ExitStack() as ph:
            wgb = sb(ph, "wgb", 128, DC * E, BF16)
            wob = sb(ph, "wob", 128, EC * E, BF16)
            xfT = [sb(ph, f"xfT{j}", 128, DC * TTA, BF16) for j in range(2)]
            hsb = sb(ph, "hsb", 128, EC * TTA, F32)
            silg = sb(ph, "silg", 128, EC * TTA, BF16)
            obuf = sb(ph, "obuf", 128, EC * TTA, BF16)
            hhb = [sb(ph, f"hhb{j}", 128, TTA, F32) for j in range(2)]
            t1b = sb(ph, "t1b", 128, TTA, F32)
            sstage = [sb(ph, f"sstage{j}", 128, E, F32) for j in range(2)]
            skint = sb(ph, "skint", 128, 512, I32)
            mstmp = sb(ph, "mstmp", 1, TTA, F32)
            wstage = sstage

            wstage_slots = [Slot(), Slot()]
            psV_slot = Slot()
            silg_slot = Slot()
            obuf_slot = Slot()
            rstd_slot = Slot()
            xfT_slots = [Slot(), Slot()]
            hsb_slot = Slot()
            sstage_slots = [Slot(), Slot()]
            hh_slots = [Slot(), Slot()]

            load_cast_weight(w_g, wgb, DC, E, wstage, wstage_slots)
            load_cast_weight(w_oT, wob, EC, E, wstage, wstage_slots)
            w2_ready = wcd.n
            for sl in sstage_slots:
                sl.freed_by(wcd, w2_ready)

            for t in range(NT):
                tp = t % 2
                t0 = t * TTA
                xfT_slots[tp].wait(SYN)
                SYN.dma_start(out=xfT[tp][:],
                              in_=sc_xfixT[t]).then_inc(ldx.sem, 16)
                vxf = ldx.take(16)
                hsb_slot.wait(SYN)
                SYN.dma_start(out=hsb[:],
                              in_=sc_h[t]).then_inc(ldx.sem, 16)
                vh = ldx.take(16)
                pg_free = [Slot(), Slot()]
                mm2 = None
                for ec in range(EC):
                    pg = psA if ec % 2 == 0 else psC
                    TEN.wait_ge(ldx.sem, vxf)
                    if ec == 0:
                        TEN.wait_ge(wcd.sem, w2_ready)
                    pg_free[ec % 2].wait(TEN)
                    mm = None
                    for dc in range(DC):
                        mm = TEN.matmul(pg[:, :TTA],
                                        wgb[:, dc * E + ec * 128:dc * E + ec * 128 + 128],
                                        xfT[tp][:, dc * TTA:(dc + 1) * TTA],
                                        start=(dc == 0), stop=(dc == DC - 1))
                    mm.then_inc(mmd.sem, 1)
                    vm = mmd.take()
                    if ec == EC - 1:
                        xfT_slots[tp].freed_by(mmd, vm)
                    SCA.wait_ge(mmd.sem, vm)
                    if ec == 0:
                        silg_slot.wait(SCA)
                    SCA.activation(silg[:, ec * TTA:(ec + 1) * TTA], pg[:, :TTA],
                                   AF.Silu,
                                   scale=vsg[:, ec:ec + 1]).then_inc(fsd.sem, 1)
                    fsd.take()
                    pg_free[ec % 2].freed_by(fsd)
                    VEC.wait_ge(ldx.sem, vh)
                    hq = hhb[ec % 2]
                    hh_slots[ec % 2].wait(VEC)
                    VEC.tensor_tensor(hq[:], hsb[:, ec * TTA:(ec + 1) * TTA],
                                      hsb[:, ec * TTA:(ec + 1) * TTA],
                                      ALU.mult).then_inc(qvd.sem, 1)
                    vq = qvd.take()
                    TEN.wait_ge(qvd.sem, vq)
                    TEN.matmul(psU[0:1, :TTA], ones_f[:], hq[:],
                               start=(ec == 0),
                               stop=(ec == EC - 1)).then_inc(mmd.sem, 1)
                    vm2 = mmd.take()
                    hh_slots[ec % 2].freed_by(mmd, vm2)
                vms = vm2
                VEC.wait_ge(mmd.sem, vms)
                VEC.tensor_scalar(mstmp[:], psU[0:1, :TTA], 1.0 / 2048.0, 1e-5,
                                  ALU.mult, ALU.add).then_inc(qvd.sem, 1)
                vq = qvd.take()
                SCA.wait_ge(qvd.sem, vq)
                rstd_slot.wait(SCA)
                SCA.activation(rstd_b[0:1, :TTA], mstmp[:],
                               AF.Sqrt).then_inc(qsd.sem, 1)
                vs2 = qsd.take()
                VEC.wait_ge(qsd.sem, vs2)
                psV_slot.wait(VEC)
                VEC.reciprocal(rbc_src[0:1, :TTA],
                               rstd_b[0:1, :TTA]).then_inc(qvd.sem, 1)
                vq = qvd.take()
                rstd_slot.freed_by(qvd, vq)
                TEN.wait_ge(qvd.sem, vq)
                TEN.matmul(psV[:, :TTA], ones128[:], rbc_src[:, :TTA],
                           start=True, stop=True).then_inc(mmd.sem, 1)
                vbc = mmd.take()
                VEC.wait_ge(mmd.sem, vbc)
                VEC.wait_ge(fsd.sem, fsd.n)
                obuf_slot.wait(VEC)
                lastob = None
                for ec in range(EC):
                    VEC.scalar_tensor_tensor(t1b[:],
                                             hsb[:, ec * TTA:(ec + 1) * TTA],
                                             vgs[:, ec:ec + 1], psV[:, :TTA],
                                             ALU.mult, ALU.mult)
                    lastob = VEC.tensor_tensor(obuf[:, ec * TTA:(ec + 1) * TTA],
                                               t1b[:],
                                               silg[:, ec * TTA:(ec + 1) * TTA],
                                               ALU.mult)
                lastob.then_inc(scd.sem, 1)
                vob_ready = scd.take()
                hsb_slot.freed_by(scd, vob_ready)
                psV_slot.freed_by(scd, vob_ready)
                po_free = [Slot(), Slot()]
                for tc in range(TTA // 128):
                    sp = sstage[tc % 2]
                    sstage_slots[tc % 2].wait(VEC)
                    for oc in range(4):
                        po = psB if oc % 2 == 0 else psD
                        TEN.wait_ge(scd.sem, vob_ready)
                        po_free[oc % 2].wait(TEN)
                        mm3 = None
                        for ec in range(EC):
                            mm3 = TEN.matmul(
                                po[:, :512],
                                obuf[:, ec * TTA + tc * 128:ec * TTA + tc * 128 + 128],
                                wob[:, ec * E + oc * 512:ec * E + (oc + 1) * 512],
                                start=(ec == 0), stop=(ec == EC - 1))
                        mm3.then_inc(mmd.sem, 1)
                        vm3 = mmd.take()
                        VEC.wait_ge(mmd.sem, vm3)
                        VEC.tensor_scalar(skint[:], po[:, :512], 256.0, MAGIC,
                                          ALU.mult, ALU.add)
                        VEC.tensor_scalar(sp[:, oc * 512:(oc + 1) * 512],
                                          skint[:], 1.0 / 256.0, -MAGIC / 256.0,
                                          ALU.mult, ALU.add).then_inc(ivd.sem, 1)
                        ivd.take()
                        po_free[oc % 2].freed_by(ivd)
                        obuf_slot.freed_by(mmd, vm3)
                        silg_slot.freed_by(mmd, vm3)
                    VEC.engine_nop().then_inc(scd.sem, 1)
                    vsr = scd.take()
                    SYN.wait_ge(scd.sem, vsr)
                    SYN.dma_start(out=sc_s[t0 + tc * 128:t0 + tc * 128 + 128, :],
                                  in_=sp[:]).then_inc(std.sem, 16)
                    std.take(16)
                    sstage_slots[tc % 2].freed_by(std)
                maybe_barrier()
            barrier()

        with ExitStack() as ph:
            xqT = sb(ph, "xqT", 128, DC * TOK, BF16)
            xstage = [sb(ph, f"x0stage{j}", 128, D, F32) for j in range(2)]
            qp = make_qpipe(ph, "qp", D, dup=2)
            gwT = [sb(ph, f"gwT{j}", 128, DC * ITILE, BF16) for j in range(2)]
            upT = [sb(ph, f"upT{j}", 128, DC * ITILE, BF16) for j in range(2)]
            silgate = [sb(ph, f"silgate{j}", 128, ITILE, F32) for j in range(2)]
            h2b = [sb(ph, f"h2b{j}", 128, ITILE, F32) for j in range(2)]
            h2v = [sb(ph, f"h2v{j}", 128, ITILE, F32) for j in range(2)]
            h2q8s = sb(ph, "h2q8s", 128, ITILE, FP8)
            h2q8 = [h2q8s, h2q8s]
            h2k = [sb(ph, f"h2k{j}", 128, ITILE, I32) for j in range(2)]
            h2deq = [sb(ph, f"h2deq{j}", 128, ITILE, BF16) for j in range(2)]
            h2bm = [sb(ph, f"h2bm{j}", 128, ITILE // 32, F32) for j in range(2)]
            h2sc = [sb(ph, f"h2sc{j}", 128, ITILE // 32, F32) for j in range(2)]
            h2rc = [sb(ph, f"h2rc{j}", 128, ITILE // 32, F32) for j in range(2)]
            h2s2 = [sb(ph, f"h2s2{j}", 128, ITILE // 32, F32) for j in range(2)]
            _h2qTs = sb(ph, "h2qTs", 128, 512, BF16)
            h2qTs = [_h2qTs, _h2qTs]

            xst_slots = [Slot(), Slot()]
            _xns = Slot()
            xn_slots = [_xns, _xns]
            gwT_slots = [Slot(), Slot()]
            upT_slots = [Slot(), Slot()]
            _h2qTs_slot = Slot()
            h2qTs_slots = [_h2qTs_slot, _h2qTs_slot]
            psAB_slots = [Slot(), Slot()]
            silgate_slots = [Slot(), Slot()]
            h2chain_slots = [Slot(), Slot()]

            def b0_transp(g, vg):
                def srcq(j, _q=qp["dq"]):
                    return _q[:, j * 128:(j + 1) * 128]

                def dstq(gg, _g=g):
                    return xqT[:].rearrange("p (dc tt) -> p dc tt",
                                            tt=TOK)[:, 4 * gg:4 * gg + 4,
                                                    _g * 128:(_g + 1) * 128]

                qp["ev_val"] = transpose_blocks(srcq, DC, dstq, gpd2, vg)

            b0_vgs = {}
            for g in range(TOK // 128):
                cp = g % 2
                xst_slots[cp].wait(SYN)
                SYN.dma_start(out=xstage[cp][:],
                              in_=x_own[g * 128:(g + 1) * 128, :]
                              ).then_inc(ldx.sem, 16)
                vx = ldx.take(16)
                xnbuf = qp["sets"][1 - cp]["qv"]
                _qsv = qp["sets"][1 - cp]["qs_val"]
                if _qsv:
                    VEC.wait_ge(qsd2.sem, _qsv)
                emit_ln(xstage[cp][:], xnbuf[:], cp, vx,
                        qp["sets"][cp]["qv"][:])
                vg, vq = q_feed(qp, xnbuf[:], D)
                b0_vgs[g] = vg
                xst_slots[cp].freed_by(qvd2, vq)
                if g > 0:
                    b0_transp(g - 1, b0_vgs[g - 1])
            q_flush(qp)
            b0_transp(TOK // 128 - 1, b0_vgs[TOK // 128 - 1])
            if DEBUG_OUTPUTS:
                SYN.wait_ge(evd.sem, evd.n)
                SYN.dma_start(out=dbg_xqT[:], in_=xqT[:]).then_inc(std.sem, 16)
                std.take(16)
            maybe_barrier()

            NIT = II // ITILE

            def issue_wload(p, j):
                c = p * 8 + j
                xst_slots[c % 2].wait(SYN)
                w = gate_w if j < 4 else up_w
                r0 = p * ITILE + (j % 4) * 128
                SYN.dma_start(out=xstage[c % 2][:, :D],
                              in_=w[r0:r0 + 128, :]).then_inc(ldw.sem, 16)
                return ldw.take(16)

            def do_feed(p, j, vw):
                c = p * 8 + j
                vg, vq = q_feed(qp, xstage[c % 2][:, :D], D, ldw, vw)
                xst_slots[c % 2].freed_by(qvd2, vq)
                return vg

            def do_transp(p, j, vg):
                par2 = p % 2
                dstT = gwT if j < 4 else upT
                slots = gwT_slots if j < 4 else upT_slots
                if j == 0 or j == 4:
                    slots[par2].wait(TEN)
                sub = j % 4

                def srcw(jj, _q=qp["dq"]):
                    return _q[:, jj * 128:(jj + 1) * 128]

                def dstw(gg, _s=sub, _d=dstT, _p=par2):
                    return _d[_p][:].rearrange("p (dc tt) -> p dc tt",
                                               tt=ITILE)[
                        :, 4 * gg:4 * gg + 4, _s * 128:(_s + 1) * 128]

                ev = transpose_blocks(srcw, DC, dstw, gpd2, vg)
                qp["ev_val"] = ev
                return ev

            vws = {0: issue_wload(0, 0), 1: issue_wload(0, 1)}
            vgs_ = {}
            for j in range(8):
                vgs_[j] = do_feed(0, j, vws[j])
                if j + 2 < 8:
                    vws[j + 2] = issue_wload(0, j + 2)
                if j > 0:
                    do_transp(0, j - 1, vgs_[j - 1])
            q_flush(qp)
            batch_ready = do_transp(0, 7, vgs_[7])

            h2B = [None]
            h2C = [None]
            h2_k8v = {0: 0, 1: 0}
            h2_deqv = {0: 0, 1: 0}
            h2_evtr = {0: 0, 1: 0}
            NB2 = ITILE // 32
            for it in range(NIT):
                par = it % 2
                prep = it + 1 < NIT
                if prep:
                    vws = {0: issue_wload(it + 1, 0),
                           1: issue_wload(it + 1, 1)}
                    vgs_ = {}
                wready = batch_ready
                if DEBUG_OUTPUTS:
                    SYN.wait_ge(evd.sem, wready)
                    SYN.dma_start(out=dbg_gwT[it],
                                  in_=gwT[par][:]).then_inc(std.sem, 16)
                    SYN.dma_start(out=dbg_upT[it],
                                  in_=upT[par][:]).then_inc(std.sem, 16)
                    std.take(32)
                lastmm = None
                for tc in range(TOK // 128):
                    pp = tc % 2
                    pg = psA if pp == 0 else psC
                    pu = psB if pp == 0 else psD
                    TEN.wait_ge(evd.sem, wready)
                    psAB_slots[pp].wait(TEN)
                    for hc in range(DC):
                        TEN.matmul(pg[:, :ITILE],
                                   xqT[:, hc * TOK + tc * 128:hc * TOK + tc * 128 + 128],
                                   gwT[par][:, hc * ITILE:(hc + 1) * ITILE],
                                   start=(hc == 0), stop=(hc == DC - 1))
                    mmu = None
                    for hc in range(DC):
                        mmu = TEN.matmul(pu[:, :ITILE],
                                         xqT[:, hc * TOK + tc * 128:hc * TOK + tc * 128 + 128],
                                         upT[par][:, hc * ITILE:(hc + 1) * ITILE],
                                         start=(hc == 0), stop=(hc == DC - 1))
                    mmu.then_inc(mmd.sem, 1)
                    vm = mmd.take()
                    lastmm = vm
                    if h2C[0] is not None:
                        h2C[0]()
                        h2C[0] = None
                    if h2B[0] is not None:
                        h2C[0] = h2B[0]()
                        h2B[0] = None
                    SCA.wait_ge(mmd.sem, vm)
                    silgate_slots[pp].wait(SCA)
                    SCA.activation(silgate[pp][:], pg[:, :ITILE],
                                   AF.Silu).then_inc(fsd.sem, 1)
                    vf = fsd.take()
                    VEC.wait_ge(fsd.sem, vf)
                    VEC.tensor_tensor(h2b[pp][:], silgate[pp][:], pu[:, :ITILE],
                                      ALU.mult).then_inc(qvd.sem, 1)
                    qvd.take()
                    psAB_slots[pp].freed_by(qvd)
                    silgate_slots[pp].freed_by(qvd)
                    if DEBUG_OUTPUTS:
                        VEC.engine_nop().then_inc(qvd.sem, 1)
                        _vq = qvd.take()
                        SYN.wait_ge(qvd.sem, _vq)
                        SYN.dma_start(out=dbg_h2b[it, tc],
                                      in_=h2b[pp][:]).then_inc(std.sem, 16)
                        std.take(16)
                    VEC.tensor_reduce(
                        h2bm[pp][:, :NB2],
                        h2b[pp][:].rearrange("p (b c) -> p b c", c=32),
                        AX.X, ALU.max,
                        apply_absolute_value=True).then_inc(qvd.sem, 1)
                    vr = qvd.take()
                    SCA.wait_ge(qvd.sem, vr)
                    SCA.activation(h2sc[pp][:, :NB2], h2bm[pp][:, :NB2],
                                   AF.Copy, scale=C448, bias=1e-38)
                    SCA.activation(h2s2[pp][:, :NB2], h2bm[pp][:, :NB2],
                                   AF.Copy, scale=C448Q,
                                   bias=1e-38 * 2.0 ** -9).then_inc(qsd.sem, 1)
                    vsc = qsd.take()

                    def stB(_it=it, _tc=tc, _pp=pp, _vsc=vsc):
                        VEC.wait_ge(qsd.sem, _vsc)
                        VEC.reciprocal(h2rc[_pp][:, :NB2],
                                       h2sc[_pp][:, :NB2]).then_inc(qvd.sem, 1)
                        VEC.wait_ge(qvd.sem, qvd.take())
                        if h2_k8v[_pp]:
                            VEC.wait_ge(qsd.sem, h2_k8v[_pp])
                        rc3 = h2rc[_pp][:, :NB2].unsqueeze(2).broadcast_to(
                            [128, NB2, 32])
                        VEC.scalar_tensor_tensor(
                            h2v[_pp][:].rearrange("p (b c) -> p b c", c=32),
                            h2b[_pp][:].rearrange("p (b c) -> p b c", c=32),
                            0.5, rc3, ALU.mult, ALU.mult).then_inc(qvd.sem, 1)
                        vq2 = qvd.take()
                        SCA.wait_ge(qvd.sem, vq2)
                        if h2_deqv[_pp]:
                            SCA.wait_ge(gpd.sem, h2_deqv[_pp])
                        SCA.activation(h2q8[_pp][:], h2v[_pp][:], AF.Copy)
                        SCA.activation(h2k[_pp][:], h2q8[_pp][:], AF.Copy,
                                       scale=1024.0).then_inc(qsd.sem, 1)
                        vs2 = qsd.take()
                        h2_k8v[_pp] = vs2

                        def stC(_vs2=vs2):
                            VEC.wait_ge(qsd.sem, _vs2)
                            if h2_evtr[_pp]:
                                VEC.wait_ge(evd.sem, h2_evtr[_pp])
                            s23 = h2s2[_pp][:, :NB2].unsqueeze(2).broadcast_to(
                                [128, NB2, 32])
                            VEC.tensor_tensor(
                                h2deq[_pp][:].rearrange("p (b c) -> p b c",
                                                        c=32),
                                h2k[_pp][:].rearrange("p (b c) -> p b c",
                                                      c=32),
                                s23, ALU.mult).then_inc(gpd.sem, 1)
                            vdq = gpd.take()
                            h2_deqv[_pp] = vdq

                            def srch(j, _q=h2deq[_pp]):
                                return _q[:, j * 128:(j + 1) * 128]

                            sp = h2qTs[0]
                            h2qTs_slots[0].wait(TEN)

                            def dsth(gg, _sp=sp):
                                return _sp[:].rearrange(
                                    "p (a t) -> p a t",
                                    t=128)[:, 4 * gg:4 * gg + 4, :]

                            ev = transpose_blocks(srch, ITILE // 128, dsth,
                                                  gpd, vdq)
                            h2_evtr[_pp] = ev
                            SYN.wait_ge(evd.sem, ev)
                            SYN.dma_start(
                                out=sc_h2qT[_tc, :,
                                            _it * (ITILE // 128) * 128:
                                            (_it + 1) * (ITILE // 128) * 128],
                                in_=sp[:]).then_inc(std.sem, 16)
                            std.take(16)
                            h2qTs_slots[0].freed_by(std)

                        return stC

                    h2B[0] = stB
                    if prep:
                        if tc % 2 == 0:
                            j = tc // 2
                            vgs_[j] = do_feed(it + 1, j, vws[j])
                            if j + 2 < 8:
                                vws[j + 2] = issue_wload(it + 1, j + 2)
                        elif tc >= 3:
                            j = (tc - 3) // 2
                            do_transp(it + 1, j, vgs_[j])
                gwT_slots[par].freed_by(mmd, lastmm)
                upT_slots[par].freed_by(mmd, lastmm)
                if prep:
                    q_flush(qp)
                    batch_ready = do_transp(it + 1, 7, vgs_[7])
                maybe_barrier()
            if h2C[0] is not None:
                h2C[0]()
                h2C[0] = None
            if h2B[0] is not None:
                h2C[0] = h2B[0]()
                h2B[0] = None
                h2C[0]()
                h2C[0] = None
            barrier()

        with ExitStack() as ph:
            QW = 1024
            NQ = II // QW
            NCH = (HT // 128) * NQ
            dwT = [sb(ph, f"dwT{j}", 128, ICH * HT, BF16) for j in range(2)]
            h2qsb = [sb(ph, f"h2qsb{j}", 128, ICH * 128, BF16) for j in range(2)]
            wstage = [sb(ph, f"w4stage{j}", 128, QW, F32) for j in range(3)]
            qpd = make_qpipe(ph, "qd", QW, dup=2)
            _glb = sb(ph, "gl_b", 128, HT, F32)
            gl_b = [_glb, _glb]
            t_sl = sb(ph, "t_sl", 128, HT, F32)
            _ssl = sb(ph, "s_sl", 128, HT, F32)
            s_sl = [_ssl, _ssl]
            _xsl = sb(ph, "x_sl", 128, HT, F32)
            x_sl = [_xsl, _xsl]
            o_sl = [sb(ph, f"o_sl{j}", 128, HT, F32) for j in range(2)]

            wst_slots = [Slot() for _ in range(3)]
            dwT_slots = [Slot(), Slot()]
            h2q_slots = [Slot(), Slot()]
            _sx = Slot()
            sx_slots = [_sx, _sx]
            o_slots = [Slot(), Slot()]
            _gl = Slot()
            gl_slots = [_gl, _gl]
            psA_slots = [Slot(), Slot()]

            def d_load(p, c):
                wst_slots[c % 3].wait(SYN)
                sub, qq = divmod(c, NQ)
                r0 = p * HT + sub * 128
                SYN.dma_start(out=wstage[c % 3][:],
                              in_=down_w[r0:r0 + 128, qq * QW:(qq + 1) * QW]
                              ).then_inc(ldw.sem, 16)
                return ldw.take(16)

            def d_feed(c, vw):
                vg, vq = q_feed(qpd, wstage[c % 3][:], QW, ldw, vw)
                wst_slots[c % 3].freed_by(qvd2, vq)
                return vg

            def d_transp(p, c, vg):
                par2 = p % 2
                if c == 0:
                    dwT_slots[par2].wait(TEN)
                sub, qq = divmod(c, NQ)

                def srcd(j, _q=qpd["dq"]):
                    return _q[:, j * 128:(j + 1) * 128]

                def dstd(gg, _s=sub, _q=qq, _p=par2):
                    return dwT[_p][:].rearrange("p (ic tt) -> p ic tt",
                                                tt=HT)[
                        :, _q * (QW // 128) + 4 * gg:
                        _q * (QW // 128) + 4 * gg + 4,
                        _s * 128:(_s + 1) * 128]

                ev = transpose_blocks(srcd, QW // 128, dstd, gpd2, vg)
                qpd["ev_val"] = ev
                return ev

            vws = {c: d_load(0, c) for c in range(3)}
            vgs_ = {}
            for c in range(NCH):
                vgs_[c] = d_feed(c, vws[c])
                if c + 3 < NCH:
                    vws[c + 3] = d_load(0, c + 3)
                if c > 0:
                    d_transp(0, c - 1, vgs_[c - 1])
            q_flush(qpd)
            dw_ready = d_transp(0, NCH - 1, vgs_[NCH - 1])

            for ht in range(D // HT):
                par = ht % 2
                prep = ht + 1 < D // HT
                if prep:
                    vws = {c: d_load(ht + 1, c) for c in range(3)}
                    vgs_ = {}
                wready4 = dw_ready
                lastmm4 = None
                for tc in range(TOK // 128):
                    hp = tc % 2
                    h2q_slots[hp].wait(SYN)
                    vhs = []
                    for ck in range(4):
                        _ch = hch[ck]
                        SYN.dma_start(
                            out=h2qsb[hp][:, ck * 2048:(ck + 1) * 2048],
                            in_=sc_h2qT[tc, :, ck * 2048:(ck + 1) * 2048]
                        ).then_inc(_ch.sem, 16)
                        vhs.append((_ch, _ch.take(16)))
                    sx_slots[hp].wait(SYN)
                    SYN.dma_start(out=s_sl[hp][:],
                                  in_=sc_s[tc * 128:(tc + 1) * 128,
                                           ht * HT:(ht + 1) * HT]
                                  ).then_inc(ldx.sem, 16)
                    SYN.dma_start(out=x_sl[hp][:],
                                  in_=x_own[tc * 128:(tc + 1) * 128,
                                            ht * HT:(ht + 1) * HT]
                                  ).then_inc(ldx.sem, 16)
                    vx_ = ldx.take(32)
                    pd = psA if hp == 0 else psB
                    TEN.wait_ge(evd.sem, wready4)
                    psA_slots[hp].wait(TEN)
                    mm4 = None
                    for ic in range(ICH):
                        if ic % (ICH // 4) == 0:
                            _ch, _v = vhs[ic // (ICH // 4)]
                            TEN.wait_ge(_ch.sem, _v)
                        mm4 = TEN.matmul(pd[:, :HT],
                                         h2qsb[hp][:, ic * 128:(ic + 1) * 128],
                                         dwT[par][:, ic * HT:(ic + 1) * HT],
                                         start=(ic == 0), stop=(ic == ICH - 1))
                    mm4.then_inc(mmd.sem, 1)
                    vm4 = mmd.take()
                    lastmm4 = vm4
                    h2q_slots[hp].freed_by(mmd, vm4)
                    SCA.wait_ge(mmd.sem, vm4)
                    gl_slots[hp].wait(SCA)
                    SCA.activation(gl_b[hp][:], pd[:, :HT],
                                   AF.Silu).then_inc(fsd.sem, 1)
                    vf4 = fsd.take()
                    psA_slots[hp].freed_by(fsd, vf4)
                    VEC.wait_ge(fsd.sem, vf4)
                    VEC.wait_ge(ldx.sem, vx_)
                    o_slots[hp].wait(VEC)
                    VEC.tensor_tensor(t_sl[:], s_sl[hp][:], gl_b[hp][:],
                                      ALU.mult).then_inc(qvd.sem, 1)
                    vtf_ = qvd.take()
                    VEC.wait_ge(qvd.sem, vtf_)
                    VEC.tensor_tensor(o_sl[hp][:], x_sl[hp][:], t_sl[:],
                                      ALU.add).then_inc(ivd.sem, 1)
                    vo = ivd.take()
                    sx_slots[hp].freed_by(ivd, vo)
                    gl_slots[hp].freed_by(ivd, vo)
                    SYN.wait_ge(ivd.sem, vo)
                    SYN.dma_start(out=out[tc * 128:(tc + 1) * 128,
                                          ht * HT:(ht + 1) * HT],
                                  in_=o_sl[hp][:]).then_inc(sto.sem, 16)
                    sto.take(16)
                    o_slots[hp].freed_by(sto)
                    if prep:
                        c0, c1 = 2 * tc, 2 * tc + 1
                        if c0 >= 2:
                            d_transp(ht + 1, c0 - 2, vgs_[c0 - 2])
                        vgs_[c0] = d_feed(c0, vws[c0])
                        if c0 + 3 < NCH:
                            vws[c0 + 3] = d_load(ht + 1, c0 + 3)
                        if c1 >= 2:
                            d_transp(ht + 1, c1 - 2, vgs_[c1 - 2])
                        vgs_[c1] = d_feed(c1, vws[c1])
                        if c1 + 3 < NCH:
                            vws[c1 + 3] = d_load(ht + 1, c1 + 3)
                dwT_slots[par].freed_by(mmd, lastmm4)
                if prep:
                    d_transp(ht + 1, NCH - 2, vgs_[NCH - 2])
                    q_flush(qpd)
                    dw_ready = d_transp(ht + 1, NCH - 1, vgs_[NCH - 1])
                maybe_barrier()
            SYN.wait_ge(sto.sem, sto.n)

    return nc


def _prepare(inputs):
    x = np.ascontiguousarray(
        np.asarray(inputs["L0_attn_hidden_states"], dtype=np.float32)
    ).reshape(NCORE * TOK, D)
    w_i = np.ascontiguousarray(np.asarray(inputs["L0_attn_w_i"], np.float32))
    w_f = np.ascontiguousarray(np.asarray(inputs["L0_attn_w_f"], np.float32))
    w_g = np.ascontiguousarray(np.asarray(inputs["L0_attn_w_g"], np.float32))
    w_oT = np.ascontiguousarray(np.asarray(inputs["L0_attn_w_o"], np.float32).T)
    gate_w = np.ascontiguousarray(np.asarray(inputs["L0_ffn_gate_w"], np.float32))
    up_w = np.ascontiguousarray(np.asarray(inputs["L0_ffn_up_w"], np.float32))
    down_w = np.ascontiguousarray(np.asarray(inputs["L0_ffn_down_w"], np.float32))

    def vec128(v):
        return np.ascontiguousarray(np.asarray(v, np.float32).reshape(EC, 128).T)

    vs = {
        "s_i": vec128(inputs["L0_attn_s_i"]),
        "s_f": vec128(inputs["L0_attn_s_f"]),
        "s_g": vec128(inputs["L0_attn_s_g"]),
        "s_o": vec128(inputs["L0_attn_s_o"]),
        "gnw": vec128(inputs["L0_attn_g_norm_w"]),
    }
    ident = np.eye(128, dtype=ml_dtypes.bfloat16)

    in_maps = []
    for c in range(NCORE):
        b = c // 2
        m = {
            "x_own": x[c * TOK:(c + 1) * TOK],
            "x_pre": x[b * 2 * TOK:b * 2 * TOK + TOK],
            "w_i": w_i, "w_f": w_f, "w_g": w_g, "w_oT": w_oT,
            "gate_w": gate_w, "up_w": up_w, "down_w": down_w,
            "maskv": np.full((128, 1), float(c % 2), np.float32),
            "ident": ident,
        }
        m.update(vs)
        in_maps.append(m)
    return in_maps


def kernel(**inputs):
    in_maps = _prepare(inputs)
    nc = build_program()
    res = run_bass_kernel_spmd(nc, in_maps, core_ids=list(range(NCORE)))
    outv = np.concatenate([r["out"] for r in res.results], axis=0)
    return outv.reshape(4, 4096, 2048)


if __name__ == "__main__":
    import time
    t0 = time.time()
    nc = build_program()
    print(f"built in {time.time()-t0:.1f}s, instructions: {nc.n_instructions()}")
